# version 11
# baseline (speedup 1.0000x reference)
import numpy as np
import ml_dtypes
from contextlib import ExitStack
import concourse.bass as bass
import concourse.mybir as mybir
from concourse.bass_utils import run_bass_kernel_spmd

F32 = mybir.dt.float32
BF16 = mybir.dt.bfloat16
AF = mybir.ActivationFunctionType
ALU = mybir.AluOpType
AX = mybir.AxisListType.X

T = 2048
D = 1024
NB = 2
NCORES = 8
NEG = -1.0e30

A_H = 32; A_HD = 64; A_DIM = 2048; A_DS = 128; A_K = 4; A_NL = 15
A_CONV = A_DIM + 2 * A_DS
A_PROJ = A_DIM + A_CONV + A_H * (A_NL + 1)
A_REST = A_CONV + A_H * (A_NL + 1)
F_HEADS = 8; F_HD = 64; F_WIN = 512; F_INNER = 512; F_HID = 2730; F_HIDP = 2816
F_QKSCALE = 8.0
G_NH = 16; G_NKV = 4; G_HD = 64
M_E = 8; M_HID = 2048

TT = T // 128


def _f32c(a):
    return np.ascontiguousarray(a, dtype=np.float32)


def host_prep(inp):
    d = {}
    w = inp["L0_attn_in_proj_w"]
    b = inp["L0_attn_in_proj_b"]
    d["a_wz"] = _f32c(np.asarray(w)[:A_DIM].T)
    d["a_wr"] = _f32c(np.asarray(w)[A_DIM:].T)
    d["a_bz"] = _f32c(np.asarray(b)[:A_DIM][None, :])
    d["a_br"] = _f32c(np.asarray(b)[A_DIM:][None, :])
    d["a_brc"] = _f32c(np.asarray(b)[A_DIM:].reshape(A_REST // 128, 128).T)
    d["a_conv"] = _f32c(np.asarray(inp["L0_attn_conv_w"]).reshape(18, 128, 4).transpose(1, 0, 2).reshape(128, 72))
    d["a_dtb"] = _f32c(np.asarray(inp["L0_attn_dt_bias"])[:, None])
    d["a_alog"] = _f32c(np.asarray(inp["L0_attn_A_log"])[:, None])
    Lp = np.zeros(512, np.float32); Lp[32:] = np.asarray(inp["L0_attn_L"]).reshape(-1)
    d["a_L"] = _f32c(Lp.reshape(4, 128).T)
    d["a_D"] = _f32c(np.asarray(inp["L0_attn_D"])[:, None])
    d["a_nw"] = _f32c(np.asarray(inp["L0_attn_norm_w"])[None, :])
    d["a_wo"] = _f32c(np.asarray(inp["L0_attn_out_proj_w"]).T)
    d["a_bo"] = _f32c(np.asarray(inp["L0_attn_out_proj_b"])[None, :])
    d["f_wqkv"] = _f32c(inp["L0_ffn_wqkv"])
    d["f_qs"] = _f32c(inp["L0_ffn_q_scale"])
    d["f_ks"] = _f32c(inp["L0_ffn_k_scale"])
    d["f_wo"] = _f32c(inp["L0_ffn_wo"])
    d["f_lng"] = _f32c(np.asarray(inp["L0_ffn_ln_g"])[:, None, :])
    d["f_lnb"] = _f32c(np.asarray(inp["L0_ffn_ln_b"])[:, None, :])
    w1 = np.asarray(inp["L0_ffn_w1"]); b1 = np.asarray(inp["L0_ffn_b1"])
    w1a = np.zeros((2, D, F_HIDP), np.float32); w1a[:, :, :F_HID] = w1[:, :, :F_HID]
    w1g = np.zeros((2, D, F_HIDP), np.float32); w1g[:, :, :F_HID] = w1[:, :, F_HID:]
    b1a = np.zeros((2, F_HIDP), np.float32); b1a[:, :F_HID] = b1[:, :F_HID]
    b1g = np.zeros((2, F_HIDP), np.float32); b1g[:, :F_HID] = b1[:, F_HID:]
    d["f_w1a"] = _f32c(w1a); d["f_w1g"] = _f32c(w1g)
    d["f_b1a"] = _f32c(np.stack([b1a[l].reshape(F_HIDP // 128, 128).T for l in range(2)]))
    d["f_b1g"] = _f32c(np.stack([b1g[l].reshape(F_HIDP // 128, 128).T for l in range(2)]))
    w2 = np.asarray(inp["L0_ffn_w2"])
    w2p = np.zeros((2, F_HIDP, D), np.float32); w2p[:, :F_HID] = w2
    d["f_w2"] = _f32c(w2p)
    d["f_b2"] = _f32c(np.asarray(inp["L0_ffn_b2"])[:, None, :])
    wqT = np.asarray(inp["L1_attn_wq"]).T.reshape(D, 16, 64)
    perm = [0] * 16
    for h in range(16):
        tile = (h % 4) + 4 * (h // 8)
        slot = tile * 2 + ((h // 4) % 2)
        perm[slot] = h
    d["g_wq"] = _f32c(wqT[:, perm, :].reshape(D, D))
    d["g_wk"] = _f32c(np.asarray(inp["L1_attn_wk"]).T)
    d["g_wv"] = _f32c(np.asarray(inp["L1_attn_wv"]).T)
    d["g_wo"] = _f32c(np.asarray(inp["L1_attn_wo"]).T)
    d["m_wr"] = _f32c(np.asarray(inp["L1_ffn_router_w"]).T)
    d["m_wg"] = _f32c(np.asarray(inp["L1_ffn_w_gate"]).transpose(0, 2, 1))
    d["m_wu"] = _f32c(np.asarray(inp["L1_ffn_w_up"]).transpose(0, 2, 1))
    d["m_wd"] = _f32c(np.asarray(inp["L1_ffn_w_down"]).transpose(0, 2, 1))
    tbl = static_tables()
    d.update(tbl)
    return d


_TBL_CACHE = None

def static_tables():
    global _TBL_CACHE
    if _TBL_CACHE is not None:
        return _TBL_CACHE
    d = {}
    n = T; hd = F_HD
    half = np.arange(0, hd, 2, dtype=np.float64)
    inv_freq = 1.0 / 10000.0 ** (half / hd)
    t = np.arange(n, dtype=np.float64)
    freqs = t[:, None] * inv_freq[None, :]
    freqs = np.concatenate([freqs, freqs], axis=-1)
    base = (half + 0.4 * hd) / (1.4 * hd)
    power = (t - n // 2) / float(F_WIN // 2)
    scale = base[None, :] ** power[:, None]
    scale = np.concatenate([scale, scale], axis=-1)
    cos = np.cos(freqs); sin = np.sin(freqs)
    sgn = np.ones(hd); sgn[: hd // 2] = -1.0
    d["t_cq"] = _f32c(cos * scale)
    d["t_sq"] = _f32c(sin * scale * sgn[None, :])
    d["t_ck"] = _f32c(cos / scale)
    d["t_sk"] = _f32c(sin / scale * sgn[None, :])
    i = np.arange(128)[:, None]; j = np.arange(128)[None, :]
    d["t_causal01"] = _f32c(j <= i)
    d["t_causbias"] = _f32c(np.where(j <= i, 0.0, NEG))
    d["t_causbias_fm"] = _f32c(np.where(j >= i, 0.0, NEG))
    d["t_caus01_fm"] = _f32c(j >= i)
    d["t_anti01_fm"] = _f32c(j <= i)
    lvl = np.zeros((128, 128), np.int64)
    tt_ = np.arange(128)
    x_ = tt_[:, None] ^ tt_[None, :]
    with np.errstate(divide="ignore"):
        lv = 1 + np.floor(np.log2(np.maximum(x_, 1))).astype(np.int64)
    lv[x_ == 0] = 0
    fen = np.zeros((8, 128, 128), np.float32)
    for e in range(8):
        fen[e] = ((lv == e) & (tt_[None, :] >= tt_[:, None])).astype(np.float32)
    d["t_fen_fm"] = _f32c(fen.transpose(1, 0, 2).reshape(128, 8 * 128))
    fen_tm = np.zeros((8, 128, 128), np.float32)
    for e in range(8):
        fen_tm[e] = ((lv == e) & (tt_[None, :] <= tt_[:, None])).astype(np.float32)
    d["t_fen_tm"] = _f32c(fen_tm.transpose(1, 0, 2).reshape(128, 8 * 128))
    d["t_eye"] = _f32c(np.eye(128))
    _TBL_CACHE = d
    return d


class Sem:
    def __init__(self, nc, es, name):
        self.h = es.enter_context(nc.semaphore(name))
        self.n = 0
    def inc(self, k=1):
        self.n += k
        return (self, k)
    def at(self):
        return (self, self.n)


class Prog:
    ENG = ("sync", "tensor", "vector", "scalar", "gpsimd")
    def __init__(self, nc):
        self.nc = nc
        self.ops = {e: [] for e in self.ENG}
        self.phase = "init"
    def op(self, eng, fn, wait=(), inc=()):
        assert eng in self.ENG
        self.ops[eng].append((tuple(wait), fn, tuple(inc), self.phase))
    def emit(self):
        nc = self.nc
        from contextlib import ExitStack as _ES
        with nc.Block() as blk:
            for eng in self.ENG:
                ops = self.ops[eng]
                def player(e, ops=ops):
                    cur = [None]
                    es = _ES()
                    for waits, fn, incs, phase in ops:
                        if phase != cur[0]:
                            es.close()
                            es = _ES()
                            es.enter_context(nc.named_scope(phase))
                            cur[0] = phase
                        for (s, v) in waits:
                            e.wait_ge(s.h, v)
                        inst = fn(e)
                        if incs:
                            assert inst is not None, "op with incs returned no instruction"
                            for (s, k) in incs:
                                inst = inst.then_inc(s.h, k)
                    es.close()
                getattr(blk, eng)(player)


class MK:
    def __init__(self, fused_phases=(), debug=False):
        self.phases = fused_phases
        self.debug = debug
        self.es = ExitStack()
        self.nc = bass.Bass("TRN2", target_bir_lowering=False)
        self.prog = Prog(self.nc)
        self.dram = {}
        self._arena_ptr32 = 0
        self._arena_ptr16 = 0
        self.sems = {}
        self._bar_n = 0

    def dram_in(self, name, shape, dtype=F32):
        self.dram[name] = self.nc.dram_tensor(name, list(shape), dtype, kind="ExternalInput").ap()
        return self.dram[name]

    def dram_scratch(self, name, shape, dtype=F32):
        kind = "ExternalOutput" if self.debug else "Internal"
        self.dram[name] = self.nc.dram_tensor(name, list(shape), dtype, kind=kind).ap()
        return self.dram[name]

    def dram_out(self, name, shape):
        self.dram[name] = self.nc.dram_tensor(name, list(shape), F32, kind="ExternalOutput").ap()
        return self.dram[name]

    def alloc_arenas(self, w32, w16):
        self.A32 = self.es.enter_context(self.nc.sbuf_tensor("A32", [128, w32], F32))
        self.A16 = self.es.enter_context(self.nc.sbuf_tensor("A16", [128, w16], BF16))
        self.W32max, self.W16max = w32, w16

    def reset(self):
        self._arena_ptr32 = 0
        self._arena_ptr16 = 0

    def c32(self, w):
        o = self._arena_ptr32
        assert o + w <= self.W32max, f"A32 overflow {o}+{w}"
        self._arena_ptr32 = o + w
        return self.A32[:, o:o + w]

    def c16(self, w):
        o = self._arena_ptr16
        assert o + w <= self.W16max, f"A16 overflow {o}+{w}"
        self._arena_ptr16 = o + w
        return self.A16[:, o:o + w]

    def fen(self, e, inst):
        import concourse.mybir as _mb
        nm = "fence_" + str(e.engine)
        s = self.sem(nm)
        s.n += 1
        inst.then_inc(s.h, 1)
        e.wait_ge(s.h, s.n)

    def fc(self, v, rows=128):
        m = {1e-5: 0, 1.0: 1}
        return self.consts["fc"][0:rows, m[v]:m[v] + 1]

    def sem(self, name):
        if name not in self.sems:
            self.sems[name] = Sem(self.nc, self.es, name)
        return self.sems[name]

    def op(self, eng, fn, wait=(), inc=()):
        self.prog.op(eng, fn, wait, inc)

    def bar(self, extra=()):
        w = [s.at() for nm, s in self.sems.items() if nm.startswith(("ld_", "st_"))] + list(extra)
        bs = self.sem("bar")
        for e in Prog.ENG:
            tok = bs.inc(1)
            self.op(e, (lambda en: en.nop()), wait=w, inc=[tok])
        tgt = bs.n
        for e in Prog.ENG:
            self.op(e, (lambda en: None), wait=[(bs, tgt)])

    def load(self, dst, src, q="sync", sem=None, serialize=False, wait=()):
        s = self.sem(sem if sem else "ld_" + q)
        w = list(wait) + ([(s, s.n)] if serialize else [])
        tok = s.inc(16)
        self.op(q, (lambda e, d=dst, i=src: e.dma_start(out=d, in_=i)), wait=w, inc=[tok])
        return s

    def store(self, dst, src, q="sync", sem=None, serialize=False, wait=()):
        s = self.sem(sem if sem else "st_" + q)
        w = list(wait) + ([(s, s.n)] if serialize else [])
        tok = s.inc(16)
        self.op(q, (lambda e, d=dst, i=src: e.dma_start(out=d, in_=i)), wait=w, inc=[tok])
        return s

    def wait_lds(self, eng):
        w = []
        for q in ("sync", "gpsimd", "scalar"):
            for p in ("ld_", "st_"):
                if p + q in self.sems:
                    w.append(self.sems[p + q].at())
        self.op(eng, (lambda e: None), wait=w)


INPUT_SHAPES = {
        "x_in": [NB, T, D],
        "a_wz": [D, A_DIM], "a_wr": [D, A_REST], "a_bz": [1, A_DIM], "a_br": [1, A_REST], "a_brc": [128, A_REST // 128],
        "a_conv": [128, 72], "a_dtb": [A_H, 1], "a_alog": [A_H, 1], "a_L": [128, 4],
        "a_D": [A_H, 1], "a_nw": [1, A_DIM], "a_wo": [A_DIM, D], "a_bo": [1, D],
        "f_wqkv": [2, D, 3 * F_INNER], "f_qs": [2, F_HD], "f_ks": [2, F_HD],
        "f_wo": [2, F_INNER, D], "f_lng": [2, 1, D], "f_lnb": [2, 1, D],
        "f_w1a": [2, D, F_HIDP], "f_w1g": [2, D, F_HIDP],
        "f_b1a": [2, 128, F_HIDP // 128], "f_b1g": [2, 128, F_HIDP // 128],
        "f_w2": [2, F_HIDP, D], "f_b2": [2, 1, D],
        "g_wq": [D, D], "g_wk": [D, 256], "g_wv": [D, 256], "g_wo": [D, D],
        "m_wr": [D, M_E], "m_wg": [M_E, D, M_HID], "m_wu": [M_E, D, M_HID], "m_wd": [M_E, M_HID, D],
        "t_cq": [T, F_HD], "t_sq": [T, F_HD], "t_ck": [T, F_HD], "t_sk": [T, F_HD],
        "t_causal01": [128, 128], "t_causbias": [128, 128], "t_causbias_fm": [128, 128],
        "t_caus01_fm": [128, 128], "t_anti01_fm": [128, 128], "t_fen_fm": [128, 8 * 128], "t_fen_tm": [128, 8 * 128], "t_eye": [128, 128],
}
for _k in INPUT_SHAPES:
    INPUT_SHAPES[_k] = [int(x) for x in INPUT_SHAPES[_k]]


def setup(mk):
    nc = mk.nc
    for nm, s in INPUT_SHAPES.items():
        mk.dram_in(nm, s)
    mk.dram_out("out", [NB, T, D])
    if mk.debug:
        mk.dram_in("dbg_xnfm", [D, T], BF16)
        mk.dram_in("dbg_tm", [T, D])
    mk.dram_scratch("sc_xn32", [T, D])
    mk.dram_scratch("sc_xnfm", [D, T], BF16)
    mk.dram_scratch("sc_xa", [T, D])
    mk.dram_scratch("sc_xb", [T, D])
    mk.dram_scratch("sc_v", [T, A_DIM], BF16)
    mk.dram_scratch("sc_qk", [T, T], BF16)
    mk.dram_scratch("sc_small", [128, 8192])
    mk.dram_scratch("sc_fx", [T, D])
    mk.dram_scratch("sc_fn32", [T, D])
    mk.dram_scratch("sc_fnfm", [D, T], BF16)
    mk.dram_scratch("sc_qT", [F_INNER, T], BF16)
    mk.dram_scratch("sc_kT", [F_INNER, T], BF16)
    mk.dram_scratch("sc_vtm", [T, F_INNER], BF16)
    mk.dram_scratch("sc_xn1fm", [D, T], BF16)
    mk.dram_scratch("sc_xn132", [T, D])
    mk.dram_scratch("sc_B", [128, T], BF16)
    mk.dram_scratch("sc_C", [128, T], BF16)
    mk.dram_scratch("sc_lam", [128, 16 * 512], BF16)
    mk.dram_scratch("sc_cols", [128, 8192])
    mk.dram_scratch("sc_gc", [32, T])
    mk.dram_scratch("sc_y", [T, A_DIM])

    mk.alloc_arenas(w32=14000, w16=60000)

    mk.consts = {}
    C = mk.consts
    cs = mk.es
    def const16(name, shape):
        t = cs.enter_context(nc.sbuf_tensor("c_" + name, list(shape), BF16))
        return t
    def const32(name, shape):
        t = cs.enter_context(nc.sbuf_tensor("c_" + name, list(shape), F32))
        return t
    C["eye16"] = const16("eye16", [128, 128])
    C["eye32"] = const32("eye32", [128, 128])
    C["ones_row"] = const16("ones_row", [1, 512])
    C["ones_col"] = const16("ones_col", [128, 1])
    C["caus01_fm16"] = const16("caus01_fm16", [128, 128])
    C["anti01_fm16"] = const16("anti01_fm16", [128, 128])
    C["causal01"] = const32("causal01", [128, 128])
    C["causbias_fm"] = const32("causbias_fm", [128, 128])
    C["caus01_fm"] = const32("caus01_fm", [128, 128])
    C["fen_fm"] = const16("fen_fm", [128, 8 * 128])
    C["fen_tm"] = const16("fen_tm", [128, 8 * 128])
    C["a_dtb"] = const32("a_dtb", [A_H, 1])
    C["a_convc"] = const32("a_convc", [128, 72])
    C["a_negA"] = const32("a_negA", [A_H, 1])
    C["a_L"] = const32("a_L", [128, 4])
    C["a_D"] = const32("a_D", [A_H, 1])
    C["a_bz"] = const16("a_bz", [1, A_DIM])
    C["a_br32"] = const32("a_br32", [128, A_REST // 128])
    C["a_nw"] = const16("a_nw", [1, A_DIM])
    C["a_bo"] = const16("a_bo", [1, D])
    C["f_b2"] = const16("f_b2", [1, 2 * D])
    C["f_qs"] = const32("f_qs", [1, 2 * F_HD])
    C["f_ks"] = const32("f_ks", [1, 2 * F_HD])
    C["f_lng"] = const16("f_lng", [1, 2 * D])
    C["f_lnb"] = const16("f_lnb", [1, 2 * D])
    C["fc"] = const32("fc", [128, 8])
    mk.pb = [mk.es.enter_context(nc.psum_tensor(f"pb{i}", [128, 512], F32)) for i in range(7)]
    mk.ptp = mk.es.enter_context(nc.psum_tensor("ptp", [128, 1024], BF16))

    d = mk.dram
    mk.load(C["causal01"][:], d["t_causal01"])
    mk.load(C["causbias_fm"][:], d["t_causbias_fm"])
    mk.load(C["caus01_fm"][:], d["t_caus01_fm"])
    mk.load(C["a_dtb"][:], d["a_dtb"])
    mk.load(C["a_convc"][:], d["a_conv"])
    mk.load(C["a_L"][:], d["a_L"])
    mk.load(C["a_D"][:], d["a_D"])
    mk.load(C["f_qs"][:], d["f_qs"].rearrange("l d -> (l d)")[None, :])
    mk.load(C["f_ks"][:], d["f_ks"].rearrange("l d -> (l d)")[None, :])
    mk.load(C["a_br32"][:], d["a_brc"])
    mk.load(C["eye16"][:], d["t_eye"], q="gpsimd")
    mk.load(C["caus01_fm16"][:], d["t_caus01_fm"], q="gpsimd")
    mk.load(C["anti01_fm16"][:], d["t_anti01_fm"], q="gpsimd")
    mk.load(C["fen_fm"][:], d["t_fen_fm"], q="gpsimd")
    mk.load(C["fen_tm"][:], d["t_fen_tm"], q="gpsimd")
    mk.load(C["a_bz"][:], d["a_bz"], q="gpsimd")
    mk.load(C["a_nw"][:], d["a_nw"], q="gpsimd")
    mk.load(C["a_bo"][:], d["a_bo"], q="gpsimd")
    mk.load(C["f_b2"][:], d["f_b2"].rearrange("l one n -> (l one n)")[None, :], q="gpsimd")
    mk.load(C["f_lng"][:], d["f_lng"].rearrange("l one n -> (l one n)")[None, :], q="gpsimd")
    mk.load(C["f_lnb"][:], d["f_lnb"].rearrange("l one n -> (l one n)")[None, :], q="gpsimd")
    mk.load(C["eye32"][:], d["t_eye"])
    mk.load(C["a_negA"][:], d["a_alog"])
    lds = [mk.sems["ld_gpsimd"].at(), mk.sems["ld_sync"].at()]
    def mkconsts(e):
        e.memset(C["ones_row"][:], 1.0)
        e.memset(C["ones_col"][:], 1.0)
        e.memset(C["fc"][:, 0:1], 1e-5)
        e.memset(C["fc"][:, 1:2], 1.0)
        return e.engine_nop()
    gpd = mk.sem("gp_done")
    mk.op("gpsimd", mkconsts, wait=lds, inc=[gpd.inc()])
    def negA2(e):
        return e.activation(C["a_negA"][:], C["a_negA"][:], AF.Exp, scale=1.0)
    scd = mk.sem("sc_done")
    mk.op("scalar", negA2, wait=lds, inc=[scd.inc()])
    def negA3(e):
        return e.tensor_scalar(C["a_negA"][:], C["a_negA"][:], -1.0, None, ALU.mult)
    dvd = mk.sem("dv_done")
    mk.op("vector", negA3, wait=[(scd, scd.n)], inc=[dvd.inc()])
    mk.bar(extra=[gpd.at(), scd.at(), dvd.at()])


def ln_phase(mk, src, xn32, xnfm, eps=1e-5):
    mk.prog.phase = "ln"
    mk.reset()
    x_sb = mk.c32(2 * D)
    sq = mk.c32(D)
    xn_sb = mk.c32(2 * D)
    xnb = mk.c16(2 * D)
    fmb = mk.c16(2 * D)
    cols = mk.c32(8)
    eye16 = mk.consts["eye16"]
    ld = mk.sem("ld_sync"); st = mk.sem("st_sync")
    mm = mk.sem("p_mm"); ev = mk.sem("p_ev"); sc = mk.sem("p_sc"); dv = mk.sem("p_dv")
    rel = {}
    ptp = mk.ptp
    for i in range(TT):
        pi = i % 2
        xt = x_sb[:, pi * D:(pi + 1) * D]
        xnt = xn_sb[:, pi * D:(pi + 1) * D]
        xbt = xnb[:, pi * D:(pi + 1) * D]
        fmt = fmb[:, pi * D:(pi + 1) * D]
        w = rel.pop(i - 2, [])
        if w:
            w = [(st, st.n)] + w
        mk.op("sync", (lambda e: None), wait=w)
        mk.load(xt, src[i * 128:(i + 1) * 128, :])
        t_ld = ld.n
        w_sq = [(ld, t_ld)] + ([("RED_PREV")] if False else [])
        mk.op("scalar", (lambda e, a=sq, x=xt: e.activation(a, x, AF.Square)),
              wait=[(ld, t_ld)] + (rel.get(("sq", i - 1), [])), inc=[sc.inc()])
        def red(e, x=xt, s=sq, c=cols):
            e.reduce_sum(c[:, 0:1], x, axis=AX, negate=True).then_inc(dv.h, 1)
            return e.reduce_sum(c[:, 2:3], s, axis=AX)
        tok1 = dv.inc(); tok2 = dv.inc()
        mk.op("vector", red, wait=[(sc, sc.n)], inc=[tok2])
        def smu(e, c=cols):
            e.activation(c[:, 1:2], c[:, 0:1], AF.Copy, scale=1.0 / D).then_inc(sc.h, 1)
            return e.activation(c[:, 3:4], c[:, 0:1], AF.Square, scale=1.0 / D)
        tok3 = sc.inc(); tok4 = sc.inc()
        mk.op("scalar", smu, wait=[(dv, dv.n)], inc=[tok4])
        def var(e, c=cols):
            return e.tensor_scalar(c[:, 4:5], c[:, 2:3], 1.0 / D, c[:, 3:4], ALU.mult, ALU.subtract)
        mk.op("vector", var, wait=[(sc, sc.n)], inc=[dv.inc()])
        def rstd(e, c=cols):
            mk.fen(e, e.activation(c[:, 5:6], c[:, 4:5], AF.Ln, bias=mk.fc(1e-5)))
            return e.activation(c[:, 6:7], c[:, 5:6], AF.Exp, scale=-0.5)
        mk.op("scalar", rstd, wait=[(dv, dv.n)], inc=[sc.inc()])
        def norm(e, x=xt, c=cols, xn=xnt, xb=xbt):
            e.tensor_scalar(xn, x, c[:, 1:2], c[:, 6:7], ALU.add, ALU.mult).then_inc(dv.h, 1)
            return e.tensor_scalar(xb, x, c[:, 1:2], c[:, 6:7], ALU.add, ALU.mult)
        tok5 = dv.inc(); tok6 = dv.inc()
        mk.op("vector", norm, wait=[(sc, sc.n)], inc=[tok6])
        t_norm = dv.n
        mk.op("sync", (lambda e: None), wait=[(dv, t_norm)])
        mk.store(xn32[i * 128:(i + 1) * 128, :], xnt)
        def tp(e, xb=xbt):
            last = None
            for k in range(8):
                last = e.transpose(ptp[:, k * 128:(k + 1) * 128], xb[:, k * 128:(k + 1) * 128], eye16[:])
            return last
        mk.op("tensor", tp, wait=[(dv, t_norm)] + ([(ev, ev.n)] if i > 0 else []), inc=[mm.inc()])
        mk.op("vector", (lambda e, f=fmt: e.tensor_copy(f, ptp[:])), wait=[(mm, mm.n)], inc=[ev.inc()])
        t_ev = ev.n
        mk.op("sync", (lambda e: None), wait=[(ev, t_ev)])
        for k in range(8):
            mk.store(xnfm[k * 128:(k + 1) * 128, i * 128:(i + 1) * 128], fmt[:, k * 128:(k + 1) * 128])
        rel[i] = [(mm, mm.n), (dv, t_norm)]
    mk.bar()


def moe_phase(mk, xnfm_d, xn32_d, out_d):
    nc = mk.nc
    mk.prog.phase = "moe"
    mk.reset()
    d = mk.dram
    xw = mk.c16(16384)
    h_sb = mk.c16(32768)
    wcol = mk.c16(4096)
    gsb = mk.c16(2 * 512)
    usb = mk.c16(2 * 512)
    esb = mk.c32(8)
    msk = mk.c32(8)
    tmp8 = mk.c32(8)
    cu8 = mk.c32(8)
    combine = mk.c32(128)
    rc = mk.c32(8)
    ysc = mk.c32(4 * 512)
    xt32 = mk.c32(2 * D)
    xT32 = mk.c32(2 * 128)
    rw32 = mk.c32(64)
    ld = mk.sem("ld_sync"); ldg = mk.sem("ld_gpsimd")
    mm = mk.sem("p_mm"); dv = mk.sem("p_dv"); sc = mk.sem("p_sc"); ev = mk.sem("p_ev")
    sty = [mk.sem(f"st_y{i}") for i in range(4)]
    pr = mk.pb[5]
    eye32 = mk.consts["eye32"]

    mk.load(xw[:], xnfm_d.rearrange("(k p) t -> p k t", p=128), q="gpsimd", sem="ld_xw", serialize=True)
    mk.load(rw32[:], d["m_wr"].rearrange("(k p) e -> p k e", p=128), q="sync")
    t_xnld = mk.sems["ld_sync"].n

    for i in range(TT):
        xts = xt32[:, (i % 2) * D:(i % 2 + 1) * D]
        mk.load(xts, xn32_d[i * 128:(i + 1) * 128, :], q="sync", sem="ld_rx", serialize=True,
                wait=([(mm, mm.n)] if i >= 2 else []))
        t_x = mk.sem("ld_rx").n
        for k in range(8):
            pf = mk.pb[6][:, (k % 2) * 128:(k % 2) * 128 + 128]
            xTs = xT32[:, (k % 2) * 128:(k % 2) * 128 + 128]
            def tp(e, xts=xts, k=k, pf=pf):
                return e.transpose(pf, xts[:, k * 128:(k + 1) * 128], eye32[:])
            w = [(mk.sem("ld_rx"), t_x), (dv, dv.n)]
            mk.op("tensor", tp, wait=w, inc=[mm.inc()])
            mk.op("vector", (lambda e, pf=pf, xTs=xTs: e.tensor_copy(xTs, pf)),
                  wait=[(mm, mm.n)], inc=[dv.inc()])
            def rmm(e, k=k, xTs=xTs):
                return e.matmul(pr[:, 0:8], xTs, rw32[:, k * 8:(k + 1) * 8],
                                start=(k == 0), stop=(k == 7))
            mk.op("tensor", rmm, wait=[(dv, dv.n), (ld, t_xnld)], inc=[mm.inc()])
        mk.op("vector", (lambda e: e.reduce_max(rc[:, 0:1], pr[:, 0:8], axis=AX, negate=True)),
              wait=[(mm, mm.n)], inc=[dv.inc()])
        mk.op("scalar", (lambda e: e.activation(esb, pr[:, 0:8], AF.Exp, bias=rc[:, 0:1], accum_out=rc[:, 1:2])),
              wait=[(dv, dv.n)], inc=[sc.inc()])
        def rtop(e, i=i):
            mk.fen(e, e.reduce_max(rc[:, 2:3], esb, axis=AX))
            mk.fen(e, e.tensor_scalar(msk, esb, rc[:, 2:3], None, ALU.is_equal))
            mk.fen(e, e.scalar_tensor_tensor(tmp8, msk, NEG, esb, ALU.mult, ALU.add))
            mk.fen(e, e.reduce_max(rc[:, 3:4], tmp8, axis=AX))
            mk.fen(e, e.tensor_scalar(msk, esb, rc[:, 3:4], None, ALU.is_ge))
            mk.fen(e, e.tensor_tensor(cu8, esb, msk, ALU.mult))
            mk.fen(e, e.reduce_sum(rc[:, 4:5], cu8, axis=AX))
            mk.fen(e, e.reciprocal(rc[:, 5:6], rc[:, 4:5]))
            return e.tensor_scalar(combine[:, i * 8:(i + 1) * 8], cu8, rc[:, 5:6], None, ALU.mult)
        mk.op("vector", rtop, wait=[(sc, sc.n)], inc=[dv.inc()])
    t_router = dv.n

    t_down_prev = mm.n
    for e_i in range(M_E):
        if e_i > 0:
            mk.load(xw[:], xnfm_d.rearrange("(k p) t -> p k t", p=128), q="gpsimd",
                    sem="ld_xw", serialize=True, wait=[(mm, t_down_prev)])
        t_xn = mk.sem("ld_xw").n
        wslot_mm = {}
        for ht in range(16):
            wslot = wcol[:, (ht % 2) * 2048:(ht % 2) * 2048 + 2048]
            if ht >= 2:
                mk.op("gpsimd", (lambda e: None), wait=[(mm, wslot_mm[ht - 2])])
            mk.load(wslot[:, 0:1024], d["m_wg"][e_i].rearrange("(k p) h -> p k h", p=128)[:, :, ht * 128:(ht + 1) * 128], q="gpsimd", sem="ld_wcol", serialize=True)
            mk.load(wslot[:, 1024:2048], d["m_wu"][e_i].rearrange("(k p) h -> p k h", p=128)[:, :, ht * 128:(ht + 1) * 128], q="gpsimd", sem="ld_wcol")
            ldw = mk.sem("ld_wcol"); t_w = ldw.n
            for tc in range(4):
                pg = mk.pb[tc % 2]; pu = mk.pb[2 + tc % 2]
                def gu(e, ht=ht, tc=tc, pg=pg, pu=pu, wslot=wslot, t_w=t_w, t_xn=t_xn):
                    last = None
                    for k in range(8):
                        last = e.matmul(pg[:], wslot[:, k * 128:(k + 1) * 128],
                                        xw[:, k * 2048 + tc * 512: k * 2048 + (tc + 1) * 512],
                                        start=(k == 0), stop=(k == 7))
                    for k in range(8):
                        last = e.matmul(pu[:], wslot[:, 1024 + k * 128: 1024 + (k + 1) * 128],
                                        xw[:, k * 2048 + tc * 512: k * 2048 + (tc + 1) * 512],
                                        start=(k == 0), stop=(k == 7))
                    return last
                w = [(ldw, t_w), (mk.sem("ld_xw"), t_xn)]
                if (ht * 4 + tc) >= 2:
                    w.append((sc, max(0, sc.n - 2)))
                mk.op("tensor", gu, wait=w, inc=[mm.inc()])
                def evac(e, tc=tc, pg=pg, pu=pu):
                    e.activation(gsb[:, (tc % 2) * 512:(tc % 2 + 1) * 512], pg[:], AF.Silu).then_inc(sc.h, 1)
                    return e.activation(usb[:, (tc % 2) * 512:(tc % 2 + 1) * 512], pu[:], AF.Copy)
                tokA = sc.inc(); tokB = sc.inc()
                mk.op("scalar", evac, wait=[(mm, mm.n), (dv, max(0, dv.n - 1))], inc=[tokB])
                def hmul(e, ht=ht, tc=tc):
                    return e.tensor_tensor(h_sb[:, ht * 2048 + tc * 512: ht * 2048 + (tc + 1) * 512],
                                           gsb[:, (tc % 2) * 512:(tc % 2 + 1) * 512],
                                           usb[:, (tc % 2) * 512:(tc % 2 + 1) * 512], ALU.mult)
                mk.op("vector", hmul, wait=[(sc, sc.n), (mm, t_down_prev)], inc=[dv.inc()])
            wslot_mm[ht] = mm.n
        t_gu_prev = mm.n
        t_h = dv.n
        mk.load(xw[:], d["m_wd"][e_i].rearrange("(k p) dd -> p k dd", p=128), q="gpsimd",
                sem="ld_xw", serialize=True, wait=[(mm, t_gu_prev)])
        t_wd = mk.sem("ld_xw").n
        for tt in range(TT):
            for dc in range(2):
                slot = (tt * 2 + dc) % 2
                py = mk.pb[4 + slot]
                def down(e, tt=tt, dc=dc, py=py):
                    last = None
                    for k in range(16):
                        last = e.matmul(py[:], h_sb[:, k * 2048 + tt * 128: k * 2048 + (tt + 1) * 128],
                                        xw[:, k * 1024 + dc * 512: k * 1024 + (dc + 1) * 512],
                                        start=(k == 0), stop=(k == 15))
                    return last
                w = [(mk.sem("ld_xw"), t_wd), (dv, t_h)]
                if (tt * 2 + dc) >= 2:
                    w.append((ev, ev.n - 1))
                mk.op("tensor", down, wait=w, inc=[mm.inc()])
                yslot = (tt * 2 + dc) % 4
                def yevac(e, tt=tt, dc=dc, py=py, yslot=yslot, e_i=e_i):
                    return e.tensor_scalar(ysc[:, yslot * 512:(yslot + 1) * 512], py[:],
                                           combine[:, tt * 8 + e_i: tt * 8 + e_i + 1], None, ALU.mult)
                w2 = [(mm, mm.n), (dv, t_router)]
                if sty[yslot].n > 0:
                    w2.append((sty[yslot], sty[yslot].n))
                mk.op("vector", yevac, wait=w2, inc=[ev.inc()])
                mk.op("gpsimd", (lambda e: None), wait=[(ev, ev.n)])
                def acst(e, tt=tt, dc=dc, yslot=yslot):
                    return e.dma_start(out=out_d[tt * 128:(tt + 1) * 128, dc * 512:(dc + 1) * 512],
                                      in_=ysc[:, yslot * 512:(yslot + 1) * 512], accum_op=ALU.add)
                mk.op("gpsimd", acst, inc=[sty[yslot].inc(16)])
        t_down_prev = mm.n
    mk.bar()


def l1_attn_phase(mk, xnfm_d, xacc_d, out_d):
    nc = mk.nc
    mk.prog.phase = "l1a"
    mk.reset()
    d = mk.dram
    C = mk.consts
    qT = mk.c16(16384)
    kT = mk.c16(4096)
    vsb = mk.c16(16 * 260)
    attb = mk.c16(2 * 512)
    xnl = mk.c16(16384)
    ctx = mk.c16(16384)
    woc = xnl[:, 10240:11264]
    nrow = xnl[:, 8192:10240]
    negc = mk.c32(256)
    ycol = mk.c32(4)
    xat = mk.c32(2 * D)
    osb = mk.c32(2 * D)
    ld = mk.sem("ld_sync"); ldg = mk.sem("ld_gpsimd")
    mm = mk.sem("p_mm"); dv = mk.sem("p_dv"); sc = mk.sem("p_sc"); ev = mk.sem("p_ev")
    mk.load(xnl[:], xnfm_d.rearrange("(k p) t -> p k t", p=128), q="sync")
    wq_sb = None
    wq_sb = ctx[:, 0:8192]; wk_sb = ctx[:, 8192:10240]; wv_sb = ctx[:, 10240:12288]
    mk.load(wq_sb, d["g_wq"].rearrange("(k p) o -> p k o", p=128), q="gpsimd")
    mk.load(wk_sb, d["g_wk"].rearrange("(k p) o -> p k o", p=128), q="gpsimd")
    mk.load(wv_sb, d["g_wv"].rearrange("(k p) o -> p k o", p=128), q="gpsimd")
    t_ld = ld.n; t_ldg = ldg.n

    for o in range(8):
        for c in range(4):
            pbk = mk.pb[(o * 4 + c) % 2]
            def qmm(e, o=o, c=c, pbk=pbk):
                last = None
                for k in range(8):
                    last = e.matmul(pbk[:], wq_sb[:, k * 1024 + o * 128: k * 1024 + (o + 1) * 128],
                                    xnl[:, k * 2048 + c * 512: k * 2048 + (c + 1) * 512],
                                    start=(k == 0), stop=(k == 7))
                return last
            w = [(ld, t_ld), (ldg, t_ldg)]
            if (o * 4 + c) >= 2:
                w.append((dv, dv.n - 1))
            mk.op("tensor", qmm, wait=w, inc=[mm.inc()])
            mk.op("vector", (lambda e, o=o, c=c, pbk=pbk: e.tensor_copy(qT[:, o * 2048 + c * 512: o * 2048 + (c + 1) * 512], pbk[:])),
                  wait=[(mm, mm.n)], inc=[dv.inc()])
    for o in range(2):
        for c in range(4):
            pbk = mk.pb[(o * 4 + c) % 2]
            def kmm(e, o=o, c=c, pbk=pbk):
                last = None
                for k in range(8):
                    last = e.matmul(pbk[:], wk_sb[:, k * 256 + o * 128: k * 256 + (o + 1) * 128],
                                    xnl[:, k * 2048 + c * 512: k * 2048 + (c + 1) * 512],
                                    start=(k == 0), stop=(k == 7))
                return last
            mk.op("tensor", kmm, wait=[(dv, dv.n - 1)], inc=[mm.inc()])
            mk.op("vector", (lambda e, o=o, c=c, pbk=pbk: e.tensor_copy(kT[:, o * 2048 + c * 512: o * 2048 + (c + 1) * 512], pbk[:])),
                  wait=[(mm, mm.n)], inc=[dv.inc()])
    for i in range(TT):
        pbk = mk.pb[i % 2]
        def vmm(e, i=i, pbk=pbk):
            last = None
            for k in range(8):
                last = e.matmul(pbk[:, 0:256], xnl[:, k * 2048 + i * 128: k * 2048 + (i + 1) * 128],
                                wv_sb[:, k * 256:(k + 1) * 256], start=(k == 0), stop=(k == 7))
            return last
        mk.op("tensor", vmm, wait=[(dv, dv.n - 1)], inc=[mm.inc()])
        def vev(e, i=i, pbk=pbk):
            last = None
            for g in range(4):
                last = e.tensor_copy(vsb[:, i * 260 + g * 65: i * 260 + g * 65 + 64], pbk[:, g * 64:(g + 1) * 64])
            return last
        mk.op("vector", vev, wait=[(mm, mm.n)], inc=[dv.inc()])
    def vones(e):
        return e.memset(vsb[:].rearrange("p (i g c) -> p i g c", i=16, g=4)[:, :, :, 64:65], 1.0)
    mk.op("vector", vones, inc=[dv.inc()])
    t_proj = dv.n

    nrow2 = xnl[:, 11264:15360]
    if not hasattr(mk, "_l1_attb_mm"):
        mk._l1_attb_mm = [0, 0]
        mk._l1_ybank_free = [0, 0, 0, 0]
    attb_mm = mk._l1_attb_mm
    ybank_free = mk._l1_ybank_free
    t_p1r = [0]
    par_end = [mm.n, mm.n]

    def head_geom(h):
        qtile = (h % 4) + 4 * (h // 8)
        qbase = ((h // 4) % 2) * 64
        g = h // 4
        qsl = lambda tsl, qt=qtile, qb=qbase: qT[qb:qb + 64, qt * 2048 + tsl[0]:qt * 2048 + tsl[1]]
        ksl = lambda ssl, g=g: kT[(g % 2) * 64:(g % 2) * 64 + 64, (g // 2) * 2048 + ssl[0]:(g // 2) * 2048 + ssl[1]]
        return qsl, ksl

    def mk_pass1_ops(h, first=False):
        qsl, ksl = head_geom(h)
        ops = []
        for i in range(TT):
            wdt = (i + 1) * 128
            for c in range((wdt + 511) // 512):
                n = min(512, wdt - c * 512)
                def ep1(i=i, c=c, n=n, qsl=qsl, ksl=ksl, fw=(first and i == 0 and c == 0)):
                    def p1(e, i=i, c=c, n=n, qsl=qsl, ksl=ksl):
                        return e.matmul(mk.pb[6][:, 0:n], qsl((i * 128, (i + 1) * 128)),
                                        ksl((c * 512, c * 512 + n)), start=True, stop=True)
                    w = [(dv, t_p1r[0])] if not fw else [(dv, t_proj)]
                    mk.op("tensor", p1, wait=w, inc=[mm.inc()])
                def ep1r(h=h, i=i, c=c, n=n):
                    def p1r(e, h=h, i=i, c=c, n=n):
                        if c == 0:
                            return e.reduce_max(negc[:, h * 16 + i: h * 16 + i + 1], mk.pb[6][:, 0:n], axis=AX, negate=True)
                        last = e.reduce_max(ycol[:, 0:1], mk.pb[6][:, 0:n], axis=AX, negate=True)
                        mk.fen(e, last)
                        return e.tensor_tensor(negc[:, h * 16 + i: h * 16 + i + 1],
                                               negc[:, h * 16 + i: h * 16 + i + 1], ycol[:, 0:1], ALU.min)
                    mk.op("vector", p1r, wait=[(mm, mm.n)], inc=[dv.inc()])
                    t_p1r[0] = dv.n
                ops.append((ep1, ep1r))
        return ops

    t_nrow_tab = [0, 0]
    def emit_nrow(h):
        par = h % 2
        ac = 8000 + par * 128
        def ntp(e, h=h):
            return e.transpose(mk.pb[4][0:16, 0:128], negc[:, h * 16:(h + 1) * 16], C["eye32"][:])
        mk.op("tensor", ntp, wait=[(dv, dv.n), (sc, sc.n)], inc=[mm.inc()])
        def nev(e, ac=ac):
            return e.tensor_copy(mk.A32[0:16, ac:ac + 128], mk.pb[4][0:16, 0:128])
        mk.op("vector", nev, wait=[(mm, mm.n)], inc=[dv.inc()])
        nr_flat = mk.dram["sc_small"].rearrange("a b -> (a b)")
        sst = "st_nrow" + str(par); sld = "ld_nrow" + str(par)
        mk.store(nr_flat[par * 2048:par * 2048 + 2048].rearrange("(i x) -> i x", i=16), mk.A32[0:16, ac:ac + 128],
                 q="sync", sem=sst, serialize=True, wait=[(dv, dv.n), (mk.sem(sld), mk.sem(sld).n)])
        nr_src = nr_flat[par * 2048:par * 2048 + 2048].rearrange("(one t) -> one t", one=1)
        mk.load(nrow2[0:1, par * 2048:par * 2048 + 2048], nr_src, q="gpsimd", sem=sld, serialize=True,
                wait=[(mk.sem(sst), mk.sem(sst).n), (mm, par_end[par])])
        t_nrow_tab[par] = mk.sem(sld).n

    for (ep1, ep1r) in mk_pass1_ops(0, first=True):
        ep1(); ep1r()
    emit_nrow(0)

    for h in range(G_NH):
        qsl, ksl = head_geom(h)
        nslot = nrow2[:, (h % 2) * 2048:(h % 2) * 2048 + 2048]
        t_nrow = t_nrow_tab[h % 2]
        nxt = mk_pass1_ops(h + 1) if h + 1 < G_NH else []
        nxi = [0]
        def pump():
            if nxi[0] < len(nxt):
                ep1, ep1r = nxt[nxi[0]]
                ep1(); ep1r()
                nxi[0] += 1
        for ib in range(4):
            iis = [ib * 4 + 0, ib * 4 + 1, ib * 4 + 2, ib * 4 + 3]
            for j in range(iis[-1] + 1):
                pbs = mk.pb[4 + j % 2]
                def p2(e, j=j, ib=ib, pbs=pbs, qsl=qsl, ksl=ksl, nslot=nslot):
                    e.matmul(pbs[:], ksl((j * 128, (j + 1) * 128)), qsl((ib * 512, (ib + 1) * 512)), start=True, stop=False)
                    return e.matmul(pbs[:], C["ones_row"][0:1, 0:128], nslot[0:1, ib * 512:(ib + 1) * 512], start=False, stop=True)
                mk.op("tensor", p2, wait=[(mk.sem("ld_nrow" + str(h % 2)), t_nrow), (sc, max(0, sc.n - 1))], inc=[mm.inc()])
                asl = attb[:, (j % 2) * 512:(j % 2) * 512 + 512]
                def p2e(e, pbs=pbs, asl=asl):
                    return e.activation(asl, pbs[:], AF.Exp, scale=0.125)
                mk.op("scalar", p2e, wait=[(mm, mm.n), (mm, attb_mm[j % 2])], inc=[sc.inc()])
                t_exp = sc.n
                masked = False
                if ib * 4 <= j <= iis[-1]:
                    def pmask(e, j=j, ib=ib, asl=asl):
                        sl2 = asl[:, (j - ib * 4) * 128:(j - ib * 4) * 128 + 128]
                        return e.tensor_tensor(sl2, sl2, C["caus01_fm16"][:], ALU.mult)
                    mk.op("vector", pmask, wait=[(sc, t_exp)], inc=[dv.inc()])
                    masked = True
                t_rdy = (dv, dv.n) if masked else (sc, t_exp)
                for i in iis:
                    if j > i:
                        continue
                    pby = mk.pb[i % 4][:, 0:65]
                    def ymm(e, h=h, i=i, j=j, ib=ib, asl=asl, pby=pby):
                        return e.matmul(pby, asl[:, (i - ib * 4) * 128:(i - ib * 4) * 128 + 128],
                                        vsb[:, j * 260 + (h // 4) * 65: j * 260 + (h // 4) * 65 + 65],
                                        start=(j == 0), stop=(j == i))
                    wy = [t_rdy]
                    if j == 0:
                        wy.append((dv, ybank_free[i % 4]))
                    mk.op("tensor", ymm, wait=wy, inc=[mm.inc()])
                attb_mm[j % 2] = mm.n
                pump()
            for i in iis:
                pby = mk.pb[i % 4][:, 0:65]
                def yev(e, h=h, i=i, pby=pby):
                    mk.fen(e, e.reciprocal(ycol[:, 1:2], pby[:, 64:65]))
                    return e.tensor_scalar(ctx[:, i * 1024 + h * 64: i * 1024 + h * 64 + 64], pby[:, 0:64],
                                           ycol[:, 1:2], None, ALU.mult)
                mk.op("vector", yev, wait=[(mm, mm.n), (dv, dv.n)], inc=[dv.inc()])
                ybank_free[i % 4] = dv.n
        while nxi[0] < len(nxt):
            pump()
        par_end[h % 2] = mm.n
        if h + 1 < G_NH:
            emit_nrow(h + 1)
    t_ctx = dv.n

    for i in range(TT):
        def ctp(e, i=i):
            last = None
            for k in range(8):
                last = e.transpose(mk.ptp[:, k * 128:(k + 1) * 128], ctx[:, i * 1024 + k * 128: i * 1024 + (k + 1) * 128], C["eye16"][:])
            return last
        w = [(dv, t_ctx)] if i == 0 else [(dv, dv.n)]
        mk.op("tensor", ctp, wait=w, inc=[mm.inc()])
        mk.op("vector", (lambda e: e.tensor_copy(woc[:], mk.ptp[:])), wait=[(mm, mm.n)], inc=[dv.inc()])
        if i == 0:
            wo_sb = xnl[:, 0:8192]
            mk.load(wo_sb, d["g_wo"].rearrange("(k p) o -> p k o", p=128), q="gpsimd", sem="ld_wo",
                    wait=[(mm, mm.n)])
        def wmm(e, i=i):
            last = None
            for k in range(8):
                last = e.matmul(mk.pb[(i % 2)][:, 0:512], woc[:, k * 128:(k + 1) * 128], wo_sb[:, k * 1024:k * 1024 + 512],
                                start=(k == 0), stop=(k == 7))
            for k in range(8):
                last = e.matmul(mk.pb[2 + (i % 2)][:, 0:512], woc[:, k * 128:(k + 1) * 128], wo_sb[:, k * 1024 + 512:(k + 1) * 1024],
                                start=(k == 0), stop=(k == 7))
            return last
        mk.op("tensor", wmm, wait=[(dv, dv.n), (mk.sem("ld_wo"), mk.sem("ld_wo").n)] + ([(dv, dv.n)] if i >= 2 else []), inc=[mm.inc()])
        xs = xat[:, (i % 2) * D:(i % 2 + 1) * D]
        os_ = osb[:, (i % 2) * D:(i % 2 + 1) * D]
        mk.load(xs, xacc_d[i * 128:(i + 1) * 128, :], q="sync", sem="ld_xa", serialize=True)
        def radd(e, i=i, xs=xs, os_=os_):
            e.scalar_tensor_tensor(os_[:, 0:512], mk.pb[(i % 2)][:, 0:512], 1.0, xs[:, 0:512], ALU.mult, ALU.add).then_inc(dv.h, 1)
            return e.scalar_tensor_tensor(os_[:, 512:1024], mk.pb[2 + (i % 2)][:, 0:512], 1.0, xs[:, 512:1024], ALU.mult, ALU.add)
        tokA = dv.inc(); tokB = dv.inc()
        mk.op("vector", radd, wait=[(mm, mm.n), (mk.sem("ld_xa"), mk.sem("ld_xa").n)], inc=[tokB])
        mk.op("sync", (lambda e: None), wait=[(dv, dv.n)])
        mk.store(out_d[i * 128:(i + 1) * 128, :], os_, q="sync", sem="st_out", serialize=True)
    mk.bar()


def l0f_attn_phase(mk, l, xin_d, xmid_d):
    nc = mk.nc
    mk.prog.phase = "l0fa"
    mk.reset()
    d = mk.dram
    C = mk.consts
    hs = mk.c16(16384)
    qfT = mk.c16(8192)
    kfT = mk.c16(8192)
    vtm = mk.c16(16 * 520)
    attw = mk.c16(70 * 128)
    wq2 = mk.c16(6144)
    octx = hs[:, 0:8192]
    wbuf = mk.c16(2 * 1536)
    nrow = wbuf[:, 0:2048]
    xt = mk.c32(2 * D)
    qk32 = mk.c32(2 * 1024)
    sqs = mk.c32(2048)
    hcol = mk.c32(32)
    tabq = mk.c32(2048)
    tabk = mk.c32(2048)
    qsb = mk.c32(2 * 64)
    qsrb = mk.c32(2 * 64)
    negc = mk.c32(8 * 16)
    nT = mk.c32(256)
    ld = mk.sem("ld_sync"); ldg = mk.sem("ld_gpsimd")
    mm = mk.sem("p_mm"); dv = mk.sem("p_dv"); sc = mk.sem("p_sc"); ev = mk.sem("p_ev")
    off = [0] * 17
    for j in range(16):
        off[j + 1] = off[j] + (min(j + 4, 15) - j + 1) * 128

    def qsrows(e):
        o = l * 64
        e.tensor_copy(nrow[0:1, 0:64], C["f_qs"][0:1, o:o + 64])
        e.tensor_copy(nrow[0:1, 64:96], C["f_qs"][0:1, o + 32:o + 64])
        e.tensor_copy(nrow[0:1, 96:128], C["f_qs"][0:1, o:o + 32])
        e.tensor_copy(nrow[0:1, 128:192], C["f_ks"][0:1, o:o + 64])
        e.tensor_copy(nrow[0:1, 192:224], C["f_ks"][0:1, o + 32:o + 64])
        return e.tensor_copy(nrow[0:1, 224:256], C["f_ks"][0:1, o:o + 32])
    mk.op("vector", qsrows, inc=[dv.inc()])
    def qsouter(e):
        return e.matmul(mk.pb[0][:, 0:256], C["ones_row"][0:1, 0:128], nrow[0:1, 0:256], start=True, stop=True)
    mk.op("tensor", qsouter, wait=[(dv, dv.n)], inc=[mm.inc()])
    def qstt(e):
        e.tensor_copy(qsb[:, 0:64], mk.pb[0][:, 0:64]).then_inc(dv.h, 1)
        e.tensor_copy(qsrb[:, 0:64], mk.pb[0][:, 64:128]).then_inc(dv.h, 1)
        e.tensor_copy(qsb[:, 64:128], mk.pb[0][:, 128:192]).then_inc(dv.h, 1)
        return e.tensor_copy(qsrb[:, 64:128], mk.pb[0][:, 192:256])
    toks = [dv.inc() for _ in range(4)]
    mk.op("vector", qstt, wait=[(mm, mm.n)], inc=[toks[3]])
    rotbuf = xt[:].rearrange("p x -> p x") if False else None
    for k, nm in enumerate(["t_cq", "t_sq", "t_ck", "t_sk"]):
        dst = xt[:, k * 1024:(k + 1) * 1024] if k < 2 else qk32[:, (k - 2) * 1024:(k - 1) * 1024]
        mk.load(dst, d[nm].rearrange("(i p) d -> p i d", p=128), q="sync", sem="ld_rot")
    t_rotld = mk.sem("ld_rot").n
    def tabs(e):
        v3 = lambda ap: ap.rearrange("p (i d) -> p i d", d=64)
        b3 = lambda ap: ap.rearrange("p (o d) -> p o d", o=1).broadcast_to([128, 16, 64])
        e.tensor_tensor(v3(tabq[:, 0:1024]), v3(xt[:, 0:1024]), b3(qsb[:, 0:64]), ALU.mult).then_inc(dv.h, 1)
        e.tensor_tensor(v3(tabq[:, 1024:2048]), v3(xt[:, 1024:2048]), b3(qsrb[:, 0:64]), ALU.mult).then_inc(dv.h, 1)
        e.tensor_tensor(v3(tabk[:, 0:1024]), v3(qk32[:, 0:1024]), b3(qsb[:, 64:128]), ALU.mult).then_inc(dv.h, 1)
        return e.tensor_tensor(v3(tabk[:, 1024:2048]), v3(qk32[:, 1024:2048]), b3(qsrb[:, 64:128]), ALU.mult)
    toks2 = [dv.inc() for _ in range(4)]
    mk.op("vector", tabs, wait=[(dv, dv.n - 4), (mk.sem("ld_rot"), t_rotld)], inc=[toks2[3]])
    t_tabs = dv.n

    for i in range(TT):
        x0 = xt[:, (i % 2) * D:(i % 2) * D + 512]
        x1 = xt[:, (i % 2) * D + 512:(i % 2) * D + 1024]
        sx = "ld_x" + str(i % 2)
        w_ld = [(mm, mm.n)] if i >= 2 else [(dv, t_tabs)]
        mk.load(x0, xin_d[i * 128:(i + 1) * 128, 0:512], q="sync", sem=sx, serialize=True, wait=w_ld)
        if i == 0:
            mk.load(x1[1:128, :], xin_d[0:127, 512:1024], q="sync", sem=sx)
            mk.op("vector", (lambda e, x1=x1: e.memset(x1[0:1, :], 0.0)),
                  wait=[(mk.sem(sx), mk.sem(sx).n)], inc=[dv.inc()])
        else:
            mk.load(x1, xin_d[i * 128 - 1:(i + 1) * 128 - 1, 512:1024], q="sync", sem=sx)
        t_x = mk.sem(sx).n
        cb = wbuf[:, (i % 2) * 1536:(i % 2) * 1536 + 1024]
        wv = [(mk.sem(sx), t_x), (dv, dv.n)]
        mk.op("vector", (lambda e, cb=cb, xx=xt[:, (i % 2) * D:(i % 2) * D + 1024]: e.tensor_copy(cb, xx)),
              wait=wv, inc=[dv.inc()])
        def tp(e, cb=cb):
            last = None
            for k in range(8):
                last = e.transpose(mk.ptp[:, k * 128:(k + 1) * 128], cb[:, k * 128:(k + 1) * 128], C["eye16"][:])
            return last
        mk.op("tensor", tp, wait=[(dv, dv.n)] + ([(ev, ev.n)] if i > 0 else []), inc=[mm.inc()])
        def ev8(e, i=i):
            last = None
            for k in range(8):
                last = e.tensor_copy(hs[:, k * 2048 + i * 128:k * 2048 + (i + 1) * 128], mk.ptp[:, k * 128:(k + 1) * 128])
            return last
        mk.op("vector", ev8, wait=[(mm, mm.n)], inc=[ev.inc()])
    t_hs = mk.sems["p_ev"].n

    mk.load(attw[:, 0:6144], d["f_wqkv"][l].rearrange("(k p) o -> p k o", p=128)[:, 0:4, :], q="gpsimd", sem="ld_wq")
    mk.load(wq2[:, 0:6144], d["f_wqkv"][l].rearrange("(k p) o -> p k o", p=128)[:, 4:8, :], q="gpsimd", sem="ld_wq")
    t_wq = mk.sem("ld_wq").n
    for i in range(TT):
        pq = mk.pb[0 if i % 2 == 0 else 3]
        pk = mk.pb[1 if i % 2 == 0 else 4]
        pv = mk.pb[2 if i % 2 == 0 else 5]
        def qkvmm(e, i=i, pq=pq, pk=pk, pv=pv):
            last = None
            for c, pbx in ((0, pq), (1, pk), (2, pv)):
                for k in range(8):
                    wsrc = (attw if k < 4 else wq2)[:, (k % 4) * 1536 + c * 512:(k % 4) * 1536 + c * 512 + 512]
                    last = e.matmul(pbx[:], hs[:, k * 2048 + i * 128:k * 2048 + (i + 1) * 128], wsrc,
                                    start=(k == 0), stop=(k == 7))
            return last
        w = [(mk.sem("ld_wq"), t_wq), (ev, t_hs), (dv, t_tabs)]
        if i >= 2:
            w.append((dv, dv.n - 1))
        mk.op("tensor", qkvmm, wait=w, inc=[mm.inc()])
        t_qkv = mm.n
        q32 = qk32[:, (i % 2) * 1024:(i % 2) * 1024 + 512]
        k32 = qk32[:, (i % 2) * 1024 + 512:(i % 2) * 1024 + 1024]
        def qev(e, pq=pq, pk=pk, q32=q32, k32=k32):
            e.tensor_copy(q32, pq[:]).then_inc(dv.h, 1)
            return e.tensor_copy(k32, pk[:])
        tka = dv.inc(); tkb = dv.inc()
        mk.op("vector", qev, wait=[(mm, t_qkv), (dv, max(0, dv.n - 2) if i >= 2 else 0)], inc=[tkb])
        def vev(e, i=i, pv=pv):
            last = None
            for hh in range(8):
                last = e.tensor_copy(vtm[:, i * 520 + hh * 65:i * 520 + hh * 65 + 64], pv[:, hh * 64:(hh + 1) * 64])
            return last
        mk.op("vector", vev, wait=[(mm, t_qkv)], inc=[dv.inc()])
        for sidx, (src32, tab, dstT) in enumerate(((q32, tabq, qfT), (k32, tabk, kfT))):
            sq_s = sqs[:, sidx * 1024:sidx * 1024 + 512]
            rt_s = sqs[:, sidx * 1024 + 512:sidx * 1024 + 1024]
            def sqr(e, src32=src32, sq_s=sq_s):
                return e.activation(sq_s, src32, AF.Square)
            mk.op("scalar", sqr, wait=[(dv, dv.n), (sc, max(0, sc.n - 3))], inc=[sc.inc()])
            def redn(e, sq_s=sq_s, sidx=sidx):
                return e.reduce_sum(hcol[:, sidx * 16:sidx * 16 + 8], sq_s.rearrange("p (h x) -> p h x", h=8), axis=AX)
            mk.op("vector", redn, wait=[(sc, sc.n)], inc=[dv.inc()])
            def rsq(e, sidx=sidx):
                return e.activation(hcol[:, sidx * 16 + 8:sidx * 16 + 16], hcol[:, sidx * 16:sidx * 16 + 8], AF.Sqrt)
            mk.op("scalar", rsq, wait=[(dv, dv.n)], inc=[sc.inc()])
            qfb = wbuf[:, (i % 2) * 1536 + sidx * 512:(i % 2) * 1536 + (sidx + 1) * 512]
            def rota(e, src32=src32, tab=tab, i=i, qfb=qfb, sidx=sidx, sq_s=sq_s, rt_s=rt_s):
                hc = hcol[:, sidx * 16 + 8:sidx * 16 + 16]
                hr = hcol[:, sidx * 16:sidx * 16 + 8]
                mk.fen(e, e.tensor_scalar(hc, hc, 1e-12, None, ALU.max))
                mk.fen(e, e.reciprocal(hr, hc))
                s4 = src32.rearrange("p (h a d) -> p h a d", h=8, a=2)
                cqb = tab[:, i * 64:(i + 1) * 64].rearrange("p (o a d) -> p o a d", o=1, a=2).broadcast_to([128, 8, 2, 32])
                sqb = tab[:, 1024 + i * 64:1024 + (i + 1) * 64].rearrange("p (o a d) -> p o a d", o=1, a=2).broadcast_to([128, 8, 2, 32])
                e.tensor_tensor(sq_s.rearrange("p (h a d) -> p h a d", h=8, a=2), s4, cqb, ALU.mult)
                mk.fen(e, e.tensor_tensor(rt_s.rearrange("p (h a d) -> p h a d", h=8, a=2), s4[:, :, ::-1, :], sqb, ALU.mult))
                mk.fen(e, e.tensor_tensor(sq_s, sq_s, rt_s, ALU.add))
                rb = hr.rearrange("p (h o) -> p h o", o=1).broadcast_to([128, 8, 64])
                return e.tensor_tensor(qfb.rearrange("p (h x) -> p h x", h=8), sq_s.rearrange("p (h x) -> p h x", h=8), rb, ALU.mult)
            mk.op("vector", rota, wait=[(sc, sc.n)], inc=[dv.inc()])
            t_rot = dv.n
            def tpq(e, qfb=qfb, sidx=sidx, i=i):
                last = None
                for b in range(4):
                    last = e.transpose(mk.ptp[:, (sidx * 4 + b) * 128:(sidx * 4 + b + 1) * 128], qfb[:, b * 128:(b + 1) * 128], C["eye16"][:])
                return last
            mk.op("tensor", tpq, wait=[(dv, t_rot)] + ([(ev, ev.n - 1)] if (i > 0 or sidx > 0) else []), inc=[mm.inc()])
            def evq(e, dstT=dstT, i=i, sidx=sidx):
                last = None
                for b in range(4):
                    last = e.tensor_copy(dstT[:, b * 2048 + i * 128:b * 2048 + (i + 1) * 128], mk.ptp[:, (sidx * 4 + b) * 128:(sidx * 4 + b + 1) * 128])
                return last
            mk.op("vector", evq, wait=[(mm, mm.n)], inc=[ev.inc()])
    mk.op("vector", (lambda e: e.memset(vtm[:].rearrange("p (i h c) -> p i h c", i=16, h=8)[:, :, :, 64:65], 1.0)), inc=[dv.inc()])
    t_proj = (mm.n, dv.n, mk.sems["p_ev"].n)

    nrow2 = hs[:, 8192:12288]
    t_p1r = [0]
    par_end = [mm.n, mm.n]
    t_nrow_tab = [0, 0]
    ybank2 = getattr(mk, "_l0f_ybank2", None) or [0, 0]
    mk._l0f_ybank2 = ybank2

    def head_geom(h):
        qsl = lambda a, b, h=h: qfT[(h % 2) * 64:(h % 2) * 64 + 64, (h // 2) * 2048 + a:(h // 2) * 2048 + b]
        ksl = lambda a, b, h=h: kfT[(h % 2) * 64:(h % 2) * 64 + 64, (h // 2) * 2048 + a:(h // 2) * 2048 + b]
        return qsl, ksl

    def mk_pass1_ops(h, first=False):
        qsl, ksl = head_geom(h)
        ops = []
        for i in range(TT):
            lo = max(0, i - 4) * 128
            wdt = (i + 1) * 128 - lo
            for c in range((wdt + 511) // 512):
                n = min(512, wdt - c * 512)
                def ep1(i=i, c=c, n=n, lo=lo, qsl=qsl, ksl=ksl, fw=(first and i == 0 and c == 0)):
                    def p1(e, i=i, c=c, n=n, lo=lo, qsl=qsl, ksl=ksl):
                        return e.matmul(mk.pb[6][:, 0:n], qsl(i * 128, (i + 1) * 128), ksl(lo + c * 512, lo + c * 512 + n),
                                        start=True, stop=True)
                    w = [(dv, t_proj[1]), (mk.sems["p_ev"], t_proj[2])] if fw else [(dv, t_p1r[0])]
                    mk.op("tensor", p1, wait=w, inc=[mm.inc()])
                def ep1r(h=h, i=i, c=c, n=n):
                    def p1r(e, h=h, i=i, c=c, n=n):
                        if c == 0:
                            return e.reduce_max(negc[:, h * 16 + i:h * 16 + i + 1], mk.pb[6][:, 0:n], axis=AX, negate=True)
                        last = e.reduce_max(hcol[:, 0:1], mk.pb[6][:, 0:n], axis=AX, negate=True)
                        mk.fen(e, last)
                        return e.tensor_tensor(negc[:, h * 16 + i:h * 16 + i + 1], negc[:, h * 16 + i:h * 16 + i + 1], hcol[:, 0:1], ALU.min)
                    mk.op("vector", p1r, wait=[(mm, mm.n)], inc=[dv.inc()])
                    t_p1r[0] = dv.n
                ops.append((ep1, ep1r))
        return ops

    def emit_nrow(h):
        par = h % 2
        def ntp(e, h=h):
            return e.transpose(mk.pb[4][0:16, 0:128], negc[:, h * 16:(h + 1) * 16], C["eye32"][:])
        mk.op("tensor", ntp, wait=[(dv, dv.n), (sc, sc.n)], inc=[mm.inc()])
        mk.op("vector", (lambda e, par=par: e.tensor_copy(nT[0:16, par * 128:par * 128 + 128], mk.pb[4][0:16, 0:128])),
              wait=[(mm, mm.n)], inc=[dv.inc()])
        nr_flat = mk.dram["sc_small"].rearrange("a b -> (a b)")
        sst = "st_nrow" + str(par); sld = "ld_nrow" + str(par)
        mk.store(nr_flat[par * 2048:par * 2048 + 2048].rearrange("(i x) -> i x", i=16), nT[0:16, par * 128:par * 128 + 128],
                 q="sync", sem=sst, serialize=True, wait=[(dv, dv.n), (mk.sem(sld), mk.sem(sld).n)])
        mk.load(nrow2[0:1, par * 2048:par * 2048 + 2048], nr_flat[par * 2048:par * 2048 + 2048].rearrange("(one t) -> one t", one=1),
                q="gpsimd", sem=sld, serialize=True, wait=[(mk.sem(sst), mk.sem(sst).n), (mm, par_end[par])])
        t_nrow_tab[par] = mk.sem(sld).n

    for (ep1, ep1r) in mk_pass1_ops(0, first=True):
        ep1(); ep1r()
    emit_nrow(0)

    for h in range(F_HEADS):
        nslot = nrow2[:, (h % 2) * 2048:(h % 2) * 2048 + 2048]
        t_nrow = t_nrow_tab[h % 2]
        nxt = mk_pass1_ops(h + 1) if h + 1 < F_HEADS else []
        nxi = [0]
        def pump():
            if nxi[0] < len(nxt):
                ep1, ep1r = nxt[nxi[0]]
                ep1(); ep1r()
                nxi[0] += 1
        def emit_ymm(i, t_dv, t_sc, h=h):
            pby = mk.pb[2 + i % 2][:, 0:65]
            jlo = max(0, i - 4)
            def ymm(e, h=h, i=i, pby=pby, jlo=jlo):
                last = None
                for j in range(jlo, i + 1):
                    last = e.matmul(pby, attw[:, off[j] + (i - j) * 128:off[j] + (i - j) * 128 + 128],
                                    vtm[:, j * 520 + h * 65:j * 520 + h * 65 + 65],
                                    start=(j == jlo), stop=(j == i))
                return last
            mk.op("tensor", ymm, wait=[(dv, t_dv), (sc, t_sc), (dv, ybank2[i % 2])], inc=[mm.inc()])
            def yev(e, h=h, i=i, pby=pby):
                mk.fen(e, e.reciprocal(hcol[:, 1:2], pby[:, 64:65]))
                return e.tensor_scalar(octx[:, i * 512 + h * 64:i * 512 + h * 64 + 64], pby[:, 0:64],
                                       hcol[:, 1:2], None, ALU.mult)
            mk.op("vector", yev, wait=[(mm, mm.n), (dv, dv.n)], inc=[dv.inc()])
            ybank2[i % 2] = dv.n
        pend = None
        for j in range(TT):
            t0 = j * 128
            t1 = min(j + 5, 16) * 128
            spans = []
            a = t0
            while a < t1:
                b = min(a + 512, t1)
                spans.append((a, b))
                a = b
            for (a, b) in spans:
                pbk = mk.pb[4 + (j % 2)]
                def p2(e, j=j, a=a, b=b, pbk=pbk, nslot=nslot, h=h):
                    qsl, ksl = head_geom(h)
                    e.matmul(pbk[:, 0:b - a], ksl(j * 128, (j + 1) * 128), qsl(a, b), start=True, stop=False)
                    return e.matmul(pbk[:, 0:b - a], C["ones_row"][0:1, 0:128], nslot[0:1, a:b], start=False, stop=True)
                mk.op("tensor", p2, wait=[(mk.sem("ld_nrow" + str(h % 2)), t_nrow), (sc, sc.n)], inc=[mm.inc()])
                def p2e(e, j=j, a=a, b=b, pbk=pbk, t0=t0):
                    return e.activation(attw[:, off[j] + (a - t0):off[j] + (b - t0)], pbk[:, 0:b - a], AF.Exp, scale=8.0)
                mk.op("scalar", p2e, wait=[(mm, mm.n)], inc=[sc.inc()])
                pump()
            def pmask(e, j=j, t0=t0, t1=t1):
                last = e.tensor_tensor(attw[:, off[j]:off[j] + 128], attw[:, off[j]:off[j] + 128], C["caus01_fm16"][:], ALU.mult)
                if j + 4 < 16:
                    last = e.tensor_tensor(attw[:, off[j] + 512:off[j] + 640], attw[:, off[j] + 512:off[j] + 640],
                                           C["anti01_fm16"][:], ALU.mult)
                return last
            mk.op("vector", pmask, wait=[(sc, sc.n)], inc=[dv.inc()])
            if pend is not None:
                emit_ymm(*pend)
            pend = (j, dv.n, sc.n)
        emit_ymm(*pend)
        while nxi[0] < len(nxt):
            pump()
        par_end[h % 2] = mm.n
        if h + 1 < F_HEADS:
            emit_nrow(h + 1)
    t_ctx = dv.n

    mk.load(attw[:, 0:4096], d["f_wo"][l].rearrange("(k p) o -> p k o", p=128), q="gpsimd", sem="ld_wo",
            serialize=True, wait=[(mm, mm.n)])
    t_wo = mk.sem("ld_wo").n
    for i in range(TT):
        def ctp(e, i=i):
            last = None
            for b in range(4):
                last = e.transpose(mk.ptp[:, b * 128:(b + 1) * 128], octx[:, i * 512 + b * 128:i * 512 + (b + 1) * 128], C["eye16"][:])
            return last
        w = [(dv, t_ctx)] if i == 0 else [(dv, dv.n)]
        mk.op("tensor", ctp, wait=w + ([(ev, ev.n)] if i > 0 else []), inc=[mm.inc()])
        oT = wbuf[:, (i % 2) * 1536:(i % 2) * 1536 + 512]
        mk.op("vector", (lambda e, oT=oT: e.tensor_copy(oT, mk.ptp[:, 0:512])), wait=[(mm, mm.n)], inc=[ev.inc()])
        def wmm(e, i=i, oT=oT):
            last = None
            for dc in range(2):
                for k in range(4):
                    last = e.matmul(mk.pb[2 + (i % 2) * 2 + dc][:, 0:512] if False else mk.pb[2 + dc][:, 0:512],
                                    oT[:, k * 128:(k + 1) * 128], attw[:, k * 1024 + dc * 512:k * 1024 + dc * 512 + 512],
                                    start=(k == 0), stop=(k == 3))
            return last
        mk.op("tensor", wmm, wait=[(mk.sem("ld_wo"), t_wo), (mk.sems["p_ev"], mk.sems["p_ev"].n)] + ([(dv, dv.n)] if i >= 1 else []), inc=[mm.inc()])
        xs = xt[:, (i % 2) * D:(i % 2 + 1) * D]
        mk.load(xs, xin_d[i * 128:(i + 1) * 128, :], q="sync", sem="ld_xr" + str(i % 2), serialize=True)
        os_ = qk32[:, (i % 2) * 1024:(i % 2) * 1024 + 1024]
        def radd(e, i=i, xs=xs, os_=os_):
            e.scalar_tensor_tensor(os_[:, 0:512], mk.pb[2][:, 0:512], 1.0, xs[:, 0:512], ALU.mult, ALU.add).then_inc(dv.h, 1)
            return e.scalar_tensor_tensor(os_[:, 512:1024], mk.pb[3][:, 0:512], 1.0, xs[:, 512:1024], ALU.mult, ALU.add)
        ta = dv.inc(); tb = dv.inc()
        mk.op("vector", radd, wait=[(mm, mm.n), (mk.sem("ld_xr" + str(i % 2)), mk.sem("ld_xr" + str(i % 2)).n)], inc=[tb])
        mk.store(xmid_d[i * 128:(i + 1) * 128, :], os_, q="sync", sem="st_xm", serialize=True, wait=[(dv, dv.n)])
    mk.bar()


def l0f_mlp_phase(mk, l, xmid_d, xout_d, xa_in_d=None):
    nc = mk.nc
    mk.prog.phase = "l0fm"
    mk.reset()
    d = mk.dram
    C = mk.consts
    hln = mk.c16(8192)
    h3 = mk.c16(22528)
    w2s = mk.c16(22528)
    w1c = mk.c16(2 * 2048)
    ab = mk.c16(2 * 1024)
    cb = ab[:, 0:1024]
    xt = mk.c32(2 * D)
    sq = mk.c32(2 * D)
    cols = mk.c32(8)
    lgb = mk.c32(2 * D)
    b1c = mk.c32(2 * 22)
    osb = mk.c32(2 * D)
    xab = mk.c32(2 * D)
    ld = mk.sem("ld_sync")
    mm = mk.sem("p_mm"); dv = mk.sem("p_dv"); sc = mk.sem("p_sc"); ev = mk.sem("p_ev")

    mk.load(w2s[:], d["f_w2"][l].rearrange("(k p) dd -> p k dd", p=128), q="gpsimd", sem="ld_w2", serialize=True)
    mk.load(b1c[:, 0:22], d["f_b1a"][l], q="sync", sem="ld_b1", serialize=True)
    mk.load(b1c[:, 22:44], d["f_b1g"][l], q="sync", sem="ld_b1")
    t_b1 = mk.sem("ld_b1").n
    t_w2 = mk.sem("ld_w2").n
    def lgouter(e, l=l):
        e.matmul(mk.pb[6][:, 0:512], C["ones_row"][0:1, 0:128], C["f_lng"][0:1, l * D:l * D + 512], start=True, stop=True)
        e.matmul(mk.pb[6][:, 0:512], C["ones_row"][0:1, 0:128], C["f_lng"][0:1, l * D:l * D + 512], start=True, stop=True) if False else None
        return e.matmul(mk.pb[5][:, 0:512], C["ones_row"][0:1, 0:128], C["f_lng"][0:1, l * D + 512:(l + 1) * D], start=True, stop=True)
    mk.op("tensor", lgouter, inc=[mm.inc()])
    mk.op("vector", (lambda e: e.tensor_copy(lgb[:, 0:512], mk.pb[6][:, 0:512])), wait=[(mm, mm.n)], inc=[dv.inc()])
    mk.op("vector", (lambda e: e.tensor_copy(lgb[:, 512:1024], mk.pb[5][:, 0:512])), wait=[(mm, mm.n)], inc=[dv.inc()])
    def lbouter(e, l=l):
        e.matmul(mk.pb[6][:, 0:512], C["ones_row"][0:1, 0:128], C["f_lnb"][0:1, l * D:l * D + 512], start=True, stop=True)
        return e.matmul(mk.pb[5][:, 0:512], C["ones_row"][0:1, 0:128], C["f_lnb"][0:1, l * D + 512:(l + 1) * D], start=True, stop=True)
    mk.op("tensor", lbouter, wait=[(dv, dv.n)], inc=[mm.inc()])
    mk.op("vector", (lambda e: e.tensor_copy(lgb[:, 1024:1536], mk.pb[6][:, 0:512])), wait=[(mm, mm.n)], inc=[dv.inc()])
    mk.op("vector", (lambda e: e.tensor_copy(lgb[:, 1536:2048], mk.pb[5][:, 0:512])), wait=[(mm, mm.n)], inc=[dv.inc()])
    t_lg = dv.n

    t_w1mm = [0, 0]
    t_norm = [0, 0]
    t_red = [0, 0]
    t_tpc = [0, 0]
    t_ms = [0]
    for half in range(2):
        tbase = half * 8
        for ii in range(8):
            i = tbase + ii
            x0 = xt[:, (ii % 2) * D:(ii % 2) * D + 512]
            x1 = xt[:, (ii % 2) * D + 512:(ii % 2) * D + 1024]
            sx = "ld_x" + str(ii % 2)
            wv = [(dv, t_norm[ii % 2]), (sc, t_red[ii % 2])] if (half > 0 or ii >= 2) else []
            mk.load(x0, xmid_d[i * 128:(i + 1) * 128, 0:512], q="sync", sem=sx, serialize=True, wait=wv)
            if i == 0:
                mk.load(x1[1:128, :], xmid_d[0:127, 512:1024], q="sync", sem=sx)
                mk.op("vector", (lambda e, x1=x1: e.memset(x1[0:1, :], 0.0)),
                      wait=[(mk.sem(sx), mk.sem(sx).n)], inc=[dv.inc()])
                t_ms[0] = dv.n
            else:
                mk.load(x1, xmid_d[i * 128 - 1:(i + 1) * 128 - 1, 512:1024], q="sync", sem=sx)
            t_x = mk.sem(sx).n
            xfull = xt[:, (ii % 2) * D:(ii % 2) * D + 1024]
            sqs_ = sq[:, (ii % 2) * D:(ii % 2) * D + 1024]
            def red(e, x=xfull, s=sqs_, c=cols):
                return e.activation(s, x, AF.Square)
            mk.op("scalar", red, wait=[(mk.sem(sx), t_x), (dv, max(t_norm[ii % 2], t_ms[0])), (sc, max(0, sc.n - 1))], inc=[sc.inc()])
            t_red[ii % 2] = sc.n
            def red2(e, x=xfull, s=sqs_, c=cols):
                e.reduce_sum(c[:, 0:1], x, axis=AX, negate=True).then_inc(dv.h, 1)
                return e.reduce_sum(c[:, 2:3], s, axis=AX)
            tk1 = dv.inc(); tk2 = dv.inc()
            mk.op("vector", red2, wait=[(sc, sc.n)], inc=[tk2])
            def smu(e, c=cols):
                e.activation(c[:, 1:2], c[:, 0:1], AF.Copy, scale=1.0 / D).then_inc(sc.h, 1)
                return e.activation(c[:, 3:4], c[:, 0:1], AF.Square, scale=1.0 / D)
            tk3 = sc.inc(); tk4 = sc.inc()
            mk.op("scalar", smu, wait=[(dv, dv.n)], inc=[tk4])
            def var(e, c=cols):
                return e.tensor_scalar(c[:, 4:5], c[:, 2:3], 1.0 / D, c[:, 3:4], ALU.mult, ALU.subtract)
            mk.op("vector", var, wait=[(sc, sc.n)], inc=[dv.inc()])
            def rstd(e, c=cols):
                mk.fen(e, e.activation(c[:, 5:6], c[:, 4:5], AF.Ln, bias=mk.fc(1e-5)))
                return e.activation(c[:, 6:7], c[:, 5:6], AF.Exp, scale=-0.5)
            mk.op("scalar", rstd, wait=[(dv, dv.n)], inc=[sc.inc()])
            cbs = ab[:, (ii % 2) * 1024:(ii % 2) * 1024 + 1024]
            def norm(e, x=xfull, c=cols, s=sqs_, cbs=cbs):
                mk.fen(e, e.tensor_scalar(s, x, c[:, 1:2], c[:, 6:7], ALU.add, ALU.mult))
                mk.fen(e, e.tensor_tensor(s, s, lgb[:, 0:1024], ALU.mult))
                return e.tensor_tensor(cbs[:], s, lgb[:, 1024:2048], ALU.add)
            mk.op("vector", norm, wait=[(sc, sc.n), (dv, t_lg), (mm, t_tpc[ii % 2])], inc=[dv.inc()])
            t_norm[ii % 2] = dv.n
            def tp(e, cbs=cbs):
                last = None
                for k in range(8):
                    last = e.transpose(mk.ptp[:, k * 128:(k + 1) * 128], cbs[:, k * 128:(k + 1) * 128], C["eye16"][:])
                return last
            mk.op("tensor", tp, wait=[(dv, dv.n)] + ([(ev, ev.n)] if (half > 0 or ii > 0) else []), inc=[mm.inc()])
            t_tpc[ii % 2] = mm.n
            def ev8(e, ii=ii):
                last = None
                for k in range(8):
                    last = e.tensor_copy(hln[:, k * 1024 + ii * 128:k * 1024 + (ii + 1) * 128], mk.ptp[:, k * 128:(k + 1) * 128])
                return last
            mk.op("vector", ev8, wait=[(mm, mm.n)], inc=[ev.inc()])
        t_hln = (mk.sems["p_ev"].n, dv.n)
        for hk in range(22):
            wsl = w1c[:, (hk % 2) * 2048:(hk % 2) * 2048 + 2048]
            wA = d["f_w1a"][l].rearrange("(k p) o -> p k o", p=128)[:, :, hk * 128:(hk + 1) * 128]
            wG = d["f_w1g"][l].rearrange("(k p) o -> p k o", p=128)[:, :, hk * 128:(hk + 1) * 128]
            wld = [(mm, t_w1mm[hk % 2])] if (half > 0 or hk >= 2) else []
            mk.load(wsl[:, 0:1024], wA, q="gpsimd", sem="ld_w1", serialize=True, wait=wld)
            mk.load(wsl[:, 1024:2048], wG, q="gpsimd", sem="ld_w1")
            t_w1 = mk.sem("ld_w1").n
            for tc in range(2):
                pa = mk.pb[tc % 2]; pg = mk.pb[2 + tc % 2]
                def w1mm(e, hk=hk, tc=tc, pa=pa, pg=pg, wsl=wsl):
                    last = None
                    for k in range(8):
                        last = e.matmul(pa[:], wsl[:, k * 128:(k + 1) * 128], hln[:, k * 1024 + tc * 512:k * 1024 + (tc + 1) * 512],
                                        start=(k == 0), stop=(k == 7))
                    for k in range(8):
                        last = e.matmul(pg[:], wsl[:, 1024 + k * 128:1024 + (k + 1) * 128], hln[:, k * 1024 + tc * 512:k * 1024 + (tc + 1) * 512],
                                        start=(k == 0), stop=(k == 7))
                    return last
                w = [(mk.sem("ld_w1"), t_w1), (mk.sems["p_ev"], t_hln[0])]
                if hk * 2 + tc >= 2:
                    w.append((dv, dv.n - 1))
                mk.op("tensor", w1mm, wait=w, inc=[mm.inc()])
                def gel(e, hk=hk, tc=tc, pg=pg):
                    return e.activation(ab[:, (tc % 2) * 1024 + 512:(tc % 2) * 1024 + 1024], pg[:], AF.Gelu,
                                        bias=b1c[:, 22 + hk:23 + hk])
                mk.op("scalar", gel, wait=[(mm, mm.n), (mk.sem("ld_b1"), t_b1)], inc=[sc.inc()])
                def aev(e, hk=hk, tc=tc, pa=pa):
                    mk.fen(e, e.tensor_scalar(ab[:, (tc % 2) * 1024:(tc % 2) * 1024 + 512], pa[:], b1c[:, hk:hk + 1], None, ALU.add))
                    return e.tensor_tensor(h3[:, hk * 1024 + tc * 512:hk * 1024 + (tc + 1) * 512],
                                           ab[:, (tc % 2) * 1024:(tc % 2) * 1024 + 512],
                                           ab[:, (tc % 2) * 1024 + 512:(tc % 2) * 1024 + 1024], ALU.mult)
                mk.op("vector", aev, wait=[(sc, sc.n), (mm, mm.n), (mk.sem("ld_b1"), t_b1)], inc=[dv.inc()])
            t_w1mm[hk % 2] = mm.n
        t_h3 = dv.n
        for ii in range(8):
            i = tbase + ii
            for dc in range(2):
                py = mk.pb[4 + (ii * 2 + dc) % 2]
                def w2mm(e, ii=ii, dc=dc, py=py, l=l):
                    last = e.matmul(py[:], C["ones_row"][0:1, 0:128], C["f_b2"][0:1, l * D + dc * 512:l * D + (dc + 1) * 512],
                                    start=True, stop=False)
                    for k in range(22):
                        last = e.matmul(py[:], h3[:, k * 1024 + ii * 128:k * 1024 + (ii + 1) * 128],
                                        w2s[:, k * 1024 + dc * 512:k * 1024 + (dc + 1) * 512],
                                        start=False, stop=(k == 21))
                    return last
                w = [(mk.sem("ld_w2"), t_w2), (dv, t_h3)]
                if ii * 2 + dc >= 2:
                    w.append((ev, ev.n - 1))
                mk.op("tensor", w2mm, wait=w, inc=[mm.inc()])
                xs = xt[:, (ii % 2) * D + dc * 512:(ii % 2) * D + (dc + 1) * 512]
                sxr = "ld_xr" + str(ii % 2)
                mk.load(xs, xmid_d[i * 128:(i + 1) * 128, dc * 512:(dc + 1) * 512], q="sync", sem=sxr, serialize=True,
                        wait=[(dv, dv.n)])
                if xa_in_d is not None:
                    xa = xab[:, (ii % 2) * D + dc * 512:(ii % 2) * D + (dc + 1) * 512]
                    mk.load(xa, xa_in_d[i * 128:(i + 1) * 128, dc * 512:(dc + 1) * 512], q="sync", sem=sxr)
                os_ = osb[:, (ii % 2) * D + dc * 512:(ii % 2) * D + (dc + 1) * 512]
                def radd(e, xs=xs, os_=os_, py=py, ii=ii, dc=dc):
                    r = e.scalar_tensor_tensor(os_, py[:], 1.0, xs, ALU.mult, ALU.add)
                    if xa_in_d is not None:
                        xa = xab[:, (ii % 2) * D + dc * 512:(ii % 2) * D + (dc + 1) * 512]
                        mk.fen(e, r)
                        r = e.tensor_tensor(os_, os_, xa, ALU.add)
                    return r
                mk.op("vector", radd, wait=[(mm, mm.n), (mk.sem("ld_xr" + str(ii % 2)), mk.sem("ld_xr" + str(ii % 2)).n)], inc=[ev.inc()])
                mk.store(xout_d[i * 128:(i + 1) * 128, dc * 512:(dc + 1) * 512], os_, q="sync", sem="st_xo",
                         serialize=True, wait=[(ev, ev.n)])
    mk.bar()


def l0a_prep_phase(mk, xnfm_d):
    nc = mk.nc
    mk.prog.phase = "l0ap"
    mk.reset()
    d = mk.dram
    C = mk.consts
    xnl = mk.c16(16384)
    wrb = mk.c16(2 * 1024)
    lam_tm = mk.c16(8192)
    cb16 = mk.c16(2048)
    vst = mk.c16(2 * 128)
    xpad = mk.c32(2051)
    cacc = mk.c32(2048)
    csc2 = mk.c32(2048)
    dtg = mk.c32(3 * 2048)
    cstage = mk.c32(2 * 512)
    dtm = mk.c32(512)
    mcols = mk.c32(16)
    ld = mk.sem("ld_sync"); ldg = mk.sem("ld_gpsimd")
    mm = mk.sem("p_mm"); dv = mk.sem("p_dv"); sc = mk.sem("p_sc"); ev = mk.sem("p_ev")

    mk.load(xnl[:], xnfm_d.rearrange("(k p) t -> p k t", p=128), q="sync")
    t_xn = ld.n
    DT = dtg[0:32, 0:2048]
    GC = dtg[0:32, 2048:4096]
    SC3 = dtg[0:32, 4096:6144]

    t_rmm = [0, 0]
    def rest_tile(o):
        wsl = wrb[:, (o % 2) * 1024:(o % 2) * 1024 + 1024]
        mk.load(wsl, d["a_wr"].rearrange("(k p) o -> p k o", p=128)[:, :, o * 128:(o + 1) * 128],
                q="gpsimd", sem="ld_wr", serialize=True, wait=[(mm, t_rmm[o % 2])])
        t_w = mk.sem("ld_wr").n
        slot = xpad[:, 0:2051]
        for tc in range(4):
            pbk = mk.pb[tc % 2]
            def rmm(e, tc=tc, pbk=pbk, wsl=wsl):
                last = None
                for k in range(8):
                    last = e.matmul(pbk[:], wsl[:, k * 128:(k + 1) * 128],
                                    xnl[:, k * 2048 + tc * 512:k * 2048 + (tc + 1) * 512],
                                    start=(k == 0), stop=(k == 7))
                return last
            w = [(mk.sem("ld_wr"), t_w), (ld, t_xn), (dv, dv.n)]
            mk.op("tensor", rmm, wait=w, inc=[mm.inc()])
            t_rmm[o % 2] = mm.n
            def bev(e, o=o, tc=tc, pbk=pbk, slot=slot):
                return e.tensor_scalar(slot[:, 3 + tc * 512:3 + (tc + 1) * 512], pbk[:],
                                       C["a_br32"][:, o:o + 1], None, ALU.add)
            mk.op("vector", bev, wait=[(mm, mm.n)], inc=[dv.inc()])
        mk.op("vector", (lambda e, slot=slot: e.memset(slot[:, 0:3], 0.0)), inc=[dv.inc()])
        return slot, dv.n

    def softplus(e, dst, x, scr):
        mk.fen(e, e.tensor_scalar(scr, x, 0.0, None, ALU.abs_max))
        return scr

    lam_done = []
    for o in range(18, 22):
        slot, t_slot = rest_tile(o)
        if o == 18:
            def dtp(e, slot=slot):
                return e.tensor_scalar(SC3, slot[0:32, 3:2051], C["a_dtb"][:, 0:1], None, ALU.add)
            mk.op("vector", dtp, wait=[(dv, t_slot)], inc=[dv.inc()])
            def dte(e):
                mk.fen(e, e.activation(DT, SC3, AF.Abs))
                return e.activation(GC, DT, AF.Exp, scale=-1.0)
            mk.op("scalar", dte, wait=[(dv, dv.n)], inc=[sc.inc()])
            def dtl(e):
                return e.activation(GC, GC, AF.Ln, bias=mk.fc(1.0, 32))
            mk.op("scalar", dtl, wait=[(sc, sc.n)], inc=[sc.inc()])
            def dtf(e, slot=slot):
                mk.fen(e, e.tensor_scalar(DT, SC3, 0.0, None, ALU.max))
                return e.tensor_tensor(DT, DT, GC, ALU.add)
            mk.op("vector", dtf, wait=[(sc, sc.n)], inc=[dv.inc()])
            t_dt = dv.n
        def lm1(e, o=o, slot=slot):
            return e.tensor_scalar(cacc[:], slot[:, 3:2051], C["a_L"][:, o - 18:o - 17], None, ALU.mult)
        mk.op("vector", lm1, wait=[(dv, dv.n), (sc, sc.n)], inc=[dv.inc()])
        def lm2(e):
            mk.fen(e, e.activation(csc2[:], cacc[:], AF.Abs))
            mk.fen(e, e.activation(csc2[:], csc2[:], AF.Exp, scale=-1.0))
            return e.activation(csc2[:], csc2[:], AF.Ln, bias=mk.fc(1.0))
        mk.op("scalar", lm2, wait=[(dv, dv.n)], inc=[sc.inc()])
        def lm3(e):
            mk.fen(e, e.tensor_scalar(cacc[:], cacc[:], 0.0, None, ALU.max))
            mk.fen(e, e.tensor_tensor(cacc[:], cacc[:], csc2[:], ALU.add))
            return e.tensor_copy(cb16[:], cacc[:])
        mk.op("vector", lm3, wait=[(sc, sc.n)], inc=[dv.inc()])
        def lmt(e, o=o):
            last = None
            for i in range(16):
                last = e.transpose(mk.ptp[:, (i % 8) * 128:(i % 8) * 128 + 128], cb16[:, i * 128:(i + 1) * 128], C["eye16"][:])
                if i % 8 == 7:
                    pass
            return last
        for wv in range(2):
            def lmtw(e, o=o, wv=wv):
                last = None
                for i2 in range(8):
                    i = wv * 8 + i2
                    last = e.transpose(mk.ptp[:, i2 * 128:(i2 + 1) * 128], cb16[:, i * 128:(i + 1) * 128], C["eye16"][:])
                return last
            mk.op("tensor", lmtw, wait=[(dv, dv.n)] + ([(ev, ev.n)] if (o > 18 or wv > 0) else []), inc=[mm.inc()])
            def lmev(e, o=o, wv=wv):
                last = None
                for i2 in range(8):
                    i = wv * 8 + i2
                    last = e.tensor_copy(lam_tm[:, i * 512 + (o - 18) * 128:i * 512 + (o - 18) * 128 + 128],
                                         mk.ptp[:, i2 * 128:(i2 + 1) * 128])
                return last
            mk.op("vector", lmev, wait=[(mm, mm.n)], inc=[ev.inc()])
    t_lam = mk.sems["p_ev"].n
    mk.store(d["sc_lam"], lam_tm[:], q="sync", sem="st_lam", wait=[(mk.sems["p_ev"], t_lam)])

    def gmk(e):
        mk.fen(e, e.tensor_scalar(SC3, DT, C["a_negA"][:, 0:1], None, ALU.mult))
        mk.fen(e, e.tensor_tensor_scan(GC, SC3, SC3, 0.0, ALU.add, ALU.bypass))
        mk.fen(e, e.reciprocal(SC3, DT))
        return e.tensor_scalar(SC3, SC3, C["a_D"][:, 0:1], None, ALU.mult)
    mk.op("vector", gmk, wait=[(dv, t_dt), (mk.sems["p_ev"], t_lam)], inc=[dv.inc()])
    t_gc = dv.n
    mk.store(d["sc_gc"], GC, q="sync", sem="st_gc", wait=[(dv, t_gc)])
    for si, (srcm, dsto) in enumerate(((GC, 4096), (SC3, 4608), (DT, 5120))):
        slot = cstage[:, (si % 2) * 512:(si % 2) * 512 + 512]
        ident = C["eye32"][0:32, 0:32]
        for i in range(TT):
            pf = mk.pb[5][:, (i % 2) * 128:(i % 2) * 128 + 32]
            def tpf(e, srcm=srcm, i=i, pf=pf, ident=ident):
                return e.transpose(pf, srcm[:, i * 128:(i + 1) * 128], ident)
            mk.op("tensor", tpf, wait=[(dv, dv.n)], inc=[mm.inc()])
            def evf(e, slot=slot, i=i, pf=pf, srcm=srcm):
                if srcm is DT:
                    return e.tensor_copy(dtm[:, i * 32:(i + 1) * 32], pf)
                return e.tensor_copy(slot[:, i * 32:(i + 1) * 32], pf)
            w4 = [(mm, mm.n)]
            if i == 0 and si >= 2:
                w4.append((mk.sem("st_cT"), mk.sem("st_cT").n))
            mk.op("vector", evf, wait=w4, inc=[dv.inc()])
        mk.store(d["sc_cols"][:, dsto:dsto + 512], dtm[:] if srcm is DT else slot, q="sync", sem="st_cT",
                 serialize=True, wait=[(dv, dv.n)])
    for eL in range(4):
        C_ = 128 << eL
        nb = 2048 // C_
        gcv = GC.rearrange("p (b c) -> p b c", c=C_)
        def cA(e, eL=eL, C_=C_, nb=nb, gcv=gcv):
            mx = GC[:, C_ - 1::C_].rearrange("p (b o) -> p b o", o=1).broadcast_to([32, nb, C_])
            return e.tensor_tensor(cacc[0:32, :].rearrange("p (b c) -> p b c", c=C_), mx, gcv, ALU.subtract)
        mk.op("vector", cA, wait=[(dv, dv.n), (sc, sc.n)], inc=[dv.inc()])
        mk.op("scalar", (lambda e: e.activation(cacc[0:32, :], cacc[0:32, :], AF.Exp)), wait=[(dv, dv.n)], inc=[sc.inc()])
        for i in range(TT):
            pf = mk.pb[5][:, (i % 2) * 128:(i % 2) * 128 + 32]
            mk.op("tensor", (lambda e, i=i, pf=pf: e.transpose(pf, cacc[0:32, i * 128:(i + 1) * 128], C["eye32"][0:32, 0:32])),
                  wait=[(sc, sc.n), (dv, dv.n)], inc=[mm.inc()])
            w5 = [(mm, mm.n)]
            if i == 0:
                w5.append((mk.sem("st_cT"), mk.sem("st_cT").n))
            mk.op("vector", (lambda e, i=i, pf=pf: e.tensor_copy(cstage[:, i * 32:(i + 1) * 32], pf)),
                  wait=w5, inc=[dv.inc()])
        mk.store(d["sc_cols"][:, eL * 512:(eL + 1) * 512], cstage[:, 0:512], q="sync", sem="st_cT",
                 serialize=True, wait=[(dv, dv.n)])
        def cF(e, eL=eL, C_=C_, nb=nb, gcv=gcv):
            mk.fen(e, e.tensor_copy(mcols[0:32, 0:1], GC[:, 0:1]))
            if nb > 1:
                mk.fen(e, e.tensor_copy(mcols[0:32, 1:nb], GC[:, C_ - 1:2048 - C_:C_]))
            mx = mcols[0:32, 0:nb].rearrange("p (b o) -> p b o", o=1).broadcast_to([32, nb, C_])
            return e.tensor_tensor(csc2[0:32, :].rearrange("p (b c) -> p b c", c=C_), gcv, mx, ALU.subtract)
        mk.op("vector", cF, wait=[(dv, dv.n), (sc, sc.n)], inc=[dv.inc()])
        mk.op("scalar", (lambda e: e.activation(csc2[0:32, :], csc2[0:32, :], AF.Exp)), wait=[(dv, dv.n)], inc=[sc.inc()])
        for i in range(TT):
            pf = mk.pb[5][:, (i % 2) * 128:(i % 2) * 128 + 32]
            mk.op("tensor", (lambda e, i=i, pf=pf: e.transpose(pf, csc2[0:32, i * 128:(i + 1) * 128], C["eye32"][0:32, 0:32])),
                  wait=[(sc, sc.n), (dv, dv.n)], inc=[mm.inc()])
            w6 = [(mm, mm.n)]
            if i == 0:
                w6.append((mk.sem("st_cT"), mk.sem("st_cT").n))
            mk.op("vector", (lambda e, i=i, pf=pf: e.tensor_copy(cstage[:, 512 + i * 32:512 + (i + 1) * 32], pf)),
                  wait=w6, inc=[dv.inc()])
        def cFl(e, eL=eL):
            lamap = lam_tm[:].rearrange("p (i c) -> p i c", c=512)[:, :, 32 + 8 + eL:32 + 8 + eL + 15 * 31 + 1:15]
            mk.fen(e, e.tensor_copy(cstage[:, 512:1024], cstage[:, 512:1024])) if False else None
            return e.tensor_tensor(cstage[:, 512:1024].rearrange("p (i h) -> p i h", h=32),
                                   cstage[:, 512:1024].rearrange("p (i h) -> p i h", h=32), lamap, ALU.mult)
        mk.op("vector", cFl, wait=[(dv, dv.n), (mk.sems["p_ev"], t_lam)], inc=[dv.inc()])
        mk.store(d["sc_cols"][:, 2048 + eL * 512:2048 + (eL + 1) * 512], cstage[:, 512:1024], q="sync",
                 sem="st_cT", serialize=True, wait=[(dv, dv.n)])
    t_cols = dv.n

    for o in (16, 17):
        slot, t_slot = rest_tile(o)
        def conv(e, o=o, slot=slot):
            wc = C["a_convc"]
            mk.fen(e, e.tensor_scalar(cacc[:], slot[:, 0:2048], wc[:, o * 4:o * 4 + 1], None, ALU.mult))
            mk.fen(e, e.scalar_tensor_tensor(cacc[:], slot[:, 1:2049], wc[:, o * 4 + 1:o * 4 + 2], cacc[:], ALU.mult, ALU.add))
            mk.fen(e, e.scalar_tensor_tensor(cacc[:], slot[:, 2:2050], wc[:, o * 4 + 2:o * 4 + 3], cacc[:], ALU.mult, ALU.add))
            return e.scalar_tensor_tensor(cacc[:], slot[:, 3:2051], wc[:, o * 4 + 3:o * 4 + 4], cacc[:], ALU.mult, ALU.add)
        mk.op("vector", conv, wait=[(dv, t_slot), (sc, sc.n), (mk.sem("st_cT"), mk.sem("st_cT").n)], inc=[dv.inc()])
        mk.op("scalar", (lambda e: e.activation(csc2[:], cacc[:], AF.Silu)), wait=[(dv, dv.n)], inc=[sc.inc()])
        mk.op("vector", (lambda e: e.tensor_copy(cb16[:], csc2[:])),
              wait=[(sc, sc.n), (mk.sem("st_bc"), mk.sem("st_bc").n)], inc=[dv.inc()])
        mk.store(d["sc_B" if o == 16 else "sc_C"], cb16[:], q="sync", sem="st_bc", serialize=True, wait=[(dv, dv.n)])

    for o in range(16):
        slot, t_slot = rest_tile(o)
        def conv(e, o=o, slot=slot):
            wc = C["a_convc"]
            mk.fen(e, e.tensor_scalar(cacc[:], slot[:, 0:2048], wc[:, o * 4:o * 4 + 1], None, ALU.mult))
            mk.fen(e, e.scalar_tensor_tensor(cacc[:], slot[:, 1:2049], wc[:, o * 4 + 1:o * 4 + 2], cacc[:], ALU.mult, ALU.add))
            mk.fen(e, e.scalar_tensor_tensor(cacc[:], slot[:, 2:2050], wc[:, o * 4 + 2:o * 4 + 3], cacc[:], ALU.mult, ALU.add))
            return e.scalar_tensor_tensor(cacc[:], slot[:, 3:2051], wc[:, o * 4 + 3:o * 4 + 4], cacc[:], ALU.mult, ALU.add)
        wv = [(dv, t_slot), (sc, sc.n)]
        if o == 0:
            wv.append((mk.sem("st_bc"), mk.sem("st_bc").n))
        mk.op("vector", conv, wait=wv, inc=[dv.inc()])
        mk.op("scalar", (lambda e: e.activation(csc2[:], cacc[:], AF.Silu)), wait=[(dv, dv.n), (mm, mm.n)], inc=[sc.inc()])
        t_silu = sc.n
        for i in range(TT):
            pf = mk.pb[6][:, (i % 4) * 128:(i % 4) * 128 + 128]
            def tpv(e, i=i, pf=pf):
                return e.transpose(pf, csc2[:, i * 128:(i + 1) * 128], C["eye32"][:])
            w2 = [(sc, t_silu), (dv, dv.n)]
            mk.op("tensor", tpv, wait=w2, inc=[mm.inc()])
            vsl = vst[:, (i % 2) * 128:(i % 2) * 128 + 128]
            def vev(e, o=o, i=i, pf=pf, vsl=vsl):
                mk.fen(e, e.tensor_scalar(vsl[:, 0:64], pf[:, 0:64], dtm[:, i * 32 + 2 * o:i * 32 + 2 * o + 1], None, ALU.mult))
                return e.tensor_scalar(vsl[:, 64:128], pf[:, 64:128], dtm[:, i * 32 + 2 * o + 1:i * 32 + 2 * o + 2], None, ALU.mult)
            w3 = [(mm, mm.n), (dv, t_cols)]
            if mk.sem("st_v").n > 0:
                w3.append((mk.sem("st_v"), mk.sem("st_v").n))
            mk.op("vector", vev, wait=w3, inc=[dv.inc()])
            mk.store(d["sc_v"][i * 128:(i + 1) * 128, o * 128:(o + 1) * 128], vsl, q="sync", sem="st_v",
                     serialize=True, wait=[(dv, dv.n)])
    mk.bar()


def l0a_core_phase(mk):
    nc = mk.nc
    mk.prog.phase = "l0ac"
    mk.reset()
    d = mk.dram
    C = mk.consts
    vtm = mk.c16(32768)
    qk = mk.c16(2 * 2048)
    lamsl = mk.c16(16 * 512)
    LT = mk.c16(4096)
    LF = mk.c16(4096)
    asc = mk.c16(16 * 128)
    bsb = mk.c16(2048)
    csb = mk.c16(2048)
    colsT = mk.c32(5632)
    ebc = mk.c32(8 * 512)
    e32 = mk.c32(4 * 128)
    yac = mk.c32(2048)
    ld = mk.sem("ld_sync"); ldg = mk.sem("ld_gpsimd")
    mm = mk.sem("p_mm"); dv = mk.sem("p_dv"); sc = mk.sem("p_sc"); ev = mk.sem("p_ev")
    mk.load(vtm[:], d["sc_v"].rearrange("(i p) c -> p i c", p=128), q="sync")
    mk.load(bsb[:], d["sc_B"], q="sync")
    mk.load(csb[:], d["sc_C"], q="sync")
    mk.load(colsT[:], d["sc_cols"][:, 0:5632], q="sync")
    t_ld0 = ld.n
    mk.load(lamsl[:], d["sc_lam"][:, 0:16 * 512], q="gpsimd", sem="ld_lam")
    t_lam0 = mk.sem("ld_lam").n

    GCL = 4096; DDL = 4608
    slot_mm = getattr(mk, "_l0ac_slot_mm", None) or [0] * 16
    bank_free = getattr(mk, "_l0ac_bank_free", None) or [0] * 5
    mk._l0ac_slot_mm = slot_mm
    mk._l0ac_bank_free = bank_free
    qk_free = getattr(mk, "_l0ac_qk_free", None) or [0, 0]
    mk._l0ac_qk_free = qk_free
    gb_free = getattr(mk, "_l0ac_gb_free", None) or [0] * 8
    mk._l0ac_gb_free = gb_free
    t_qk_tab = getattr(mk, "_l0ac_t_qk", None) or [0, 0]
    mk._l0ac_t_qk = t_qk_tab
    esc_free = [0, 0, 0, 0]
    gpe = mk.sem("gp_eb")
    t_ltp_prev = [getattr(mk, "_l0ac_t_ltp", 0)]
    sl = getattr(mk, "_l0ac_sl", 0)

    def emit_qk_season(si):
        par = si % 2
        for j in range(si + 1):
            pbk = mk.pb[4 + j % 2]
            def qmm(e, si=si, j=j, pbk=pbk):
                return e.matmul(pbk[:, 0:128], bsb[:, j * 128:(j + 1) * 128], csb[:, si * 128:(si + 1) * 128],
                                start=True, stop=True)
            w = [(ld, t_ld0), (sc, max(0, sc.n - 1))]
            mk.op("tensor", qmm, wait=w, inc=[mm.inc()])
            def qev(e, par=par, j=j, pbk=pbk):
                return e.activation(qk[:, par * 2048 + j * 128:par * 2048 + (j + 1) * 128], pbk[:, 0:128], AF.Copy)
            mk.op("scalar", qev, wait=[(mm, mm.n), (dv, qk_free[par])], inc=[sc.inc()])
        t_qk_tab[par] = sc.n

    emit_qk_season(0)
    for i in range(TT):
        qkb = qk[:, (i % 2) * 2048:(i % 2) * 2048 + 2048]
        t_gb_tab = []
        for g in range(8):
            hb_g = ebc[:, g * 512:(g + 1) * 512]
            gsrc = d["sc_gc"][4 * g:4 * g + 4, i * 128:(i + 1) * 128].rearrange("(o g) t -> o g t", o=1).broadcast_to([128, 4, 128])
            mk.load(hb_g.rearrange("p (g t) -> p g t", g=4), gsrc, q="gpsimd", sem="ld_gb", serialize=True,
                    wait=[(gpe, gb_free[g])])
            t_gb_tab.append(mk.sem("ld_gb").n)
        lsl = lamsl[:, i * 512:(i + 1) * 512]
        def mk_lbuild(h0, h1):
            def lbuild(e, i=i, lsl=lsl, h0=h0, h1=h1):
                fen = C["fen_tm"]
                nh = h1 - h0
                dst = LT[:, h0 * 128:h1 * 128].rearrange("p (h s) -> p h s", s=128)
                tmp = LF[:, h0 * 128:h1 * 128].rearrange("p (h s) -> p h s", s=128)
                last = None
                for eL in range(8):
                    mblk = fen[:, eL * 128:(eL + 1) * 128].rearrange("p (o s) -> p o s", o=1).broadcast_to([128, nh, 128])
                    lamb = lsl[:, 32 + eL + 15 * h0:32 + eL + 15 * (h1 - 1) + 1:15].rearrange("p (h o) -> p h o", o=1).broadcast_to([128, nh, 128])
                    if eL == 0:
                        last = e.tensor_tensor(dst, mblk, lamb, ALU.mult)
                    else:
                        mk.fen(e, last)
                        t1 = e.tensor_tensor(tmp, mblk, lamb, ALU.mult)
                        mk.fen(e, t1)
                        last = e.tensor_tensor(dst, dst, tmp, ALU.add)
                return last
            return lbuild
        mk.op("vector", mk_lbuild(0, 18), wait=[(mk.sem("ld_lam"), t_lam0), (mm, t_ltp_prev[0])], inc=[dv.inc()])
        gpb = mk.sem("gp_lb")
        mk.op("gpsimd", mk_lbuild(18, 32), wait=[(mk.sem("ld_lam"), t_lam0), (mm, t_ltp_prev[0]),
                                                 (dv, qk_free[(i + 1) % 2])], inc=[gpb.inc()])
        lf_q = []
        for wv2 in range(4):
            def ltp(e, wv2=wv2):
                last = None
                for h2 in range(8):
                    h = wv2 * 8 + h2
                    last = e.transpose(mk.ptp[:, h2 * 128:(h2 + 1) * 128], LT[:, h * 128:(h + 1) * 128], C["eye16"][:])
                return last
            mk.op("tensor", ltp, wait=[(dv, dv.n), (gpb, gpb.n)] + ([(ev, ev.n)] if (i > 0 or wv2 > 0) else []), inc=[mm.inc()])
            def lev(e, wv2=wv2):
                last = None
                for h2 in range(8):
                    h = wv2 * 8 + h2
                    last = e.activation(LF[:, h * 128:(h + 1) * 128], mk.ptp[:, h2 * 128:(h2 + 1) * 128], AF.Copy)
                return last
            mk.op("scalar", lev, wait=[(mm, mm.n)], inc=[ev.inc()])
            lf_q.append(ev.n)
        t_ltp_prev[0] = mm.n
        groups = [eL for eL in range(4) if (i >> eL) & 1]
        pending = None
        first_ab = True
        for h in range(A_H):
            g = h // 4
            hb = ebc[:, g * 512:(g + 1) * 512]
            t_gb = t_gb_tab[g]
            built = []
            for gi, eL in enumerate(groups):
                j0 = (i >> (eL + 1)) << (eL + 1)
                njt = 1 << eL
                if (sl % 16) + njt > 16:
                    sl += 16 - (sl % 16)
                s0 = sl % 16
                sl += njt
                js = []
                for q2 in range(njt):
                    j = j0 + q2
                    st_ = (s0 + q2) % 16
                    a_sl = asc[:, st_ * 128:(st_ + 1) * 128]
                    def abuild(e, j=j, eL=eL, a_sl=a_sl, h=h, qkb=qkb):
                        cc = colsT[:, eL * 512 + j * 32 + h:eL * 512 + j * 32 + h + 1]
                        return e.tensor_scalar(a_sl, qkb[:, j * 128:(j + 1) * 128], cc, None, ALU.mult)
                    w = [(mm, slot_mm[st_])]
                    if first_ab:
                        w += [(sc, t_qk_tab[i % 2]), (ld, t_ld0)]
                        first_ab = False
                    mk.op("vector", abuild, wait=w, inc=[dv.inc()])
                    js.append((j, a_sl, st_, dv.n))
                built.append((gi, eL, js))
            s_ = sl % 16
            sl += 1
            a_d = asc[:, s_ * 128:(s_ + 1) * 128]
            def ebuild(e, h=h, i=i, hb=hb):
                gcc = colsT[:, GCL + i * 32 + h:GCL + i * 32 + h + 1].broadcast_to([128, 128])
                dst = e32[:, (h % 4) * 128:(h % 4) * 128 + 128]
                mk.fen(e, e.tensor_tensor(dst, hb[:, (h % 4) * 128:(h % 4) * 128 + 128], C["causbias_fm"][:], ALU.add))
                return e.tensor_tensor(dst, dst, gcc, ALU.subtract)
            wb = [(mk.sem("ld_gb"), t_gb), (sc, esc_free[h % 4])]
            if i == 0 and h == 0:
                wb.append((ld, t_ld0))
            mk.op("gpsimd", ebuild, wait=wb, inc=[gpe.inc()])
            if h % 4 == 3:
                gb_free[g] = gpe.n
            def eexp(e, h=h, s_=s_, a_d=a_d):
                return e.activation(a_d, e32[:, (h % 4) * 128:(h % 4) * 128 + 128], AF.Exp)
            mk.op("scalar", eexp, wait=[(gpe, gpe.n), (mm, slot_mm[s_])], inc=[sc.inc()])
            esc_free[h % 4] = sc.n
            def dmul(e, h=h, i=i, a_d=a_d, qkb=qkb):
                mk.fen(e, e.tensor_tensor(a_d, a_d, qkb[:, i * 128:(i + 1) * 128], ALU.mult))
                return e.tensor_tensor(a_d, a_d, LF[:, h * 128:(h + 1) * 128], ALU.mult)
            mk.op("vector", dmul, wait=[(sc, sc.n), (ev, lf_q[h // 8])], inc=[dv.inc()])
            t_dmul = dv.n
            if pending is not None:
                ph, pbuilt, pdiag = pending
                for (gi, eL, js) in pbuilt:
                    pb_ = mk.pb[gi][:, 0:64]
                    def comb(e, h=ph, i=i, eL=eL, pb_=pb_, first=(gi == 0)):
                        s1 = colsT[:, 2048 + eL * 512 + i * 32 + h:2048 + eL * 512 + i * 32 + h + 1]
                        dst = yac[:, h * 64:(h + 1) * 64]
                        if first:
                            return e.tensor_scalar(dst, pb_, s1, None, ALU.mult)
                        return e.scalar_tensor_tensor(dst, pb_, s1, dst, ALU.mult, ALU.add)
                    wcb = [(mm, pdiag[2]), (dv, dv.n)]
                    if gi == 0:
                        wcb.append((mk.sem("st_y"), mk.sem("st_y").n))
                    mk.op("vector", comb, wait=wcb, inc=[dv.inc()])
                    bank_free[gi] = dv.n
                nbd = pdiag[0]
                pbd = mk.pb[nbd][:, 0:64]
                def yfin(e, h=ph, i=i, pbd=pbd, first=(pdiag[1] == 0)):
                    dst = yac[:, h * 64:(h + 1) * 64]
                    if first:
                        mk.fen(e, e.tensor_copy(dst, pbd))
                    else:
                        mk.fen(e, e.tensor_tensor(dst, dst, pbd, ALU.add))
                    ddt = colsT[:, DDL + i * 32 + h:DDL + i * 32 + h + 1]
                    return e.scalar_tensor_tensor(dst, vtm[:, i * 2048 + h * 64:i * 2048 + h * 64 + 64], ddt, dst, ALU.mult, ALU.add)
                wyf = [(mm, pdiag[2]), (dv, dv.n)]
                if pdiag[1] == 0:
                    wyf.append((mk.sem("st_y"), mk.sem("st_y").n))
                mk.op("vector", yfin, wait=wyf, inc=[dv.inc()])
                bank_free[nbd] = dv.n
            for (gi, eL, js) in built:
                pb_ = mk.pb[gi][:, 0:64]
                for k2, (j, a_s, s_2, t_ab) in enumerate(js):
                    def pmm(e, h=h, j=j, a_s=a_s, pb_=pb_, st=(k2 == 0), sp=(k2 == len(js) - 1)):
                        return e.matmul(pb_, a_s, vtm[:, j * 2048 + h * 64:j * 2048 + h * 64 + 64],
                                        start=st, stop=sp)
                    w = [(dv, t_ab)]
                    if k2 == 0:
                        w.append((dv, bank_free[gi]))
                    mk.op("tensor", pmm, wait=w, inc=[mm.inc()])
                    slot_mm[s_2] = mm.n
            nbd = len(built)
            pbd = mk.pb[nbd][:, 0:64]
            def dpmm2(e, h=h, i=i, a_d=a_d, pbd=pbd):
                return e.matmul(pbd, a_d, vtm[:, i * 2048 + h * 64:i * 2048 + h * 64 + 64], start=True, stop=True)
            mk.op("tensor", dpmm2, wait=[(dv, t_dmul), (dv, bank_free[nbd])], inc=[mm.inc()])
            slot_mm[s_] = mm.n
            pending = (h, built, (nbd, len(built), mm.n))
            if h == 16 and i + 1 < TT:
                emit_qk_season(i + 1)
        qk_free[i % 2] = dv.n
        ph, pbuilt, pdiag = pending
        for (gi, eL, js) in pbuilt:
            pb_ = mk.pb[gi][:, 0:64]
            def comb(e, h=ph, i=i, eL=eL, pb_=pb_, first=(gi == 0)):
                s1 = colsT[:, 2048 + eL * 512 + i * 32 + h:2048 + eL * 512 + i * 32 + h + 1]
                dst = yac[:, h * 64:(h + 1) * 64]
                if first:
                    return e.tensor_scalar(dst, pb_, s1, None, ALU.mult)
                return e.scalar_tensor_tensor(dst, pb_, s1, dst, ALU.mult, ALU.add)
            wcb = [(mm, pdiag[2]), (dv, dv.n)]
            if gi == 0:
                wcb.append((mk.sem("st_y"), mk.sem("st_y").n))
            mk.op("vector", comb, wait=wcb, inc=[dv.inc()])
            bank_free[gi] = dv.n
        nbd = pdiag[0]
        pbd = mk.pb[nbd][:, 0:64]
        def yfin(e, h=ph, i=i, pbd=pbd, first=(pdiag[1] == 0)):
            dst = yac[:, h * 64:(h + 1) * 64]
            if first:
                mk.fen(e, e.tensor_copy(dst, pbd))
            else:
                mk.fen(e, e.tensor_tensor(dst, dst, pbd, ALU.add))
            ddt = colsT[:, DDL + i * 32 + h:DDL + i * 32 + h + 1]
            return e.scalar_tensor_tensor(dst, vtm[:, i * 2048 + h * 64:i * 2048 + h * 64 + 64], ddt, dst, ALU.mult, ALU.add)
        wyf = [(mm, pdiag[2]), (dv, dv.n)]
        if pdiag[1] == 0:
            wyf.append((mk.sem("st_y"), mk.sem("st_y").n))
        mk.op("vector", yfin, wait=wyf, inc=[dv.inc()])
        bank_free[nbd] = dv.n
        mk._l0ac_sl = sl
        t_y = dv.n
        mk.store(d["sc_y"][i * 128:(i + 1) * 128, :], yac[:], q="sync", sem="st_y", serialize=True, wait=[(dv, t_y)])
    mk._l0ac_t_ltp = t_ltp_prev[0]
    mk.bar()


def l0a_epi_phase(mk, xnfm_d, x0_d, xa_d):
    nc = mk.nc
    mk.prog.phase = "l0ae"
    mk.reset()
    d = mk.dram
    C = mk.consts
    xnl = mk.c16(16384)
    wz = mk.c16(16384)
    wo = mk.c16(16384)
    nwb = mk.c16(2048)
    ytb = mk.c16(2048)
    yT = mk.c16(2048)
    yac = mk.c32(2 * 2048)
    zsb = mk.c32(2048)
    cols = mk.c32(8)
    osb = mk.c32(2 * D)
    x0t = mk.c32(2 * D)
    ld = mk.sem("ld_sync"); ldg = mk.sem("ld_gpsimd")
    mm = mk.sem("p_mm"); dv = mk.sem("p_dv"); sc = mk.sem("p_sc"); ev = mk.sem("p_ev")

    mk.load(xnl[:], xnfm_d.rearrange("(k p) t -> p k t", p=128), q="sync")
    mk.load(wz[:], d["a_wz"].rearrange("(k p) o -> p k o", p=128), q="gpsimd", sem="ld_wz")
    mk.load(wo[:], d["a_wo"].rearrange("(k p) o -> p k o", p=128), q="gpsimd", sem="ld_wz")
    t_ld = (ld.n, mk.sem("ld_wz").n)
    def nwo(e):
        e.matmul(mk.pb[6][:, 0:512], C["ones_row"][0:1, 0:128], C["a_nw"][0:1, 0:512], start=True, stop=True)
        e.matmul(mk.pb[5][:, 0:512], C["ones_row"][0:1, 0:128], C["a_nw"][0:1, 512:1024], start=True, stop=True)
        e.matmul(mk.pb[4][:, 0:512], C["ones_row"][0:1, 0:128], C["a_nw"][0:1, 1024:1536], start=True, stop=True)
        return e.matmul(mk.pb[3][:, 0:512], C["ones_row"][0:1, 0:128], C["a_nw"][0:1, 1536:2048], start=True, stop=True)
    mk.op("tensor", nwo, inc=[mm.inc()])
    def nwe(e):
        e.tensor_copy(nwb[:, 0:512], mk.pb[6][:, 0:512])
        e.tensor_copy(nwb[:, 512:1024], mk.pb[5][:, 0:512])
        e.tensor_copy(nwb[:, 1024:1536], mk.pb[4][:, 0:512])
        return e.tensor_copy(nwb[:, 1536:2048], mk.pb[3][:, 0:512])
    mk.op("vector", nwe, wait=[(mm, mm.n)], inc=[dv.inc()])
    t_nw = dv.n

    for i in range(TT):
        ysl = yac[:, (i % 2) * 2048:(i % 2) * 2048 + 2048]
        mk.load(ysl, d["sc_y"][i * 128:(i + 1) * 128, :], q="sync", sem="ld_y", serialize=True,
                wait=[(dv, dv.n), (sc, sc.n)])
        t_yld = mk.sem("ld_y").n
        for zc in range(4):
            pbk = mk.pb[zc % 2]
            def zmm(e, i=i, zc=zc, pbk=pbk):
                last = e.matmul(pbk[:], C["ones_row"][0:1, 0:128], C["a_bz"][0:1, zc * 512:(zc + 1) * 512],
                                start=True, stop=False)
                for k in range(8):
                    last = e.matmul(pbk[:], xnl[:, k * 2048 + i * 128:k * 2048 + (i + 1) * 128],
                                    wz[:, k * 2048 + zc * 512:k * 2048 + (zc + 1) * 512],
                                    start=False, stop=(k == 7))
                return last
            w = [(ld, t_ld[0]), (mk.sem("ld_wz"), t_ld[1])]
            if i > 0 or zc >= 2:
                w.append((sc, sc.n))
            mk.op("tensor", zmm, wait=w, inc=[mm.inc()])
            def sil(e, zc=zc, pbk=pbk):
                return e.activation(zsb[:, zc * 512:(zc + 1) * 512], pbk[:], AF.Silu)
            mk.op("scalar", sil, wait=[(mm, mm.n), (sc, sc.n)], inc=[sc.inc()])
        def yz2(e, ysl=ysl):
            return e.tensor_tensor(ysl, ysl, zsb[:], ALU.mult)
        mk.op("vector", yz2, wait=[(sc, sc.n), (mk.sem("ld_y"), t_yld)], inc=[dv.inc()])
        def sqr(e, ysl=ysl):
            return e.activation(zsb[:], ysl, AF.Square, accum_out=cols[:, 0:1])
        mk.op("scalar", sqr, wait=[(dv, dv.n)], inc=[sc.inc()])
        def rstd(e, ysl=ysl):
            mk.fen(e, e.activation(cols[:, 1:2], cols[:, 0:1], AF.Ln, bias=mk.fc(1e-5), scale=1.0 / A_DIM))
            return e.activation(cols[:, 2:3], cols[:, 1:2], AF.Exp, scale=-0.5)
        mk.op("scalar", rstd, wait=[(sc, sc.n)], inc=[sc.inc()])
        def ynorm(e, ysl=ysl):
            mk.fen(e, e.tensor_scalar(ysl, ysl, cols[:, 2:3], None, ALU.mult))
            return e.tensor_tensor(ytb[:], ysl, nwb[:], ALU.mult)
        mk.op("vector", ynorm, wait=[(sc, sc.n), (dv, t_nw)], inc=[dv.inc()])
        for wv2 in range(2):
            def ytp(e, wv2=wv2):
                last = None
                for k2 in range(8):
                    k = wv2 * 8 + k2
                    last = e.transpose(mk.ptp[:, k2 * 128:(k2 + 1) * 128], ytb[:, k * 128:(k + 1) * 128], C["eye16"][:])
                return last
            mk.op("tensor", ytp, wait=[(dv, dv.n)] + ([(ev, ev.n)] if (i > 0 or wv2 > 0) else []), inc=[mm.inc()])
            def yte(e, wv2=wv2):
                last = None
                for k2 in range(8):
                    k = wv2 * 8 + k2
                    last = e.tensor_copy(yT[:, k * 128:(k + 1) * 128], mk.ptp[:, k2 * 128:(k2 + 1) * 128])
                return last
            mk.op("vector", yte, wait=[(mm, mm.n)], inc=[ev.inc()])
        t_yT = mk.sems["p_ev"].n
        x0s = x0t[:, (i % 2) * D:(i % 2) * D + D]
        mk.load(x0s, x0_d[i * 128:(i + 1) * 128, :], q="sync", sem="ld_x0", serialize=True)
        for dc in range(2):
            pbk = mk.pb[2 + dc]
            def omm(e, i=i, dc=dc, pbk=pbk):
                last = e.matmul(pbk[:], C["ones_row"][0:1, 0:128], C["a_bo"][0:1, dc * 512:(dc + 1) * 512], start=True, stop=False)
                for k in range(16):
                    last = e.matmul(pbk[:], yT[:, k * 128:(k + 1) * 128], wo[:, k * 1024 + dc * 512:k * 1024 + (dc + 1) * 512],
                                    start=False, stop=(k == 15))
                return last
            mk.op("tensor", omm, wait=[(mk.sems["p_ev"], t_yT)] + ([(dv, dv.n)] if i > 0 else []), inc=[mm.inc()])
            os_ = osb[:, (i % 2) * D + dc * 512:(i % 2) * D + (dc + 1) * 512]
            def radd(e, os_=os_, x0s=x0s, dc=dc, pbk=pbk):
                return e.scalar_tensor_tensor(os_, pbk[:], 1.0, x0s[:, dc * 512:(dc + 1) * 512], ALU.mult, ALU.add)
            mk.op("vector", radd, wait=[(mm, mm.n), (mk.sem("ld_x0"), mk.sem("ld_x0").n)], inc=[dv.inc()])
        mk.store(xa_d[i * 128:(i + 1) * 128, :], osb[:, (i % 2) * D:(i % 2) * D + D], q="sync", sem="st_xa",
                 serialize=True, wait=[(dv, dv.n)])
    mk.bar()


def emit_batch(mk, b):
    d = mk.dram
    x0 = d["x_in"][b]
    out = d["out"][b]
    ln_phase(mk, x0, d["sc_xn32"], d["sc_xnfm"])
    l0a_prep_phase(mk, d["sc_xnfm"])
    l0a_core_phase(mk)
    l0a_epi_phase(mk, d["sc_xnfm"], x0, d["sc_xa"])
    l0f_attn_phase(mk, 0, d["sc_xn32"], d["sc_fx"])
    l0f_mlp_phase(mk, 0, d["sc_fx"], d["sc_fn32"])
    l0f_attn_phase(mk, 1, d["sc_fn32"], d["sc_fx"])
    l0f_mlp_phase(mk, 1, d["sc_fx"], d["sc_xb"], xa_in_d=d["sc_xa"])
    ln_phase(mk, d["sc_xb"], d["sc_xn132"], d["sc_xn1fm"])
    l1_attn_phase(mk, d["sc_xn1fm"], d["sc_xb"], out)
    moe_phase(mk, d["sc_xn1fm"], d["sc_xn132"], out)


def build_full(debug=False, batches=(0, 1)):
    mk = MK(debug=debug)
    setup(mk)
    for b in batches:
        emit_batch(mk, b)
    mk.prog.emit()
    mk.es.close()
    return mk


_MK_CACHE = {}

def kernel(**inputs):
    prep = host_prep(inputs)
    x = np.asarray(inputs["L0_attn_hidden_states"], dtype=np.float32)
    key = "full"
    if key not in _MK_CACHE:
        _MK_CACHE[key] = build_full()
    mk = _MK_CACHE[key]
    in_maps = []
    for c in range(NCORES):
        m = {k: v for k, v in prep.items()}
        m["x_in"] = np.ascontiguousarray(x[2 * c:2 * c + 2])
        in_maps.append(m)
    res = run_bass_kernel_spmd(mk.nc, in_maps, core_ids=list(range(NCORES)))
    out = np.concatenate([r["out"] for r in res.results], axis=0)
    return out.astype(np.float32)


# version 12 (final)
# speedup vs baseline: 1.0304x; 1.0304x over previous
import numpy as np
import ml_dtypes
from contextlib import ExitStack
import concourse.bass as bass
import concourse.mybir as mybir
from concourse.bass_utils import run_bass_kernel_spmd

F32 = mybir.dt.float32
BF16 = mybir.dt.bfloat16
AF = mybir.ActivationFunctionType
ALU = mybir.AluOpType
AX = mybir.AxisListType.X

T = 2048
D = 1024
NB = 2
NCORES = 8
NEG = -1.0e30

A_H = 32; A_HD = 64; A_DIM = 2048; A_DS = 128; A_K = 4; A_NL = 15
A_CONV = A_DIM + 2 * A_DS
A_PROJ = A_DIM + A_CONV + A_H * (A_NL + 1)
A_REST = A_CONV + A_H * (A_NL + 1)
F_HEADS = 8; F_HD = 64; F_WIN = 512; F_INNER = 512; F_HID = 2730; F_HIDP = 2816
F_QKSCALE = 8.0
G_NH = 16; G_NKV = 4; G_HD = 64
M_E = 8; M_HID = 2048

TT = T // 128


def _f32c(a):
    return np.ascontiguousarray(a, dtype=np.float32)


def host_prep(inp):
    d = {}
    w = inp["L0_attn_in_proj_w"]
    b = inp["L0_attn_in_proj_b"]
    d["a_wz"] = _f32c(np.asarray(w)[:A_DIM].T)
    d["a_wr"] = _f32c(np.asarray(w)[A_DIM:].T)
    d["a_bz"] = _f32c(np.asarray(b)[:A_DIM][None, :])
    d["a_br"] = _f32c(np.asarray(b)[A_DIM:][None, :])
    d["a_brc"] = _f32c(np.asarray(b)[A_DIM:].reshape(A_REST // 128, 128).T)
    d["a_conv"] = _f32c(np.asarray(inp["L0_attn_conv_w"]).reshape(18, 128, 4).transpose(1, 0, 2).reshape(128, 72))
    d["a_dtb"] = _f32c(np.asarray(inp["L0_attn_dt_bias"])[:, None])
    d["a_alog"] = _f32c(np.asarray(inp["L0_attn_A_log"])[:, None])
    Lp = np.zeros(512, np.float32); Lp[32:] = np.asarray(inp["L0_attn_L"]).reshape(-1)
    d["a_L"] = _f32c(Lp.reshape(4, 128).T)
    d["a_D"] = _f32c(np.asarray(inp["L0_attn_D"])[:, None])
    d["a_nw"] = _f32c(np.asarray(inp["L0_attn_norm_w"])[None, :])
    d["a_wo"] = _f32c(np.asarray(inp["L0_attn_out_proj_w"]).T)
    d["a_bo"] = _f32c(np.asarray(inp["L0_attn_out_proj_b"])[None, :])
    d["f_wqkv"] = _f32c(inp["L0_ffn_wqkv"])
    d["f_qs"] = _f32c(inp["L0_ffn_q_scale"])
    d["f_ks"] = _f32c(inp["L0_ffn_k_scale"])
    d["f_wo"] = _f32c(inp["L0_ffn_wo"])
    d["f_lng"] = _f32c(np.asarray(inp["L0_ffn_ln_g"])[:, None, :])
    d["f_lnb"] = _f32c(np.asarray(inp["L0_ffn_ln_b"])[:, None, :])
    w1 = np.asarray(inp["L0_ffn_w1"]); b1 = np.asarray(inp["L0_ffn_b1"])
    w1a = np.zeros((2, D, F_HIDP), np.float32); w1a[:, :, :F_HID] = w1[:, :, :F_HID]
    w1g = np.zeros((2, D, F_HIDP), np.float32); w1g[:, :, :F_HID] = w1[:, :, F_HID:]
    b1a = np.zeros((2, F_HIDP), np.float32); b1a[:, :F_HID] = b1[:, :F_HID]
    b1g = np.zeros((2, F_HIDP), np.float32); b1g[:, :F_HID] = b1[:, F_HID:]
    d["f_w1a"] = _f32c(w1a); d["f_w1g"] = _f32c(w1g)
    d["f_b1a"] = _f32c(np.stack([b1a[l].reshape(F_HIDP // 128, 128).T for l in range(2)]))
    d["f_b1g"] = _f32c(np.stack([b1g[l].reshape(F_HIDP // 128, 128).T for l in range(2)]))
    w2 = np.asarray(inp["L0_ffn_w2"])
    w2p = np.zeros((2, F_HIDP, D), np.float32); w2p[:, :F_HID] = w2
    d["f_w2"] = _f32c(w2p)
    d["f_b2"] = _f32c(np.asarray(inp["L0_ffn_b2"])[:, None, :])
    wqT = np.asarray(inp["L1_attn_wq"]).T.reshape(D, 16, 64)
    perm = [0] * 16
    for h in range(16):
        tile = (h % 4) + 4 * (h // 8)
        slot = tile * 2 + ((h // 4) % 2)
        perm[slot] = h
    d["g_wq"] = _f32c(wqT[:, perm, :].reshape(D, D))
    d["g_wk"] = _f32c(np.asarray(inp["L1_attn_wk"]).T)
    d["g_wv"] = _f32c(np.asarray(inp["L1_attn_wv"]).T)
    d["g_wo"] = _f32c(np.asarray(inp["L1_attn_wo"]).T)
    d["m_wr"] = _f32c(np.asarray(inp["L1_ffn_router_w"]).T)
    d["m_wg"] = _f32c(np.asarray(inp["L1_ffn_w_gate"]).transpose(0, 2, 1))
    d["m_wu"] = _f32c(np.asarray(inp["L1_ffn_w_up"]).transpose(0, 2, 1))
    d["m_wd"] = _f32c(np.asarray(inp["L1_ffn_w_down"]).transpose(0, 2, 1))
    tbl = static_tables()
    d.update(tbl)
    return d


_TBL_CACHE = None

def static_tables():
    global _TBL_CACHE
    if _TBL_CACHE is not None:
        return _TBL_CACHE
    d = {}
    n = T; hd = F_HD
    half = np.arange(0, hd, 2, dtype=np.float64)
    inv_freq = 1.0 / 10000.0 ** (half / hd)
    t = np.arange(n, dtype=np.float64)
    freqs = t[:, None] * inv_freq[None, :]
    freqs = np.concatenate([freqs, freqs], axis=-1)
    base = (half + 0.4 * hd) / (1.4 * hd)
    power = (t - n // 2) / float(F_WIN // 2)
    scale = base[None, :] ** power[:, None]
    scale = np.concatenate([scale, scale], axis=-1)
    cos = np.cos(freqs); sin = np.sin(freqs)
    sgn = np.ones(hd); sgn[: hd // 2] = -1.0
    d["t_cq"] = _f32c(cos * scale)
    d["t_sq"] = _f32c(sin * scale * sgn[None, :])
    d["t_ck"] = _f32c(cos / scale)
    d["t_sk"] = _f32c(sin / scale * sgn[None, :])
    i = np.arange(128)[:, None]; j = np.arange(128)[None, :]
    d["t_causal01"] = _f32c(j <= i)
    d["t_causbias"] = _f32c(np.where(j <= i, 0.0, NEG))
    d["t_causbias_fm"] = _f32c(np.where(j >= i, 0.0, NEG))
    d["t_caus01_fm"] = _f32c(j >= i)
    d["t_anti01_fm"] = _f32c(j <= i)
    lvl = np.zeros((128, 128), np.int64)
    tt_ = np.arange(128)
    x_ = tt_[:, None] ^ tt_[None, :]
    with np.errstate(divide="ignore"):
        lv = 1 + np.floor(np.log2(np.maximum(x_, 1))).astype(np.int64)
    lv[x_ == 0] = 0
    fen = np.zeros((8, 128, 128), np.float32)
    for e in range(8):
        fen[e] = ((lv == e) & (tt_[None, :] >= tt_[:, None])).astype(np.float32)
    d["t_fen_fm"] = _f32c(fen.transpose(1, 0, 2).reshape(128, 8 * 128))
    fen_tm = np.zeros((8, 128, 128), np.float32)
    for e in range(8):
        fen_tm[e] = ((lv == e) & (tt_[None, :] <= tt_[:, None])).astype(np.float32)
    d["t_fen_tm"] = _f32c(fen_tm.transpose(1, 0, 2).reshape(128, 8 * 128))
    d["t_eye"] = _f32c(np.eye(128))
    _TBL_CACHE = d
    return d


class Sem:
    def __init__(self, nc, es, name):
        self.h = es.enter_context(nc.semaphore(name))
        self.n = 0
    def inc(self, k=1):
        self.n += k
        return (self, k)
    def at(self):
        return (self, self.n)


class Prog:
    ENG = ("sync", "tensor", "vector", "scalar", "gpsimd")
    def __init__(self, nc):
        self.nc = nc
        self.ops = {e: [] for e in self.ENG}
        self.phase = "init"
    def op(self, eng, fn, wait=(), inc=()):
        assert eng in self.ENG
        self.ops[eng].append((tuple(wait), fn, tuple(inc), self.phase))
    def emit(self):
        nc = self.nc
        from contextlib import ExitStack as _ES
        with nc.Block() as blk:
            for eng in self.ENG:
                ops = self.ops[eng]
                def player(e, ops=ops):
                    cur = [None]
                    es = _ES()
                    for waits, fn, incs, phase in ops:
                        if phase != cur[0]:
                            es.close()
                            es = _ES()
                            es.enter_context(nc.named_scope(phase))
                            cur[0] = phase
                        for (s, v) in waits:
                            e.wait_ge(s.h, v)
                        inst = fn(e)
                        if incs:
                            assert inst is not None, "op with incs returned no instruction"
                            for (s, k) in incs:
                                inst = inst.then_inc(s.h, k)
                    es.close()
                getattr(blk, eng)(player)


class MK:
    def __init__(self, fused_phases=(), debug=False):
        self.phases = fused_phases
        self.debug = debug
        self.es = ExitStack()
        self.nc = bass.Bass("TRN2", target_bir_lowering=False)
        self.prog = Prog(self.nc)
        self.dram = {}
        self._arena_ptr32 = 0
        self._arena_ptr16 = 0
        self.sems = {}
        self._bar_n = 0

    def dram_in(self, name, shape, dtype=F32):
        self.dram[name] = self.nc.dram_tensor(name, list(shape), dtype, kind="ExternalInput").ap()
        return self.dram[name]

    def dram_scratch(self, name, shape, dtype=F32):
        kind = "ExternalOutput" if self.debug else "Internal"
        self.dram[name] = self.nc.dram_tensor(name, list(shape), dtype, kind=kind).ap()
        return self.dram[name]

    def dram_out(self, name, shape):
        self.dram[name] = self.nc.dram_tensor(name, list(shape), F32, kind="ExternalOutput").ap()
        return self.dram[name]

    def alloc_arenas(self, w32, w16):
        self.A32 = self.es.enter_context(self.nc.sbuf_tensor("A32", [128, w32], F32))
        self.A16 = self.es.enter_context(self.nc.sbuf_tensor("A16", [128, w16], BF16))
        self.W32max, self.W16max = w32, w16

    def reset(self):
        self._arena_ptr32 = 0
        self._arena_ptr16 = 0

    def c32(self, w):
        o = self._arena_ptr32
        assert o + w <= self.W32max, f"A32 overflow {o}+{w}"
        self._arena_ptr32 = o + w
        return self.A32[:, o:o + w]

    def c16(self, w):
        o = self._arena_ptr16
        assert o + w <= self.W16max, f"A16 overflow {o}+{w}"
        self._arena_ptr16 = o + w
        return self.A16[:, o:o + w]

    def fen(self, e, inst):
        import concourse.mybir as _mb
        nm = "fence_" + str(e.engine)
        s = self.sem(nm)
        s.n += 1
        inst.then_inc(s.h, 1)
        e.wait_ge(s.h, s.n)

    def fc(self, v, rows=128):
        m = {1e-5: 0, 1.0: 1}
        return self.consts["fc"][0:rows, m[v]:m[v] + 1]

    def sem(self, name):
        if name not in self.sems:
            self.sems[name] = Sem(self.nc, self.es, name)
        return self.sems[name]

    def op(self, eng, fn, wait=(), inc=()):
        self.prog.op(eng, fn, wait, inc)

    def bar(self, extra=()):
        w = [s.at() for nm, s in self.sems.items() if nm.startswith(("ld_", "st_"))] + list(extra)
        bs = self.sem("bar")
        for e in Prog.ENG:
            tok = bs.inc(1)
            self.op(e, (lambda en: en.nop()), wait=w, inc=[tok])
        tgt = bs.n
        for e in Prog.ENG:
            self.op(e, (lambda en: None), wait=[(bs, tgt)])

    def load(self, dst, src, q="sync", sem=None, serialize=False, wait=()):
        s = self.sem(sem if sem else "ld_" + q)
        w = list(wait) + ([(s, s.n)] if serialize else [])
        tok = s.inc(16)
        self.op(q, (lambda e, d=dst, i=src: e.dma_start(out=d, in_=i)), wait=w, inc=[tok])
        return s

    def store(self, dst, src, q="sync", sem=None, serialize=False, wait=()):
        s = self.sem(sem if sem else "st_" + q)
        w = list(wait) + ([(s, s.n)] if serialize else [])
        tok = s.inc(16)
        self.op(q, (lambda e, d=dst, i=src: e.dma_start(out=d, in_=i)), wait=w, inc=[tok])
        return s

    def wait_lds(self, eng):
        w = []
        for q in ("sync", "gpsimd", "scalar"):
            for p in ("ld_", "st_"):
                if p + q in self.sems:
                    w.append(self.sems[p + q].at())
        self.op(eng, (lambda e: None), wait=w)


INPUT_SHAPES = {
        "x_in": [NB, T, D],
        "a_wz": [D, A_DIM], "a_wr": [D, A_REST], "a_bz": [1, A_DIM], "a_br": [1, A_REST], "a_brc": [128, A_REST // 128],
        "a_conv": [128, 72], "a_dtb": [A_H, 1], "a_alog": [A_H, 1], "a_L": [128, 4],
        "a_D": [A_H, 1], "a_nw": [1, A_DIM], "a_wo": [A_DIM, D], "a_bo": [1, D],
        "f_wqkv": [2, D, 3 * F_INNER], "f_qs": [2, F_HD], "f_ks": [2, F_HD],
        "f_wo": [2, F_INNER, D], "f_lng": [2, 1, D], "f_lnb": [2, 1, D],
        "f_w1a": [2, D, F_HIDP], "f_w1g": [2, D, F_HIDP],
        "f_b1a": [2, 128, F_HIDP // 128], "f_b1g": [2, 128, F_HIDP // 128],
        "f_w2": [2, F_HIDP, D], "f_b2": [2, 1, D],
        "g_wq": [D, D], "g_wk": [D, 256], "g_wv": [D, 256], "g_wo": [D, D],
        "m_wr": [D, M_E], "m_wg": [M_E, D, M_HID], "m_wu": [M_E, D, M_HID], "m_wd": [M_E, M_HID, D],
        "t_cq": [T, F_HD], "t_sq": [T, F_HD], "t_ck": [T, F_HD], "t_sk": [T, F_HD],
        "t_causal01": [128, 128], "t_causbias": [128, 128], "t_causbias_fm": [128, 128],
        "t_caus01_fm": [128, 128], "t_anti01_fm": [128, 128], "t_fen_fm": [128, 8 * 128], "t_fen_tm": [128, 8 * 128], "t_eye": [128, 128],
}
for _k in INPUT_SHAPES:
    INPUT_SHAPES[_k] = [int(x) for x in INPUT_SHAPES[_k]]


def setup(mk):
    nc = mk.nc
    for nm, s in INPUT_SHAPES.items():
        mk.dram_in(nm, s)
    mk.dram_out("out", [NB, T, D])
    if mk.debug:
        mk.dram_in("dbg_xnfm", [D, T], BF16)
        mk.dram_in("dbg_tm", [T, D])
    mk.dram_scratch("sc_xn32", [T, D])
    mk.dram_scratch("sc_xnfm", [D, T], BF16)
    mk.dram_scratch("sc_xa", [T, D])
    mk.dram_scratch("sc_xb", [T, D])
    mk.dram_scratch("sc_v", [T, A_DIM], BF16)
    mk.dram_scratch("sc_qk", [T, T], BF16)
    mk.dram_scratch("sc_small", [128, 8192])
    mk.dram_scratch("sc_fx", [T, D])
    mk.dram_scratch("sc_fn32", [T, D])
    mk.dram_scratch("sc_fnfm", [D, T], BF16)
    mk.dram_scratch("sc_qT", [F_INNER, T], BF16)
    mk.dram_scratch("sc_kT", [F_INNER, T], BF16)
    mk.dram_scratch("sc_vtm", [T, F_INNER], BF16)
    mk.dram_scratch("sc_xn1fm", [D, T], BF16)
    mk.dram_scratch("sc_xn132", [T, D])
    mk.dram_scratch("sc_B", [128, T], BF16)
    mk.dram_scratch("sc_C", [128, T], BF16)
    mk.dram_scratch("sc_lam", [128, 16 * 512], BF16)
    mk.dram_scratch("sc_cols", [128, 8192])
    mk.dram_scratch("sc_gc", [32, T])
    mk.dram_scratch("sc_y", [T, A_DIM])

    mk.alloc_arenas(w32=14000, w16=60000)

    mk.consts = {}
    C = mk.consts
    cs = mk.es
    def const16(name, shape):
        t = cs.enter_context(nc.sbuf_tensor("c_" + name, list(shape), BF16))
        return t
    def const32(name, shape):
        t = cs.enter_context(nc.sbuf_tensor("c_" + name, list(shape), F32))
        return t
    C["eye16"] = const16("eye16", [128, 128])
    C["eye32"] = const32("eye32", [128, 128])
    C["ones_row"] = const16("ones_row", [1, 512])
    C["ones_col"] = const16("ones_col", [128, 1])
    C["caus01_fm16"] = const16("caus01_fm16", [128, 128])
    C["anti01_fm16"] = const16("anti01_fm16", [128, 128])
    C["causal01"] = const32("causal01", [128, 128])
    C["causbias_fm"] = const32("causbias_fm", [128, 128])
    C["caus01_fm"] = const32("caus01_fm", [128, 128])
    C["fen_fm"] = const16("fen_fm", [128, 8 * 128])
    C["fen_tm"] = const16("fen_tm", [128, 8 * 128])
    C["a_dtb"] = const32("a_dtb", [A_H, 1])
    C["a_convc"] = const32("a_convc", [128, 72])
    C["a_negA"] = const32("a_negA", [A_H, 1])
    C["a_L"] = const32("a_L", [128, 4])
    C["a_D"] = const32("a_D", [A_H, 1])
    C["a_bz"] = const16("a_bz", [1, A_DIM])
    C["a_br32"] = const32("a_br32", [128, A_REST // 128])
    C["a_nw"] = const16("a_nw", [1, A_DIM])
    C["a_bo"] = const16("a_bo", [1, D])
    C["f_b2"] = const16("f_b2", [1, 2 * D])
    C["f_qs"] = const32("f_qs", [1, 2 * F_HD])
    C["f_ks"] = const32("f_ks", [1, 2 * F_HD])
    C["f_lng"] = const16("f_lng", [1, 2 * D])
    C["f_lnb"] = const16("f_lnb", [1, 2 * D])
    C["fc"] = const32("fc", [128, 8])
    mk.pb = [mk.es.enter_context(nc.psum_tensor(f"pb{i}", [128, 512], F32)) for i in range(7)]
    mk.ptp = mk.es.enter_context(nc.psum_tensor("ptp", [128, 1024], BF16))

    d = mk.dram
    mk.load(C["causal01"][:], d["t_causal01"])
    mk.load(C["causbias_fm"][:], d["t_causbias_fm"])
    mk.load(C["caus01_fm"][:], d["t_caus01_fm"])
    mk.load(C["a_dtb"][:], d["a_dtb"])
    mk.load(C["a_convc"][:], d["a_conv"])
    mk.load(C["a_L"][:], d["a_L"])
    mk.load(C["a_D"][:], d["a_D"])
    mk.load(C["f_qs"][:], d["f_qs"].rearrange("l d -> (l d)")[None, :])
    mk.load(C["f_ks"][:], d["f_ks"].rearrange("l d -> (l d)")[None, :])
    mk.load(C["a_br32"][:], d["a_brc"])
    mk.load(C["eye16"][:], d["t_eye"], q="gpsimd")
    mk.load(C["caus01_fm16"][:], d["t_caus01_fm"], q="gpsimd")
    mk.load(C["anti01_fm16"][:], d["t_anti01_fm"], q="gpsimd")
    mk.load(C["fen_fm"][:], d["t_fen_fm"], q="gpsimd")
    mk.load(C["fen_tm"][:], d["t_fen_tm"], q="gpsimd")
    mk.load(C["a_bz"][:], d["a_bz"], q="gpsimd")
    mk.load(C["a_nw"][:], d["a_nw"], q="gpsimd")
    mk.load(C["a_bo"][:], d["a_bo"], q="gpsimd")
    mk.load(C["f_b2"][:], d["f_b2"].rearrange("l one n -> (l one n)")[None, :], q="gpsimd")
    mk.load(C["f_lng"][:], d["f_lng"].rearrange("l one n -> (l one n)")[None, :], q="gpsimd")
    mk.load(C["f_lnb"][:], d["f_lnb"].rearrange("l one n -> (l one n)")[None, :], q="gpsimd")
    mk.load(C["eye32"][:], d["t_eye"])
    mk.load(C["a_negA"][:], d["a_alog"])
    lds = [mk.sems["ld_gpsimd"].at(), mk.sems["ld_sync"].at()]
    def mkconsts(e):
        e.memset(C["ones_row"][:], 1.0)
        e.memset(C["ones_col"][:], 1.0)
        e.memset(C["fc"][:, 0:1], 1e-5)
        e.memset(C["fc"][:, 1:2], 1.0)
        return e.engine_nop()
    gpd = mk.sem("gp_done")
    mk.op("gpsimd", mkconsts, wait=lds, inc=[gpd.inc()])
    def negA2(e):
        return e.activation(C["a_negA"][:], C["a_negA"][:], AF.Exp, scale=1.0)
    scd = mk.sem("sc_done")
    mk.op("scalar", negA2, wait=lds, inc=[scd.inc()])
    def negA3(e):
        return e.tensor_scalar(C["a_negA"][:], C["a_negA"][:], -1.0, None, ALU.mult)
    dvd = mk.sem("dv_done")
    mk.op("vector", negA3, wait=[(scd, scd.n)], inc=[dvd.inc()])
    mk.bar(extra=[gpd.at(), scd.at(), dvd.at()])


def ln_phase(mk, src, xn32, xnfm, eps=1e-5):
    mk.prog.phase = "ln"
    mk.reset()
    x_sb = mk.c32(2 * D)
    sq = mk.c32(D)
    xn_sb = mk.c32(2 * D)
    xnb = mk.c16(2 * D)
    fmb = mk.c16(2 * D)
    cols = mk.c32(8)
    eye16 = mk.consts["eye16"]
    ld = mk.sem("ld_sync"); st = mk.sem("st_sync")
    mm = mk.sem("p_mm"); ev = mk.sem("p_ev"); sc = mk.sem("p_sc"); dv = mk.sem("p_dv")
    rel = {}
    ptp = mk.ptp
    for i in range(TT):
        pi = i % 2
        xt = x_sb[:, pi * D:(pi + 1) * D]
        xnt = xn_sb[:, pi * D:(pi + 1) * D]
        xbt = xnb[:, pi * D:(pi + 1) * D]
        fmt = fmb[:, pi * D:(pi + 1) * D]
        w = rel.pop(i - 2, [])
        if w:
            w = [(st, st.n)] + w
        mk.op("sync", (lambda e: None), wait=w)
        mk.load(xt, src[i * 128:(i + 1) * 128, :])
        t_ld = ld.n
        w_sq = [(ld, t_ld)] + ([("RED_PREV")] if False else [])
        mk.op("scalar", (lambda e, a=sq, x=xt: e.activation(a, x, AF.Square)),
              wait=[(ld, t_ld)] + (rel.get(("sq", i - 1), [])), inc=[sc.inc()])
        def red(e, x=xt, s=sq, c=cols):
            e.reduce_sum(c[:, 0:1], x, axis=AX, negate=True).then_inc(dv.h, 1)
            return e.reduce_sum(c[:, 2:3], s, axis=AX)
        tok1 = dv.inc(); tok2 = dv.inc()
        mk.op("vector", red, wait=[(sc, sc.n)], inc=[tok2])
        def smu(e, c=cols):
            e.activation(c[:, 1:2], c[:, 0:1], AF.Copy, scale=1.0 / D).then_inc(sc.h, 1)
            return e.activation(c[:, 3:4], c[:, 0:1], AF.Square, scale=1.0 / D)
        tok3 = sc.inc(); tok4 = sc.inc()
        mk.op("scalar", smu, wait=[(dv, dv.n)], inc=[tok4])
        def var(e, c=cols):
            return e.tensor_scalar(c[:, 4:5], c[:, 2:3], 1.0 / D, c[:, 3:4], ALU.mult, ALU.subtract)
        mk.op("vector", var, wait=[(sc, sc.n)], inc=[dv.inc()])
        def rstd(e, c=cols):
            mk.fen(e, e.activation(c[:, 5:6], c[:, 4:5], AF.Ln, bias=mk.fc(1e-5)))
            return e.activation(c[:, 6:7], c[:, 5:6], AF.Exp, scale=-0.5)
        mk.op("scalar", rstd, wait=[(dv, dv.n)], inc=[sc.inc()])
        def norm(e, x=xt, c=cols, xn=xnt, xb=xbt):
            e.tensor_scalar(xn, x, c[:, 1:2], c[:, 6:7], ALU.add, ALU.mult).then_inc(dv.h, 1)
            return e.tensor_scalar(xb, x, c[:, 1:2], c[:, 6:7], ALU.add, ALU.mult)
        tok5 = dv.inc(); tok6 = dv.inc()
        mk.op("vector", norm, wait=[(sc, sc.n)], inc=[tok6])
        t_norm = dv.n
        mk.op("sync", (lambda e: None), wait=[(dv, t_norm)])
        mk.store(xn32[i * 128:(i + 1) * 128, :], xnt)
        def tp(e, xb=xbt):
            last = None
            for k in range(8):
                last = e.transpose(ptp[:, k * 128:(k + 1) * 128], xb[:, k * 128:(k + 1) * 128], eye16[:])
            return last
        mk.op("tensor", tp, wait=[(dv, t_norm)] + ([(ev, ev.n)] if i > 0 else []), inc=[mm.inc()])
        mk.op("vector", (lambda e, f=fmt: e.tensor_copy(f, ptp[:])), wait=[(mm, mm.n)], inc=[ev.inc()])
        t_ev = ev.n
        mk.op("sync", (lambda e: None), wait=[(ev, t_ev)])
        for k in range(8):
            mk.store(xnfm[k * 128:(k + 1) * 128, i * 128:(i + 1) * 128], fmt[:, k * 128:(k + 1) * 128])
        rel[i] = [(mm, mm.n), (dv, t_norm)]
    mk.bar()


def moe_phase(mk, xnfm_d, xn32_d, out_d):
    nc = mk.nc
    mk.prog.phase = "moe"
    mk.reset()
    d = mk.dram
    xw = mk.c16(16384)
    h_sb = mk.c16(32768)
    wcol = mk.c16(4096)
    gsb = mk.c16(2 * 512)
    usb = mk.c16(2 * 512)
    esb = mk.c32(8)
    msk = mk.c32(8)
    tmp8 = mk.c32(8)
    cu8 = mk.c32(8)
    combine = mk.c32(128)
    rc = mk.c32(8)
    ysc = mk.c32(4 * 512)
    xt32 = mk.c32(2 * D)
    xT32 = mk.c32(2 * 128)
    rw32 = mk.c32(64)
    ld = mk.sem("ld_sync"); ldg = mk.sem("ld_gpsimd")
    mm = mk.sem("p_mm"); dv = mk.sem("p_dv"); sc = mk.sem("p_sc"); ev = mk.sem("p_ev")
    sty = [mk.sem(f"st_y{i}") for i in range(4)]
    pr = mk.pb[5]
    eye32 = mk.consts["eye32"]

    mk.load(xw[:], xnfm_d.rearrange("(k p) t -> p k t", p=128), q="gpsimd", sem="ld_xw", serialize=True)
    mk.load(rw32[:], d["m_wr"].rearrange("(k p) e -> p k e", p=128), q="sync")
    t_xnld = mk.sems["ld_sync"].n

    for i in range(TT):
        xts = xt32[:, (i % 2) * D:(i % 2 + 1) * D]
        mk.load(xts, xn32_d[i * 128:(i + 1) * 128, :], q="sync", sem="ld_rx", serialize=True,
                wait=([(mm, mm.n)] if i >= 2 else []))
        t_x = mk.sem("ld_rx").n
        for k in range(8):
            pf = mk.pb[6][:, (k % 2) * 128:(k % 2) * 128 + 128]
            xTs = xT32[:, (k % 2) * 128:(k % 2) * 128 + 128]
            def tp(e, xts=xts, k=k, pf=pf):
                return e.transpose(pf, xts[:, k * 128:(k + 1) * 128], eye32[:])
            w = [(mk.sem("ld_rx"), t_x), (dv, dv.n)]
            mk.op("tensor", tp, wait=w, inc=[mm.inc()])
            mk.op("vector", (lambda e, pf=pf, xTs=xTs: e.tensor_copy(xTs, pf)),
                  wait=[(mm, mm.n)], inc=[dv.inc()])
            def rmm(e, k=k, xTs=xTs):
                return e.matmul(pr[:, 0:8], xTs, rw32[:, k * 8:(k + 1) * 8],
                                start=(k == 0), stop=(k == 7))
            mk.op("tensor", rmm, wait=[(dv, dv.n), (ld, t_xnld)], inc=[mm.inc()])
        mk.op("vector", (lambda e: e.reduce_max(rc[:, 0:1], pr[:, 0:8], axis=AX, negate=True)),
              wait=[(mm, mm.n)], inc=[dv.inc()])
        mk.op("scalar", (lambda e: e.activation(esb, pr[:, 0:8], AF.Exp, bias=rc[:, 0:1], accum_out=rc[:, 1:2])),
              wait=[(dv, dv.n)], inc=[sc.inc()])
        def rtop(e, i=i):
            mk.fen(e, e.reduce_max(rc[:, 2:3], esb, axis=AX))
            mk.fen(e, e.tensor_scalar(msk, esb, rc[:, 2:3], None, ALU.is_equal))
            mk.fen(e, e.scalar_tensor_tensor(tmp8, msk, NEG, esb, ALU.mult, ALU.add))
            mk.fen(e, e.reduce_max(rc[:, 3:4], tmp8, axis=AX))
            mk.fen(e, e.tensor_scalar(msk, esb, rc[:, 3:4], None, ALU.is_ge))
            mk.fen(e, e.tensor_tensor(cu8, esb, msk, ALU.mult))
            mk.fen(e, e.reduce_sum(rc[:, 4:5], cu8, axis=AX))
            mk.fen(e, e.reciprocal(rc[:, 5:6], rc[:, 4:5]))
            return e.tensor_scalar(combine[:, i * 8:(i + 1) * 8], cu8, rc[:, 5:6], None, ALU.mult)
        mk.op("vector", rtop, wait=[(sc, sc.n)], inc=[dv.inc()])
    t_router = dv.n

    t_down_prev = mm.n
    for e_i in range(M_E):
        if e_i > 0:
            mk.load(xw[:], xnfm_d.rearrange("(k p) t -> p k t", p=128), q="gpsimd",
                    sem="ld_xw", serialize=True, wait=[(mm, t_down_prev)])
        t_xn = mk.sem("ld_xw").n
        wslot_mm = {}
        for ht in range(16):
            wslot = wcol[:, (ht % 2) * 2048:(ht % 2) * 2048 + 2048]
            if ht >= 2:
                mk.op("gpsimd", (lambda e: None), wait=[(mm, wslot_mm[ht - 2])])
            mk.load(wslot[:, 0:1024], d["m_wg"][e_i].rearrange("(k p) h -> p k h", p=128)[:, :, ht * 128:(ht + 1) * 128], q="gpsimd", sem="ld_wcol", serialize=True)
            mk.load(wslot[:, 1024:2048], d["m_wu"][e_i].rearrange("(k p) h -> p k h", p=128)[:, :, ht * 128:(ht + 1) * 128], q="gpsimd", sem="ld_wcol")
            ldw = mk.sem("ld_wcol"); t_w = ldw.n
            for tc in range(4):
                pg = mk.pb[tc % 2]; pu = mk.pb[2 + tc % 2]
                def gu(e, ht=ht, tc=tc, pg=pg, pu=pu, wslot=wslot, t_w=t_w, t_xn=t_xn):
                    last = None
                    for k in range(8):
                        last = e.matmul(pg[:], wslot[:, k * 128:(k + 1) * 128],
                                        xw[:, k * 2048 + tc * 512: k * 2048 + (tc + 1) * 512],
                                        start=(k == 0), stop=(k == 7))
                    for k in range(8):
                        last = e.matmul(pu[:], wslot[:, 1024 + k * 128: 1024 + (k + 1) * 128],
                                        xw[:, k * 2048 + tc * 512: k * 2048 + (tc + 1) * 512],
                                        start=(k == 0), stop=(k == 7))
                    return last
                w = [(ldw, t_w), (mk.sem("ld_xw"), t_xn)]
                if (ht * 4 + tc) >= 2:
                    w.append((sc, max(0, sc.n - 2)))
                mk.op("tensor", gu, wait=w, inc=[mm.inc()])
                def evac(e, tc=tc, pg=pg, pu=pu):
                    e.activation(gsb[:, (tc % 2) * 512:(tc % 2 + 1) * 512], pg[:], AF.Silu).then_inc(sc.h, 1)
                    return e.activation(usb[:, (tc % 2) * 512:(tc % 2 + 1) * 512], pu[:], AF.Copy)
                tokA = sc.inc(); tokB = sc.inc()
                mk.op("scalar", evac, wait=[(mm, mm.n), (dv, max(0, dv.n - 1))], inc=[tokB])
                def hmul(e, ht=ht, tc=tc):
                    return e.tensor_tensor(h_sb[:, ht * 2048 + tc * 512: ht * 2048 + (tc + 1) * 512],
                                           gsb[:, (tc % 2) * 512:(tc % 2 + 1) * 512],
                                           usb[:, (tc % 2) * 512:(tc % 2 + 1) * 512], ALU.mult)
                mk.op("vector", hmul, wait=[(sc, sc.n), (mm, t_down_prev)], inc=[dv.inc()])
            wslot_mm[ht] = mm.n
        t_gu_prev = mm.n
        t_h = dv.n
        mk.load(xw[:], d["m_wd"][e_i].rearrange("(k p) dd -> p k dd", p=128), q="gpsimd",
                sem="ld_xw", serialize=True, wait=[(mm, t_gu_prev)])
        t_wd = mk.sem("ld_xw").n
        for tt in range(TT):
            for dc in range(2):
                slot = (tt * 2 + dc) % 2
                py = mk.pb[4 + slot]
                def down(e, tt=tt, dc=dc, py=py):
                    last = None
                    for k in range(16):
                        last = e.matmul(py[:], h_sb[:, k * 2048 + tt * 128: k * 2048 + (tt + 1) * 128],
                                        xw[:, k * 1024 + dc * 512: k * 1024 + (dc + 1) * 512],
                                        start=(k == 0), stop=(k == 15))
                    return last
                w = [(mk.sem("ld_xw"), t_wd), (dv, t_h)]
                if (tt * 2 + dc) >= 2:
                    w.append((ev, ev.n - 1))
                mk.op("tensor", down, wait=w, inc=[mm.inc()])
                yslot = (tt * 2 + dc) % 4
                def yevac(e, tt=tt, dc=dc, py=py, yslot=yslot, e_i=e_i):
                    return e.tensor_scalar(ysc[:, yslot * 512:(yslot + 1) * 512], py[:],
                                           combine[:, tt * 8 + e_i: tt * 8 + e_i + 1], None, ALU.mult)
                w2 = [(mm, mm.n), (dv, t_router)]
                if sty[yslot].n > 0:
                    w2.append((sty[yslot], sty[yslot].n))
                mk.op("vector", yevac, wait=w2, inc=[ev.inc()])
                mk.op("gpsimd", (lambda e: None), wait=[(ev, ev.n)])
                def acst(e, tt=tt, dc=dc, yslot=yslot):
                    return e.dma_start(out=out_d[tt * 128:(tt + 1) * 128, dc * 512:(dc + 1) * 512],
                                      in_=ysc[:, yslot * 512:(yslot + 1) * 512], accum_op=ALU.add)
                mk.op("gpsimd", acst, inc=[sty[yslot].inc(16)])
        t_down_prev = mm.n
    mk.bar()


def l1_attn_phase(mk, xnfm_d, xacc_d, out_d):
    nc = mk.nc
    mk.prog.phase = "l1a"
    mk.reset()
    d = mk.dram
    C = mk.consts
    qT = mk.c16(16384)
    kT = mk.c16(4096)
    vsb = mk.c16(16 * 260)
    attb = mk.c16(2 * 512)
    xnl = mk.c16(16384)
    ctx = mk.c16(16384)
    woc = xnl[:, 10240:11264]
    nrow = xnl[:, 8192:10240]
    negc = mk.c32(256)
    ycol = mk.c32(4)
    xat = mk.c32(2 * D)
    osb = mk.c32(2 * D)
    ld = mk.sem("ld_sync"); ldg = mk.sem("ld_gpsimd")
    mm = mk.sem("p_mm"); dv = mk.sem("p_dv"); sc = mk.sem("p_sc"); ev = mk.sem("p_ev")
    mk.load(xnl[:], xnfm_d.rearrange("(k p) t -> p k t", p=128), q="sync")
    wq_sb = None
    wq_sb = ctx[:, 0:8192]; wk_sb = ctx[:, 8192:10240]; wv_sb = ctx[:, 10240:12288]
    mk.load(wq_sb, d["g_wq"].rearrange("(k p) o -> p k o", p=128), q="gpsimd")
    mk.load(wk_sb, d["g_wk"].rearrange("(k p) o -> p k o", p=128), q="gpsimd")
    mk.load(wv_sb, d["g_wv"].rearrange("(k p) o -> p k o", p=128), q="gpsimd")
    t_ld = ld.n; t_ldg = ldg.n

    for o in range(8):
        for c in range(4):
            pbk = mk.pb[(o * 4 + c) % 2]
            def qmm(e, o=o, c=c, pbk=pbk):
                last = None
                for k in range(8):
                    last = e.matmul(pbk[:], wq_sb[:, k * 1024 + o * 128: k * 1024 + (o + 1) * 128],
                                    xnl[:, k * 2048 + c * 512: k * 2048 + (c + 1) * 512],
                                    start=(k == 0), stop=(k == 7))
                return last
            w = [(ld, t_ld), (ldg, t_ldg)]
            if (o * 4 + c) >= 2:
                w.append((dv, dv.n - 1))
            mk.op("tensor", qmm, wait=w, inc=[mm.inc()])
            mk.op("vector", (lambda e, o=o, c=c, pbk=pbk: e.tensor_copy(qT[:, o * 2048 + c * 512: o * 2048 + (c + 1) * 512], pbk[:])),
                  wait=[(mm, mm.n)], inc=[dv.inc()])
    for o in range(2):
        for c in range(4):
            pbk = mk.pb[(o * 4 + c) % 2]
            def kmm(e, o=o, c=c, pbk=pbk):
                last = None
                for k in range(8):
                    last = e.matmul(pbk[:], wk_sb[:, k * 256 + o * 128: k * 256 + (o + 1) * 128],
                                    xnl[:, k * 2048 + c * 512: k * 2048 + (c + 1) * 512],
                                    start=(k == 0), stop=(k == 7))
                return last
            mk.op("tensor", kmm, wait=[(dv, dv.n - 1)], inc=[mm.inc()])
            mk.op("vector", (lambda e, o=o, c=c, pbk=pbk: e.tensor_copy(kT[:, o * 2048 + c * 512: o * 2048 + (c + 1) * 512], pbk[:])),
                  wait=[(mm, mm.n)], inc=[dv.inc()])
    for i in range(TT):
        pbk = mk.pb[i % 2]
        def vmm(e, i=i, pbk=pbk):
            last = None
            for k in range(8):
                last = e.matmul(pbk[:, 0:256], xnl[:, k * 2048 + i * 128: k * 2048 + (i + 1) * 128],
                                wv_sb[:, k * 256:(k + 1) * 256], start=(k == 0), stop=(k == 7))
            return last
        mk.op("tensor", vmm, wait=[(dv, dv.n - 1)], inc=[mm.inc()])
        def vev(e, i=i, pbk=pbk):
            last = None
            for g in range(4):
                last = e.tensor_copy(vsb[:, i * 260 + g * 65: i * 260 + g * 65 + 64], pbk[:, g * 64:(g + 1) * 64])
            return last
        mk.op("vector", vev, wait=[(mm, mm.n)], inc=[dv.inc()])
    def vones(e):
        return e.memset(vsb[:].rearrange("p (i g c) -> p i g c", i=16, g=4)[:, :, :, 64:65], 1.0)
    mk.op("vector", vones, inc=[dv.inc()])
    t_proj = dv.n

    nrow2 = xnl[:, 11264:15360]
    if not hasattr(mk, "_l1_attb_mm"):
        mk._l1_attb_mm = [0, 0]
        mk._l1_ybank_free = [0, 0, 0, 0]
    attb_mm = mk._l1_attb_mm
    ybank_free = mk._l1_ybank_free
    t_p1r = [0]
    par_end = [mm.n, mm.n]

    def head_geom(h):
        qtile = (h % 4) + 4 * (h // 8)
        qbase = ((h // 4) % 2) * 64
        g = h // 4
        qsl = lambda tsl, qt=qtile, qb=qbase: qT[qb:qb + 64, qt * 2048 + tsl[0]:qt * 2048 + tsl[1]]
        ksl = lambda ssl, g=g: kT[(g % 2) * 64:(g % 2) * 64 + 64, (g // 2) * 2048 + ssl[0]:(g // 2) * 2048 + ssl[1]]
        return qsl, ksl

    def mk_pass1_ops(h, first=False):
        qsl, ksl = head_geom(h)
        ops = []
        for i in range(TT):
            wdt = (i + 1) * 128
            for c in range((wdt + 511) // 512):
                n = min(512, wdt - c * 512)
                def ep1(i=i, c=c, n=n, qsl=qsl, ksl=ksl, fw=(first and i == 0 and c == 0)):
                    def p1(e, i=i, c=c, n=n, qsl=qsl, ksl=ksl):
                        return e.matmul(mk.pb[6][:, 0:n], qsl((i * 128, (i + 1) * 128)),
                                        ksl((c * 512, c * 512 + n)), start=True, stop=True)
                    w = [(dv, t_p1r[0])] if not fw else [(dv, t_proj)]
                    mk.op("tensor", p1, wait=w, inc=[mm.inc()])
                def ep1r(h=h, i=i, c=c, n=n):
                    def p1r(e, h=h, i=i, c=c, n=n):
                        if c == 0:
                            return e.reduce_max(negc[:, h * 16 + i: h * 16 + i + 1], mk.pb[6][:, 0:n], axis=AX, negate=True)
                        last = e.reduce_max(ycol[:, 0:1], mk.pb[6][:, 0:n], axis=AX, negate=True)
                        mk.fen(e, last)
                        return e.tensor_tensor(negc[:, h * 16 + i: h * 16 + i + 1],
                                               negc[:, h * 16 + i: h * 16 + i + 1], ycol[:, 0:1], ALU.min)
                    mk.op("vector", p1r, wait=[(mm, mm.n)], inc=[dv.inc()])
                    t_p1r[0] = dv.n
                ops.append((ep1, ep1r))
        return ops

    t_nrow_tab = [0, 0]
    def emit_nrow(h):
        par = h % 2
        ac = 8000 + par * 128
        def ntp(e, h=h):
            return e.transpose(mk.pb[4][0:16, 0:128], negc[:, h * 16:(h + 1) * 16], C["eye32"][:])
        mk.op("tensor", ntp, wait=[(dv, dv.n), (sc, sc.n)], inc=[mm.inc()])
        def nev(e, ac=ac):
            return e.tensor_copy(mk.A32[0:16, ac:ac + 128], mk.pb[4][0:16, 0:128])
        mk.op("vector", nev, wait=[(mm, mm.n)], inc=[dv.inc()])
        nr_flat = mk.dram["sc_small"].rearrange("a b -> (a b)")
        sst = "st_nrow" + str(par); sld = "ld_nrow" + str(par)
        mk.store(nr_flat[par * 2048:par * 2048 + 2048].rearrange("(i x) -> i x", i=16), mk.A32[0:16, ac:ac + 128],
                 q="sync", sem=sst, serialize=True, wait=[(dv, dv.n), (mk.sem(sld), mk.sem(sld).n)])
        nr_src = nr_flat[par * 2048:par * 2048 + 2048].rearrange("(one t) -> one t", one=1)
        mk.load(nrow2[0:1, par * 2048:par * 2048 + 2048], nr_src, q="gpsimd", sem=sld, serialize=True,
                wait=[(mk.sem(sst), mk.sem(sst).n), (mm, par_end[par])])
        t_nrow_tab[par] = mk.sem(sld).n

    for (ep1, ep1r) in mk_pass1_ops(0, first=True):
        ep1(); ep1r()
    emit_nrow(0)

    for h in range(G_NH):
        qsl, ksl = head_geom(h)
        nslot = nrow2[:, (h % 2) * 2048:(h % 2) * 2048 + 2048]
        t_nrow = t_nrow_tab[h % 2]
        nxt = mk_pass1_ops(h + 1) if h + 1 < G_NH else []
        nxi = [0]
        def pump():
            if nxi[0] < len(nxt):
                ep1, ep1r = nxt[nxi[0]]
                ep1(); ep1r()
                nxi[0] += 1
        for ib in range(4):
            iis = [ib * 4 + 0, ib * 4 + 1, ib * 4 + 2, ib * 4 + 3]
            for j in range(iis[-1] + 1):
                pbs = mk.pb[4 + j % 2]
                def p2(e, j=j, ib=ib, pbs=pbs, qsl=qsl, ksl=ksl, nslot=nslot):
                    e.matmul(pbs[:], ksl((j * 128, (j + 1) * 128)), qsl((ib * 512, (ib + 1) * 512)), start=True, stop=False)
                    return e.matmul(pbs[:], C["ones_row"][0:1, 0:128], nslot[0:1, ib * 512:(ib + 1) * 512], start=False, stop=True)
                mk.op("tensor", p2, wait=[(mk.sem("ld_nrow" + str(h % 2)), t_nrow), (sc, max(0, sc.n - 1))], inc=[mm.inc()])
                asl = attb[:, (j % 2) * 512:(j % 2) * 512 + 512]
                def p2e(e, pbs=pbs, asl=asl):
                    return e.activation(asl, pbs[:], AF.Exp, scale=0.125)
                mk.op("scalar", p2e, wait=[(mm, mm.n), (mm, attb_mm[j % 2])], inc=[sc.inc()])
                t_exp = sc.n
                masked = False
                if ib * 4 <= j <= iis[-1]:
                    def pmask(e, j=j, ib=ib, asl=asl):
                        sl2 = asl[:, (j - ib * 4) * 128:(j - ib * 4) * 128 + 128]
                        return e.tensor_tensor(sl2, sl2, C["caus01_fm16"][:], ALU.mult)
                    mk.op("vector", pmask, wait=[(sc, t_exp)], inc=[dv.inc()])
                    masked = True
                t_rdy = (dv, dv.n) if masked else (sc, t_exp)
                for i in iis:
                    if j > i:
                        continue
                    pby = mk.pb[i % 4][:, 0:65]
                    def ymm(e, h=h, i=i, j=j, ib=ib, asl=asl, pby=pby):
                        return e.matmul(pby, asl[:, (i - ib * 4) * 128:(i - ib * 4) * 128 + 128],
                                        vsb[:, j * 260 + (h // 4) * 65: j * 260 + (h // 4) * 65 + 65],
                                        start=(j == 0), stop=(j == i))
                    wy = [t_rdy]
                    if j == 0:
                        wy.append((dv, ybank_free[i % 4]))
                    mk.op("tensor", ymm, wait=wy, inc=[mm.inc()])
                attb_mm[j % 2] = mm.n
                pump()
            for i in iis:
                pby = mk.pb[i % 4][:, 0:65]
                def yev(e, h=h, i=i, pby=pby):
                    mk.fen(e, e.reciprocal(ycol[:, 1:2], pby[:, 64:65]))
                    return e.tensor_scalar(ctx[:, i * 1024 + h * 64: i * 1024 + h * 64 + 64], pby[:, 0:64],
                                           ycol[:, 1:2], None, ALU.mult)
                mk.op("vector", yev, wait=[(mm, mm.n), (dv, dv.n)], inc=[dv.inc()])
                ybank_free[i % 4] = dv.n
        while nxi[0] < len(nxt):
            pump()
        par_end[h % 2] = mm.n
        if h + 1 < G_NH:
            emit_nrow(h + 1)
    t_ctx = dv.n

    for i in range(TT):
        def ctp(e, i=i):
            last = None
            for k in range(8):
                last = e.transpose(mk.ptp[:, k * 128:(k + 1) * 128], ctx[:, i * 1024 + k * 128: i * 1024 + (k + 1) * 128], C["eye16"][:])
            return last
        w = [(dv, t_ctx)] if i == 0 else [(dv, dv.n)]
        mk.op("tensor", ctp, wait=w, inc=[mm.inc()])
        mk.op("vector", (lambda e: e.tensor_copy(woc[:], mk.ptp[:])), wait=[(mm, mm.n)], inc=[dv.inc()])
        if i == 0:
            wo_sb = xnl[:, 0:8192]
            mk.load(wo_sb, d["g_wo"].rearrange("(k p) o -> p k o", p=128), q="gpsimd", sem="ld_wo",
                    wait=[(mm, mm.n)])
        def wmm(e, i=i):
            last = None
            for k in range(8):
                last = e.matmul(mk.pb[(i % 2)][:, 0:512], woc[:, k * 128:(k + 1) * 128], wo_sb[:, k * 1024:k * 1024 + 512],
                                start=(k == 0), stop=(k == 7))
            for k in range(8):
                last = e.matmul(mk.pb[2 + (i % 2)][:, 0:512], woc[:, k * 128:(k + 1) * 128], wo_sb[:, k * 1024 + 512:(k + 1) * 1024],
                                start=(k == 0), stop=(k == 7))
            return last
        mk.op("tensor", wmm, wait=[(dv, dv.n), (mk.sem("ld_wo"), mk.sem("ld_wo").n)] + ([(dv, dv.n)] if i >= 2 else []), inc=[mm.inc()])
        xs = xat[:, (i % 2) * D:(i % 2 + 1) * D]
        os_ = osb[:, (i % 2) * D:(i % 2 + 1) * D]
        mk.load(xs, xacc_d[i * 128:(i + 1) * 128, :], q="sync", sem="ld_xa", serialize=True)
        def radd(e, i=i, xs=xs, os_=os_):
            e.scalar_tensor_tensor(os_[:, 0:512], mk.pb[(i % 2)][:, 0:512], 1.0, xs[:, 0:512], ALU.mult, ALU.add).then_inc(dv.h, 1)
            return e.scalar_tensor_tensor(os_[:, 512:1024], mk.pb[2 + (i % 2)][:, 0:512], 1.0, xs[:, 512:1024], ALU.mult, ALU.add)
        tokA = dv.inc(); tokB = dv.inc()
        mk.op("vector", radd, wait=[(mm, mm.n), (mk.sem("ld_xa"), mk.sem("ld_xa").n)], inc=[tokB])
        mk.op("sync", (lambda e: None), wait=[(dv, dv.n)])
        mk.store(out_d[i * 128:(i + 1) * 128, :], os_, q="sync", sem="st_out", serialize=True)
    mk.bar()


def l0f_attn_phase(mk, l, xin_d, xmid_d):
    nc = mk.nc
    mk.prog.phase = "l0fa"
    mk.reset()
    d = mk.dram
    C = mk.consts
    hs = mk.c16(16384)
    qfT = mk.c16(8192)
    kfT = mk.c16(8192)
    vtm = mk.c16(16 * 520)
    attw = mk.c16(70 * 128)
    wq2 = mk.c16(6144)
    octx = hs[:, 0:8192]
    wbuf = mk.c16(2 * 1536)
    nrow = wbuf[:, 0:2048]
    xt = mk.c32(2 * D)
    qk32 = mk.c32(2 * 1024)
    sqs = mk.c32(2048)
    hcol = mk.c32(32)
    tabq = mk.c32(2048)
    tabk = mk.c32(2048)
    qsb = mk.c32(2 * 64)
    qsrb = mk.c32(2 * 64)
    negc = mk.c32(8 * 16)
    nT = mk.c32(256)
    ld = mk.sem("ld_sync"); ldg = mk.sem("ld_gpsimd")
    mm = mk.sem("p_mm"); dv = mk.sem("p_dv"); sc = mk.sem("p_sc"); ev = mk.sem("p_ev")
    off = [0] * 17
    for j in range(16):
        off[j + 1] = off[j] + (min(j + 4, 15) - j + 1) * 128

    def qsrows(e):
        o = l * 64
        e.tensor_copy(nrow[0:1, 0:64], C["f_qs"][0:1, o:o + 64])
        e.tensor_copy(nrow[0:1, 64:96], C["f_qs"][0:1, o + 32:o + 64])
        e.tensor_copy(nrow[0:1, 96:128], C["f_qs"][0:1, o:o + 32])
        e.tensor_copy(nrow[0:1, 128:192], C["f_ks"][0:1, o:o + 64])
        e.tensor_copy(nrow[0:1, 192:224], C["f_ks"][0:1, o + 32:o + 64])
        return e.tensor_copy(nrow[0:1, 224:256], C["f_ks"][0:1, o:o + 32])
    mk.op("vector", qsrows, inc=[dv.inc()])
    def qsouter(e):
        return e.matmul(mk.pb[0][:, 0:256], C["ones_row"][0:1, 0:128], nrow[0:1, 0:256], start=True, stop=True)
    mk.op("tensor", qsouter, wait=[(dv, dv.n)], inc=[mm.inc()])
    def qstt(e):
        e.tensor_copy(qsb[:, 0:64], mk.pb[0][:, 0:64]).then_inc(dv.h, 1)
        e.tensor_copy(qsrb[:, 0:64], mk.pb[0][:, 64:128]).then_inc(dv.h, 1)
        e.tensor_copy(qsb[:, 64:128], mk.pb[0][:, 128:192]).then_inc(dv.h, 1)
        return e.tensor_copy(qsrb[:, 64:128], mk.pb[0][:, 192:256])
    toks = [dv.inc() for _ in range(4)]
    mk.op("vector", qstt, wait=[(mm, mm.n)], inc=[toks[3]])
    rotbuf = xt[:].rearrange("p x -> p x") if False else None
    for k, nm in enumerate(["t_cq", "t_sq", "t_ck", "t_sk"]):
        dst = xt[:, k * 1024:(k + 1) * 1024] if k < 2 else qk32[:, (k - 2) * 1024:(k - 1) * 1024]
        mk.load(dst, d[nm].rearrange("(i p) d -> p i d", p=128), q="sync", sem="ld_rot")
    t_rotld = mk.sem("ld_rot").n
    def tabs(e):
        v3 = lambda ap: ap.rearrange("p (i d) -> p i d", d=64)
        b3 = lambda ap: ap.rearrange("p (o d) -> p o d", o=1).broadcast_to([128, 16, 64])
        e.tensor_tensor(v3(tabq[:, 0:1024]), v3(xt[:, 0:1024]), b3(qsb[:, 0:64]), ALU.mult).then_inc(dv.h, 1)
        e.tensor_tensor(v3(tabq[:, 1024:2048]), v3(xt[:, 1024:2048]), b3(qsrb[:, 0:64]), ALU.mult).then_inc(dv.h, 1)
        e.tensor_tensor(v3(tabk[:, 0:1024]), v3(qk32[:, 0:1024]), b3(qsb[:, 64:128]), ALU.mult).then_inc(dv.h, 1)
        return e.tensor_tensor(v3(tabk[:, 1024:2048]), v3(qk32[:, 1024:2048]), b3(qsrb[:, 64:128]), ALU.mult)
    toks2 = [dv.inc() for _ in range(4)]
    mk.op("vector", tabs, wait=[(dv, dv.n - 4), (mk.sem("ld_rot"), t_rotld)], inc=[toks2[3]])
    t_tabs = dv.n

    for i in range(TT):
        x0 = xt[:, (i % 2) * D:(i % 2) * D + 512]
        x1 = xt[:, (i % 2) * D + 512:(i % 2) * D + 1024]
        sx = "ld_x" + str(i % 2)
        w_ld = [(mm, mm.n)] if i >= 2 else [(dv, t_tabs)]
        mk.load(x0, xin_d[i * 128:(i + 1) * 128, 0:512], q="sync", sem=sx, serialize=True, wait=w_ld)
        if i == 0:
            mk.load(x1[1:128, :], xin_d[0:127, 512:1024], q="sync", sem=sx)
            mk.op("vector", (lambda e, x1=x1: e.memset(x1[0:1, :], 0.0)),
                  wait=[(mk.sem(sx), mk.sem(sx).n)], inc=[dv.inc()])
        else:
            mk.load(x1, xin_d[i * 128 - 1:(i + 1) * 128 - 1, 512:1024], q="sync", sem=sx)
        t_x = mk.sem(sx).n
        cb = wbuf[:, (i % 2) * 1536:(i % 2) * 1536 + 1024]
        wv = [(mk.sem(sx), t_x), (dv, dv.n)]
        mk.op("vector", (lambda e, cb=cb, xx=xt[:, (i % 2) * D:(i % 2) * D + 1024]: e.tensor_copy(cb, xx)),
              wait=wv, inc=[dv.inc()])
        def tp(e, cb=cb):
            last = None
            for k in range(8):
                last = e.transpose(mk.ptp[:, k * 128:(k + 1) * 128], cb[:, k * 128:(k + 1) * 128], C["eye16"][:])
            return last
        mk.op("tensor", tp, wait=[(dv, dv.n)] + ([(ev, ev.n)] if i > 0 else []), inc=[mm.inc()])
        def ev8(e, i=i):
            last = None
            for k in range(8):
                last = e.tensor_copy(hs[:, k * 2048 + i * 128:k * 2048 + (i + 1) * 128], mk.ptp[:, k * 128:(k + 1) * 128])
            return last
        mk.op("vector", ev8, wait=[(mm, mm.n)], inc=[ev.inc()])
    t_hs = mk.sems["p_ev"].n

    mk.load(attw[:, 0:6144], d["f_wqkv"][l].rearrange("(k p) o -> p k o", p=128)[:, 0:4, :], q="gpsimd", sem="ld_wq")
    mk.load(wq2[:, 0:6144], d["f_wqkv"][l].rearrange("(k p) o -> p k o", p=128)[:, 4:8, :], q="gpsimd", sem="ld_wq")
    t_wq = mk.sem("ld_wq").n
    for i in range(TT):
        pq = mk.pb[0 if i % 2 == 0 else 3]
        pk = mk.pb[1 if i % 2 == 0 else 4]
        pv = mk.pb[2 if i % 2 == 0 else 5]
        def qkvmm(e, i=i, pq=pq, pk=pk, pv=pv):
            last = None
            for c, pbx in ((0, pq), (1, pk), (2, pv)):
                for k in range(8):
                    wsrc = (attw if k < 4 else wq2)[:, (k % 4) * 1536 + c * 512:(k % 4) * 1536 + c * 512 + 512]
                    last = e.matmul(pbx[:], hs[:, k * 2048 + i * 128:k * 2048 + (i + 1) * 128], wsrc,
                                    start=(k == 0), stop=(k == 7))
            return last
        w = [(mk.sem("ld_wq"), t_wq), (ev, t_hs), (dv, t_tabs)]
        if i >= 2:
            w.append((dv, dv.n - 1))
        mk.op("tensor", qkvmm, wait=w, inc=[mm.inc()])
        t_qkv = mm.n
        q32 = qk32[:, (i % 2) * 1024:(i % 2) * 1024 + 512]
        k32 = qk32[:, (i % 2) * 1024 + 512:(i % 2) * 1024 + 1024]
        def qev(e, pq=pq, pk=pk, q32=q32, k32=k32):
            e.tensor_copy(q32, pq[:]).then_inc(dv.h, 1)
            return e.tensor_copy(k32, pk[:])
        tka = dv.inc(); tkb = dv.inc()
        mk.op("vector", qev, wait=[(mm, t_qkv), (dv, max(0, dv.n - 2) if i >= 2 else 0)], inc=[tkb])
        def vev(e, i=i, pv=pv):
            last = None
            for hh in range(8):
                last = e.tensor_copy(vtm[:, i * 520 + hh * 65:i * 520 + hh * 65 + 64], pv[:, hh * 64:(hh + 1) * 64])
            return last
        mk.op("vector", vev, wait=[(mm, t_qkv)], inc=[dv.inc()])
        for sidx, (src32, tab, dstT) in enumerate(((q32, tabq, qfT), (k32, tabk, kfT))):
            sq_s = sqs[:, sidx * 1024:sidx * 1024 + 512]
            rt_s = sqs[:, sidx * 1024 + 512:sidx * 1024 + 1024]
            def sqr(e, src32=src32, sq_s=sq_s):
                return e.activation(sq_s, src32, AF.Square)
            mk.op("scalar", sqr, wait=[(dv, dv.n), (sc, max(0, sc.n - 3))], inc=[sc.inc()])
            def redn(e, sq_s=sq_s, sidx=sidx):
                return e.reduce_sum(hcol[:, sidx * 16:sidx * 16 + 8], sq_s.rearrange("p (h x) -> p h x", h=8), axis=AX)
            mk.op("vector", redn, wait=[(sc, sc.n)], inc=[dv.inc()])
            def rsq(e, sidx=sidx):
                return e.activation(hcol[:, sidx * 16 + 8:sidx * 16 + 16], hcol[:, sidx * 16:sidx * 16 + 8], AF.Sqrt)
            mk.op("scalar", rsq, wait=[(dv, dv.n)], inc=[sc.inc()])
            qfb = wbuf[:, (i % 2) * 1536 + sidx * 512:(i % 2) * 1536 + (sidx + 1) * 512]
            def rota(e, src32=src32, tab=tab, i=i, qfb=qfb, sidx=sidx, sq_s=sq_s, rt_s=rt_s):
                hc = hcol[:, sidx * 16 + 8:sidx * 16 + 16]
                hr = hcol[:, sidx * 16:sidx * 16 + 8]
                mk.fen(e, e.tensor_scalar(hc, hc, 1e-12, None, ALU.max))
                mk.fen(e, e.reciprocal(hr, hc))
                s4 = src32.rearrange("p (h a d) -> p h a d", h=8, a=2)
                cqb = tab[:, i * 64:(i + 1) * 64].rearrange("p (o a d) -> p o a d", o=1, a=2).broadcast_to([128, 8, 2, 32])
                sqb = tab[:, 1024 + i * 64:1024 + (i + 1) * 64].rearrange("p (o a d) -> p o a d", o=1, a=2).broadcast_to([128, 8, 2, 32])
                e.tensor_tensor(sq_s.rearrange("p (h a d) -> p h a d", h=8, a=2), s4, cqb, ALU.mult)
                mk.fen(e, e.tensor_tensor(rt_s.rearrange("p (h a d) -> p h a d", h=8, a=2), s4[:, :, ::-1, :], sqb, ALU.mult))
                mk.fen(e, e.tensor_tensor(sq_s, sq_s, rt_s, ALU.add))
                rb = hr.rearrange("p (h o) -> p h o", o=1).broadcast_to([128, 8, 64])
                return e.tensor_tensor(qfb.rearrange("p (h x) -> p h x", h=8), sq_s.rearrange("p (h x) -> p h x", h=8), rb, ALU.mult)
            mk.op("vector", rota, wait=[(sc, sc.n)], inc=[dv.inc()])
            t_rot = dv.n
            def tpq(e, qfb=qfb, sidx=sidx, i=i):
                last = None
                for b in range(4):
                    last = e.transpose(mk.ptp[:, (sidx * 4 + b) * 128:(sidx * 4 + b + 1) * 128], qfb[:, b * 128:(b + 1) * 128], C["eye16"][:])
                return last
            mk.op("tensor", tpq, wait=[(dv, t_rot)] + ([(ev, ev.n - 1)] if (i > 0 or sidx > 0) else []), inc=[mm.inc()])
            def evq(e, dstT=dstT, i=i, sidx=sidx):
                last = None
                for b in range(4):
                    last = e.tensor_copy(dstT[:, b * 2048 + i * 128:b * 2048 + (i + 1) * 128], mk.ptp[:, (sidx * 4 + b) * 128:(sidx * 4 + b + 1) * 128])
                return last
            mk.op("vector", evq, wait=[(mm, mm.n)], inc=[ev.inc()])
    mk.op("vector", (lambda e: e.memset(vtm[:].rearrange("p (i h c) -> p i h c", i=16, h=8)[:, :, :, 64:65], 1.0)), inc=[dv.inc()])
    t_proj = (mm.n, dv.n, mk.sems["p_ev"].n)

    nrow2 = hs[:, 8192:12288]
    t_p1r = [0]
    par_end = [mm.n, mm.n]
    t_nrow_tab = [0, 0]
    ybank2 = getattr(mk, "_l0f_ybank2", None) or [0, 0]
    mk._l0f_ybank2 = ybank2

    def head_geom(h):
        qsl = lambda a, b, h=h: qfT[(h % 2) * 64:(h % 2) * 64 + 64, (h // 2) * 2048 + a:(h // 2) * 2048 + b]
        ksl = lambda a, b, h=h: kfT[(h % 2) * 64:(h % 2) * 64 + 64, (h // 2) * 2048 + a:(h // 2) * 2048 + b]
        return qsl, ksl

    def mk_pass1_ops(h, first=False):
        qsl, ksl = head_geom(h)
        ops = []
        for i in range(TT):
            lo = max(0, i - 4) * 128
            wdt = (i + 1) * 128 - lo
            for c in range((wdt + 511) // 512):
                n = min(512, wdt - c * 512)
                def ep1(i=i, c=c, n=n, lo=lo, qsl=qsl, ksl=ksl, fw=(first and i == 0 and c == 0)):
                    def p1(e, i=i, c=c, n=n, lo=lo, qsl=qsl, ksl=ksl):
                        return e.matmul(mk.pb[6][:, 0:n], qsl(i * 128, (i + 1) * 128), ksl(lo + c * 512, lo + c * 512 + n),
                                        start=True, stop=True)
                    w = [(dv, t_proj[1]), (mk.sems["p_ev"], t_proj[2])] if fw else [(dv, t_p1r[0])]
                    mk.op("tensor", p1, wait=w, inc=[mm.inc()])
                def ep1r(h=h, i=i, c=c, n=n):
                    def p1r(e, h=h, i=i, c=c, n=n):
                        if c == 0:
                            return e.reduce_max(negc[:, h * 16 + i:h * 16 + i + 1], mk.pb[6][:, 0:n], axis=AX, negate=True)
                        last = e.reduce_max(hcol[:, 0:1], mk.pb[6][:, 0:n], axis=AX, negate=True)
                        mk.fen(e, last)
                        return e.tensor_tensor(negc[:, h * 16 + i:h * 16 + i + 1], negc[:, h * 16 + i:h * 16 + i + 1], hcol[:, 0:1], ALU.min)
                    mk.op("vector", p1r, wait=[(mm, mm.n)], inc=[dv.inc()])
                    t_p1r[0] = dv.n
                ops.append((ep1, ep1r))
        return ops

    def emit_nrow(h):
        par = h % 2
        def ntp(e, h=h):
            return e.transpose(mk.pb[4][0:16, 0:128], negc[:, h * 16:(h + 1) * 16], C["eye32"][:])
        mk.op("tensor", ntp, wait=[(dv, dv.n), (sc, sc.n)], inc=[mm.inc()])
        mk.op("vector", (lambda e, par=par: e.tensor_copy(nT[0:16, par * 128:par * 128 + 128], mk.pb[4][0:16, 0:128])),
              wait=[(mm, mm.n)], inc=[dv.inc()])
        nr_flat = mk.dram["sc_small"].rearrange("a b -> (a b)")
        sst = "st_nrow" + str(par); sld = "ld_nrow" + str(par)
        mk.store(nr_flat[par * 2048:par * 2048 + 2048].rearrange("(i x) -> i x", i=16), nT[0:16, par * 128:par * 128 + 128],
                 q="sync", sem=sst, serialize=True, wait=[(dv, dv.n), (mk.sem(sld), mk.sem(sld).n)])
        mk.load(nrow2[0:1, par * 2048:par * 2048 + 2048], nr_flat[par * 2048:par * 2048 + 2048].rearrange("(one t) -> one t", one=1),
                q="gpsimd", sem=sld, serialize=True, wait=[(mk.sem(sst), mk.sem(sst).n), (mm, par_end[par])])
        t_nrow_tab[par] = mk.sem(sld).n

    for (ep1, ep1r) in mk_pass1_ops(0, first=True):
        ep1(); ep1r()
    emit_nrow(0)

    for h in range(F_HEADS):
        nslot = nrow2[:, (h % 2) * 2048:(h % 2) * 2048 + 2048]
        t_nrow = t_nrow_tab[h % 2]
        nxt = mk_pass1_ops(h + 1) if h + 1 < F_HEADS else []
        nxi = [0]
        def pump():
            if nxi[0] < len(nxt):
                ep1, ep1r = nxt[nxi[0]]
                ep1(); ep1r()
                nxi[0] += 1
        def emit_ymm(i, t_dv, t_sc, h=h):
            pby = mk.pb[2 + i % 2][:, 0:65]
            jlo = max(0, i - 4)
            def ymm(e, h=h, i=i, pby=pby, jlo=jlo):
                last = None
                for j in range(jlo, i + 1):
                    last = e.matmul(pby, attw[:, off[j] + (i - j) * 128:off[j] + (i - j) * 128 + 128],
                                    vtm[:, j * 520 + h * 65:j * 520 + h * 65 + 65],
                                    start=(j == jlo), stop=(j == i))
                return last
            mk.op("tensor", ymm, wait=[(dv, t_dv), (sc, t_sc), (dv, ybank2[i % 2])], inc=[mm.inc()])
            def yev(e, h=h, i=i, pby=pby):
                mk.fen(e, e.reciprocal(hcol[:, 1:2], pby[:, 64:65]))
                return e.tensor_scalar(octx[:, i * 512 + h * 64:i * 512 + h * 64 + 64], pby[:, 0:64],
                                       hcol[:, 1:2], None, ALU.mult)
            mk.op("vector", yev, wait=[(mm, mm.n), (dv, dv.n)], inc=[dv.inc()])
            ybank2[i % 2] = dv.n
        pend = None
        for j in range(TT):
            t0 = j * 128
            t1 = min(j + 5, 16) * 128
            spans = []
            a = t0
            while a < t1:
                b = min(a + 512, t1)
                spans.append((a, b))
                a = b
            for (a, b) in spans:
                pbk = mk.pb[4 + (j % 2)]
                def p2(e, j=j, a=a, b=b, pbk=pbk, nslot=nslot, h=h):
                    qsl, ksl = head_geom(h)
                    e.matmul(pbk[:, 0:b - a], ksl(j * 128, (j + 1) * 128), qsl(a, b), start=True, stop=False)
                    return e.matmul(pbk[:, 0:b - a], C["ones_row"][0:1, 0:128], nslot[0:1, a:b], start=False, stop=True)
                mk.op("tensor", p2, wait=[(mk.sem("ld_nrow" + str(h % 2)), t_nrow), (sc, sc.n)], inc=[mm.inc()])
                def p2e(e, j=j, a=a, b=b, pbk=pbk, t0=t0):
                    return e.activation(attw[:, off[j] + (a - t0):off[j] + (b - t0)], pbk[:, 0:b - a], AF.Exp, scale=8.0)
                mk.op("scalar", p2e, wait=[(mm, mm.n)], inc=[sc.inc()])
                pump()
            def pmask(e, j=j, t0=t0, t1=t1):
                last = e.tensor_tensor(attw[:, off[j]:off[j] + 128], attw[:, off[j]:off[j] + 128], C["caus01_fm16"][:], ALU.mult)
                if j + 4 < 16:
                    last = e.tensor_tensor(attw[:, off[j] + 512:off[j] + 640], attw[:, off[j] + 512:off[j] + 640],
                                           C["anti01_fm16"][:], ALU.mult)
                return last
            mk.op("vector", pmask, wait=[(sc, sc.n)], inc=[dv.inc()])
            if pend is not None:
                emit_ymm(*pend)
            pend = (j, dv.n, sc.n)
        emit_ymm(*pend)
        while nxi[0] < len(nxt):
            pump()
        par_end[h % 2] = mm.n
        if h + 1 < F_HEADS:
            emit_nrow(h + 1)
    t_ctx = dv.n

    mk.load(attw[:, 0:4096], d["f_wo"][l].rearrange("(k p) o -> p k o", p=128), q="gpsimd", sem="ld_wo",
            serialize=True, wait=[(mm, mm.n)])
    t_wo = mk.sem("ld_wo").n
    for i in range(TT):
        def ctp(e, i=i):
            last = None
            for b in range(4):
                last = e.transpose(mk.ptp[:, b * 128:(b + 1) * 128], octx[:, i * 512 + b * 128:i * 512 + (b + 1) * 128], C["eye16"][:])
            return last
        w = [(dv, t_ctx)] if i == 0 else [(dv, dv.n)]
        mk.op("tensor", ctp, wait=w + ([(ev, ev.n)] if i > 0 else []), inc=[mm.inc()])
        oT = wbuf[:, (i % 2) * 1536:(i % 2) * 1536 + 512]
        mk.op("vector", (lambda e, oT=oT: e.tensor_copy(oT, mk.ptp[:, 0:512])), wait=[(mm, mm.n)], inc=[ev.inc()])
        def wmm(e, i=i, oT=oT):
            last = None
            for dc in range(2):
                for k in range(4):
                    last = e.matmul(mk.pb[2 + (i % 2) * 2 + dc][:, 0:512] if False else mk.pb[2 + dc][:, 0:512],
                                    oT[:, k * 128:(k + 1) * 128], attw[:, k * 1024 + dc * 512:k * 1024 + dc * 512 + 512],
                                    start=(k == 0), stop=(k == 3))
            return last
        mk.op("tensor", wmm, wait=[(mk.sem("ld_wo"), t_wo), (mk.sems["p_ev"], mk.sems["p_ev"].n)] + ([(dv, dv.n)] if i >= 1 else []), inc=[mm.inc()])
        xs = xt[:, (i % 2) * D:(i % 2 + 1) * D]
        mk.load(xs, xin_d[i * 128:(i + 1) * 128, :], q="sync", sem="ld_xr" + str(i % 2), serialize=True)
        os_ = qk32[:, (i % 2) * 1024:(i % 2) * 1024 + 1024]
        def radd(e, i=i, xs=xs, os_=os_):
            e.scalar_tensor_tensor(os_[:, 0:512], mk.pb[2][:, 0:512], 1.0, xs[:, 0:512], ALU.mult, ALU.add).then_inc(dv.h, 1)
            return e.scalar_tensor_tensor(os_[:, 512:1024], mk.pb[3][:, 0:512], 1.0, xs[:, 512:1024], ALU.mult, ALU.add)
        ta = dv.inc(); tb = dv.inc()
        mk.op("vector", radd, wait=[(mm, mm.n), (mk.sem("ld_xr" + str(i % 2)), mk.sem("ld_xr" + str(i % 2)).n)], inc=[tb])
        mk.store(xmid_d[i * 128:(i + 1) * 128, :], os_, q="sync", sem="st_xm", serialize=True, wait=[(dv, dv.n)])
    mk.bar()


def l0f_mlp_phase(mk, l, xmid_d, xout_d, xa_in_d=None):
    nc = mk.nc
    mk.prog.phase = "l0fm"
    mk.reset()
    d = mk.dram
    C = mk.consts
    hln = mk.c16(8192)
    h3 = mk.c16(22528)
    w2s = mk.c16(22528)
    w1c = mk.c16(2 * 2048)
    ab = mk.c16(2 * 1024)
    cb = ab[:, 0:1024]
    xt = mk.c32(2 * D)
    sq = mk.c32(2 * D)
    cols = mk.c32(8)
    lgb = mk.c32(2 * D)
    b1c = mk.c32(2 * 22)
    osb = mk.c32(2 * D)
    xab = mk.c32(2 * D)
    ld = mk.sem("ld_sync")
    mm = mk.sem("p_mm"); dv = mk.sem("p_dv"); sc = mk.sem("p_sc"); ev = mk.sem("p_ev")

    mk.load(w2s[:], d["f_w2"][l].rearrange("(k p) dd -> p k dd", p=128), q="gpsimd", sem="ld_w2", serialize=True)
    mk.load(b1c[:, 0:22], d["f_b1a"][l], q="sync", sem="ld_b1", serialize=True)
    mk.load(b1c[:, 22:44], d["f_b1g"][l], q="sync", sem="ld_b1")
    t_b1 = mk.sem("ld_b1").n
    t_w2 = mk.sem("ld_w2").n
    def lgouter(e, l=l):
        e.matmul(mk.pb[6][:, 0:512], C["ones_row"][0:1, 0:128], C["f_lng"][0:1, l * D:l * D + 512], start=True, stop=True)
        e.matmul(mk.pb[6][:, 0:512], C["ones_row"][0:1, 0:128], C["f_lng"][0:1, l * D:l * D + 512], start=True, stop=True) if False else None
        return e.matmul(mk.pb[5][:, 0:512], C["ones_row"][0:1, 0:128], C["f_lng"][0:1, l * D + 512:(l + 1) * D], start=True, stop=True)
    mk.op("tensor", lgouter, inc=[mm.inc()])
    mk.op("vector", (lambda e: e.tensor_copy(lgb[:, 0:512], mk.pb[6][:, 0:512])), wait=[(mm, mm.n)], inc=[dv.inc()])
    mk.op("vector", (lambda e: e.tensor_copy(lgb[:, 512:1024], mk.pb[5][:, 0:512])), wait=[(mm, mm.n)], inc=[dv.inc()])
    def lbouter(e, l=l):
        e.matmul(mk.pb[6][:, 0:512], C["ones_row"][0:1, 0:128], C["f_lnb"][0:1, l * D:l * D + 512], start=True, stop=True)
        return e.matmul(mk.pb[5][:, 0:512], C["ones_row"][0:1, 0:128], C["f_lnb"][0:1, l * D + 512:(l + 1) * D], start=True, stop=True)
    mk.op("tensor", lbouter, wait=[(dv, dv.n)], inc=[mm.inc()])
    mk.op("vector", (lambda e: e.tensor_copy(lgb[:, 1024:1536], mk.pb[6][:, 0:512])), wait=[(mm, mm.n)], inc=[dv.inc()])
    mk.op("vector", (lambda e: e.tensor_copy(lgb[:, 1536:2048], mk.pb[5][:, 0:512])), wait=[(mm, mm.n)], inc=[dv.inc()])
    t_lg = dv.n

    t_w1mm = [0, 0]
    t_norm = [0, 0]
    t_red = [0, 0]
    t_tpc = [0, 0]
    t_ms = [0]
    for half in range(2):
        tbase = half * 8
        for ii in range(8):
            i = tbase + ii
            x0 = xt[:, (ii % 2) * D:(ii % 2) * D + 512]
            x1 = xt[:, (ii % 2) * D + 512:(ii % 2) * D + 1024]
            sx = "ld_x" + str(ii % 2)
            wv = [(dv, t_norm[ii % 2]), (sc, t_red[ii % 2])] if (half > 0 or ii >= 2) else []
            mk.load(x0, xmid_d[i * 128:(i + 1) * 128, 0:512], q="sync", sem=sx, serialize=True, wait=wv)
            if i == 0:
                mk.load(x1[1:128, :], xmid_d[0:127, 512:1024], q="sync", sem=sx)
                mk.op("vector", (lambda e, x1=x1: e.memset(x1[0:1, :], 0.0)),
                      wait=[(mk.sem(sx), mk.sem(sx).n)], inc=[dv.inc()])
                t_ms[0] = dv.n
            else:
                mk.load(x1, xmid_d[i * 128 - 1:(i + 1) * 128 - 1, 512:1024], q="sync", sem=sx)
            t_x = mk.sem(sx).n
            xfull = xt[:, (ii % 2) * D:(ii % 2) * D + 1024]
            sqs_ = sq[:, (ii % 2) * D:(ii % 2) * D + 1024]
            def red(e, x=xfull, s=sqs_, c=cols):
                return e.activation(s, x, AF.Square)
            mk.op("scalar", red, wait=[(mk.sem(sx), t_x), (dv, max(t_norm[ii % 2], t_ms[0])), (sc, max(0, sc.n - 1))], inc=[sc.inc()])
            t_red[ii % 2] = sc.n
            def red2(e, x=xfull, s=sqs_, c=cols):
                e.reduce_sum(c[:, 0:1], x, axis=AX, negate=True).then_inc(dv.h, 1)
                return e.reduce_sum(c[:, 2:3], s, axis=AX)
            tk1 = dv.inc(); tk2 = dv.inc()
            mk.op("vector", red2, wait=[(sc, sc.n)], inc=[tk2])
            def smu(e, c=cols):
                e.activation(c[:, 1:2], c[:, 0:1], AF.Copy, scale=1.0 / D).then_inc(sc.h, 1)
                return e.activation(c[:, 3:4], c[:, 0:1], AF.Square, scale=1.0 / D)
            tk3 = sc.inc(); tk4 = sc.inc()
            mk.op("scalar", smu, wait=[(dv, dv.n)], inc=[tk4])
            def var(e, c=cols):
                return e.tensor_scalar(c[:, 4:5], c[:, 2:3], 1.0 / D, c[:, 3:4], ALU.mult, ALU.subtract)
            mk.op("vector", var, wait=[(sc, sc.n)], inc=[dv.inc()])
            def rstd(e, c=cols):
                mk.fen(e, e.activation(c[:, 5:6], c[:, 4:5], AF.Ln, bias=mk.fc(1e-5)))
                return e.activation(c[:, 6:7], c[:, 5:6], AF.Exp, scale=-0.5)
            mk.op("scalar", rstd, wait=[(dv, dv.n)], inc=[sc.inc()])
            cbs = ab[:, (ii % 2) * 1024:(ii % 2) * 1024 + 1024]
            def norm(e, x=xfull, c=cols, s=sqs_, cbs=cbs):
                mk.fen(e, e.tensor_scalar(s, x, c[:, 1:2], c[:, 6:7], ALU.add, ALU.mult))
                mk.fen(e, e.tensor_tensor(s, s, lgb[:, 0:1024], ALU.mult))
                return e.tensor_tensor(cbs[:], s, lgb[:, 1024:2048], ALU.add)
            mk.op("vector", norm, wait=[(sc, sc.n), (dv, t_lg), (mm, t_tpc[ii % 2])], inc=[dv.inc()])
            t_norm[ii % 2] = dv.n
            def tp(e, cbs=cbs):
                last = None
                for k in range(8):
                    last = e.transpose(mk.ptp[:, k * 128:(k + 1) * 128], cbs[:, k * 128:(k + 1) * 128], C["eye16"][:])
                return last
            mk.op("tensor", tp, wait=[(dv, dv.n)] + ([(ev, ev.n)] if (half > 0 or ii > 0) else []), inc=[mm.inc()])
            t_tpc[ii % 2] = mm.n
            def ev8(e, ii=ii):
                last = None
                for k in range(8):
                    last = e.tensor_copy(hln[:, k * 1024 + ii * 128:k * 1024 + (ii + 1) * 128], mk.ptp[:, k * 128:(k + 1) * 128])
                return last
            mk.op("vector", ev8, wait=[(mm, mm.n)], inc=[ev.inc()])
        t_hln = (mk.sems["p_ev"].n, dv.n)
        for hk in range(22):
            wsl = w1c[:, (hk % 2) * 2048:(hk % 2) * 2048 + 2048]
            wA = d["f_w1a"][l].rearrange("(k p) o -> p k o", p=128)[:, :, hk * 128:(hk + 1) * 128]
            wG = d["f_w1g"][l].rearrange("(k p) o -> p k o", p=128)[:, :, hk * 128:(hk + 1) * 128]
            wld = [(mm, t_w1mm[hk % 2])] if (half > 0 or hk >= 2) else []
            mk.load(wsl[:, 0:1024], wA, q="gpsimd", sem="ld_w1", serialize=True, wait=wld)
            mk.load(wsl[:, 1024:2048], wG, q="gpsimd", sem="ld_w1")
            t_w1 = mk.sem("ld_w1").n
            for tc in range(2):
                pa = mk.pb[tc % 2]; pg = mk.pb[2 + tc % 2]
                def w1mm(e, hk=hk, tc=tc, pa=pa, pg=pg, wsl=wsl):
                    last = None
                    for k in range(8):
                        last = e.matmul(pa[:], wsl[:, k * 128:(k + 1) * 128], hln[:, k * 1024 + tc * 512:k * 1024 + (tc + 1) * 512],
                                        start=(k == 0), stop=(k == 7))
                    for k in range(8):
                        last = e.matmul(pg[:], wsl[:, 1024 + k * 128:1024 + (k + 1) * 128], hln[:, k * 1024 + tc * 512:k * 1024 + (tc + 1) * 512],
                                        start=(k == 0), stop=(k == 7))
                    return last
                w = [(mk.sem("ld_w1"), t_w1), (mk.sems["p_ev"], t_hln[0])]
                if hk * 2 + tc >= 2:
                    w.append((dv, dv.n - 1))
                mk.op("tensor", w1mm, wait=w, inc=[mm.inc()])
                def gel(e, hk=hk, tc=tc, pg=pg):
                    return e.activation(ab[:, (tc % 2) * 1024 + 512:(tc % 2) * 1024 + 1024], pg[:], AF.Gelu,
                                        bias=b1c[:, 22 + hk:23 + hk])
                mk.op("scalar", gel, wait=[(mm, mm.n), (mk.sem("ld_b1"), t_b1)], inc=[sc.inc()])
                def aev(e, hk=hk, tc=tc, pa=pa):
                    mk.fen(e, e.tensor_scalar(ab[:, (tc % 2) * 1024:(tc % 2) * 1024 + 512], pa[:], b1c[:, hk:hk + 1], None, ALU.add))
                    return e.tensor_tensor(h3[:, hk * 1024 + tc * 512:hk * 1024 + (tc + 1) * 512],
                                           ab[:, (tc % 2) * 1024:(tc % 2) * 1024 + 512],
                                           ab[:, (tc % 2) * 1024 + 512:(tc % 2) * 1024 + 1024], ALU.mult)
                mk.op("vector", aev, wait=[(sc, sc.n), (mm, mm.n), (mk.sem("ld_b1"), t_b1)], inc=[dv.inc()])
            t_w1mm[hk % 2] = mm.n
        t_h3 = dv.n
        for ii in range(8):
            i = tbase + ii
            for dc in range(2):
                py = mk.pb[4 + (ii * 2 + dc) % 2]
                def w2mm(e, ii=ii, dc=dc, py=py, l=l):
                    last = e.matmul(py[:], C["ones_row"][0:1, 0:128], C["f_b2"][0:1, l * D + dc * 512:l * D + (dc + 1) * 512],
                                    start=True, stop=False)
                    for k in range(22):
                        last = e.matmul(py[:], h3[:, k * 1024 + ii * 128:k * 1024 + (ii + 1) * 128],
                                        w2s[:, k * 1024 + dc * 512:k * 1024 + (dc + 1) * 512],
                                        start=False, stop=(k == 21))
                    return last
                w = [(mk.sem("ld_w2"), t_w2), (dv, t_h3)]
                if ii * 2 + dc >= 2:
                    w.append((ev, ev.n - 1))
                mk.op("tensor", w2mm, wait=w, inc=[mm.inc()])
                xs = xt[:, (ii % 2) * D + dc * 512:(ii % 2) * D + (dc + 1) * 512]
                sxr = "ld_xr" + str(ii % 2)
                mk.load(xs, xmid_d[i * 128:(i + 1) * 128, dc * 512:(dc + 1) * 512], q="sync", sem=sxr, serialize=True,
                        wait=[(dv, dv.n)])
                if xa_in_d is not None:
                    xa = xab[:, (ii % 2) * D + dc * 512:(ii % 2) * D + (dc + 1) * 512]
                    mk.load(xa, xa_in_d[i * 128:(i + 1) * 128, dc * 512:(dc + 1) * 512], q="sync", sem=sxr)
                os_ = osb[:, (ii % 2) * D + dc * 512:(ii % 2) * D + (dc + 1) * 512]
                def radd(e, xs=xs, os_=os_, py=py, ii=ii, dc=dc):
                    r = e.scalar_tensor_tensor(os_, py[:], 1.0, xs, ALU.mult, ALU.add)
                    if xa_in_d is not None:
                        xa = xab[:, (ii % 2) * D + dc * 512:(ii % 2) * D + (dc + 1) * 512]
                        mk.fen(e, r)
                        r = e.tensor_tensor(os_, os_, xa, ALU.add)
                    return r
                mk.op("vector", radd, wait=[(mm, mm.n), (mk.sem("ld_xr" + str(ii % 2)), mk.sem("ld_xr" + str(ii % 2)).n)], inc=[ev.inc()])
                mk.store(xout_d[i * 128:(i + 1) * 128, dc * 512:(dc + 1) * 512], os_, q="sync", sem="st_xo",
                         serialize=True, wait=[(ev, ev.n)])
    mk.bar()


def l0a_prep_phase(mk, xnfm_d):
    nc = mk.nc
    mk.prog.phase = "l0ap"
    mk.reset()
    d = mk.dram
    C = mk.consts
    xnl = mk.c16(16384)
    wrb = mk.c16(2 * 1024)
    lam_tm = mk.c16(8192)
    cb16 = mk.c16(2048)
    vst = mk.c16(2 * 128)
    xpad = mk.c32(2051)
    cacc = mk.c32(2048)
    csc2 = mk.c32(2048)
    dtg = mk.c32(3 * 2048)
    cstage = mk.c32(2 * 512)
    dtm = mk.c32(512)
    mcols = mk.c32(16)
    ld = mk.sem("ld_sync"); ldg = mk.sem("ld_gpsimd")
    mm = mk.sem("p_mm"); dv = mk.sem("p_dv"); sc = mk.sem("p_sc"); ev = mk.sem("p_ev")

    mk.load(xnl[:], xnfm_d.rearrange("(k p) t -> p k t", p=128), q="sync")
    t_xn = ld.n
    DT = dtg[0:32, 0:2048]
    GC = dtg[0:32, 2048:4096]
    SC3 = dtg[0:32, 4096:6144]

    t_rmm = [0, 0]
    def rest_tile(o):
        wsl = wrb[:, (o % 2) * 1024:(o % 2) * 1024 + 1024]
        mk.load(wsl, d["a_wr"].rearrange("(k p) o -> p k o", p=128)[:, :, o * 128:(o + 1) * 128],
                q="gpsimd", sem="ld_wr", serialize=True, wait=[(mm, t_rmm[o % 2])])
        t_w = mk.sem("ld_wr").n
        slot = xpad[:, 0:2051]
        for tc in range(4):
            pbk = mk.pb[tc % 2]
            def rmm(e, tc=tc, pbk=pbk, wsl=wsl):
                last = None
                for k in range(8):
                    last = e.matmul(pbk[:], wsl[:, k * 128:(k + 1) * 128],
                                    xnl[:, k * 2048 + tc * 512:k * 2048 + (tc + 1) * 512],
                                    start=(k == 0), stop=(k == 7))
                return last
            w = [(mk.sem("ld_wr"), t_w), (ld, t_xn), (dv, dv.n)]
            mk.op("tensor", rmm, wait=w, inc=[mm.inc()])
            t_rmm[o % 2] = mm.n
            def bev(e, o=o, tc=tc, pbk=pbk, slot=slot):
                return e.tensor_scalar(slot[:, 3 + tc * 512:3 + (tc + 1) * 512], pbk[:],
                                       C["a_br32"][:, o:o + 1], None, ALU.add)
            mk.op("vector", bev, wait=[(mm, mm.n)], inc=[dv.inc()])
        mk.op("vector", (lambda e, slot=slot: e.memset(slot[:, 0:3], 0.0)), inc=[dv.inc()])
        return slot, dv.n

    def softplus(e, dst, x, scr):
        mk.fen(e, e.tensor_scalar(scr, x, 0.0, None, ALU.abs_max))
        return scr

    lam_done = []
    for o in range(18, 22):
        slot, t_slot = rest_tile(o)
        if o == 18:
            def dtp(e, slot=slot):
                return e.tensor_scalar(SC3, slot[0:32, 3:2051], C["a_dtb"][:, 0:1], None, ALU.add)
            mk.op("vector", dtp, wait=[(dv, t_slot)], inc=[dv.inc()])
            def dte(e):
                mk.fen(e, e.activation(DT, SC3, AF.Abs))
                return e.activation(GC, DT, AF.Exp, scale=-1.0)
            mk.op("scalar", dte, wait=[(dv, dv.n)], inc=[sc.inc()])
            def dtl(e):
                return e.activation(GC, GC, AF.Ln, bias=mk.fc(1.0, 32))
            mk.op("scalar", dtl, wait=[(sc, sc.n)], inc=[sc.inc()])
            def dtf(e, slot=slot):
                mk.fen(e, e.tensor_scalar(DT, SC3, 0.0, None, ALU.max))
                return e.tensor_tensor(DT, DT, GC, ALU.add)
            mk.op("vector", dtf, wait=[(sc, sc.n)], inc=[dv.inc()])
            t_dt = dv.n
        def lm1(e, o=o, slot=slot):
            return e.tensor_scalar(cacc[:], slot[:, 3:2051], C["a_L"][:, o - 18:o - 17], None, ALU.mult)
        mk.op("vector", lm1, wait=[(dv, dv.n), (sc, sc.n)], inc=[dv.inc()])
        def lm2(e):
            mk.fen(e, e.activation(csc2[:], cacc[:], AF.Abs))
            mk.fen(e, e.activation(csc2[:], csc2[:], AF.Exp, scale=-1.0))
            return e.activation(csc2[:], csc2[:], AF.Ln, bias=mk.fc(1.0))
        mk.op("scalar", lm2, wait=[(dv, dv.n)], inc=[sc.inc()])
        def lm3(e):
            mk.fen(e, e.tensor_scalar(cacc[:], cacc[:], 0.0, None, ALU.max))
            mk.fen(e, e.tensor_tensor(cacc[:], cacc[:], csc2[:], ALU.add))
            return e.tensor_copy(cb16[:], cacc[:])
        mk.op("vector", lm3, wait=[(sc, sc.n)], inc=[dv.inc()])
        def lmt(e, o=o):
            last = None
            for i in range(16):
                last = e.transpose(mk.ptp[:, (i % 8) * 128:(i % 8) * 128 + 128], cb16[:, i * 128:(i + 1) * 128], C["eye16"][:])
                if i % 8 == 7:
                    pass
            return last
        for wv in range(2):
            def lmtw(e, o=o, wv=wv):
                last = None
                for i2 in range(8):
                    i = wv * 8 + i2
                    last = e.transpose(mk.ptp[:, i2 * 128:(i2 + 1) * 128], cb16[:, i * 128:(i + 1) * 128], C["eye16"][:])
                return last
            mk.op("tensor", lmtw, wait=[(dv, dv.n)] + ([(ev, ev.n)] if (o > 18 or wv > 0) else []), inc=[mm.inc()])
            def lmev(e, o=o, wv=wv):
                last = None
                for i2 in range(8):
                    i = wv * 8 + i2
                    last = e.tensor_copy(lam_tm[:, i * 512 + (o - 18) * 128:i * 512 + (o - 18) * 128 + 128],
                                         mk.ptp[:, i2 * 128:(i2 + 1) * 128])
                return last
            mk.op("vector", lmev, wait=[(mm, mm.n)], inc=[ev.inc()])
    t_lam = mk.sems["p_ev"].n
    mk.store(d["sc_lam"], lam_tm[:], q="sync", sem="st_lam", wait=[(mk.sems["p_ev"], t_lam)])

    def gmk(e):
        mk.fen(e, e.tensor_scalar(SC3, DT, C["a_negA"][:, 0:1], None, ALU.mult))
        mk.fen(e, e.tensor_tensor_scan(GC, SC3, SC3, 0.0, ALU.add, ALU.bypass))
        mk.fen(e, e.reciprocal(SC3, DT))
        return e.tensor_scalar(SC3, SC3, C["a_D"][:, 0:1], None, ALU.mult)
    mk.op("vector", gmk, wait=[(dv, t_dt), (mk.sems["p_ev"], t_lam)], inc=[dv.inc()])
    t_gc = dv.n
    mk.store(d["sc_gc"], GC, q="sync", sem="st_gc", wait=[(dv, t_gc)])
    for si, (srcm, dsto) in enumerate(((GC, 4096), (SC3, 4608), (DT, 5120))):
        slot = cstage[:, (si % 2) * 512:(si % 2) * 512 + 512]
        ident = C["eye32"][0:32, 0:32]
        for i in range(TT):
            pf = mk.pb[5][:, (i % 2) * 128:(i % 2) * 128 + 32]
            def tpf(e, srcm=srcm, i=i, pf=pf, ident=ident):
                return e.transpose(pf, srcm[:, i * 128:(i + 1) * 128], ident)
            mk.op("tensor", tpf, wait=[(dv, dv.n)], inc=[mm.inc()])
            def evf(e, slot=slot, i=i, pf=pf, srcm=srcm):
                if srcm is DT:
                    return e.tensor_copy(dtm[:, i * 32:(i + 1) * 32], pf)
                return e.tensor_copy(slot[:, i * 32:(i + 1) * 32], pf)
            w4 = [(mm, mm.n)]
            if i == 0 and si >= 2:
                w4.append((mk.sem("st_cT"), mk.sem("st_cT").n))
            mk.op("vector", evf, wait=w4, inc=[dv.inc()])
        mk.store(d["sc_cols"][:, dsto:dsto + 512], dtm[:] if srcm is DT else slot, q="sync", sem="st_cT",
                 serialize=True, wait=[(dv, dv.n)])
    for eL in range(4):
        C_ = 128 << eL
        nb = 2048 // C_
        gcv = GC.rearrange("p (b c) -> p b c", c=C_)
        def cA(e, eL=eL, C_=C_, nb=nb, gcv=gcv):
            mx = GC[:, C_ - 1::C_].rearrange("p (b o) -> p b o", o=1).broadcast_to([32, nb, C_])
            return e.tensor_tensor(cacc[0:32, :].rearrange("p (b c) -> p b c", c=C_), mx, gcv, ALU.subtract)
        mk.op("vector", cA, wait=[(dv, dv.n), (sc, sc.n)], inc=[dv.inc()])
        mk.op("scalar", (lambda e: e.activation(cacc[0:32, :], cacc[0:32, :], AF.Exp)), wait=[(dv, dv.n)], inc=[sc.inc()])
        for i in range(TT):
            pf = mk.pb[5][:, (i % 2) * 128:(i % 2) * 128 + 32]
            mk.op("tensor", (lambda e, i=i, pf=pf: e.transpose(pf, cacc[0:32, i * 128:(i + 1) * 128], C["eye32"][0:32, 0:32])),
                  wait=[(sc, sc.n), (dv, dv.n)], inc=[mm.inc()])
            w5 = [(mm, mm.n)]
            if i == 0:
                w5.append((mk.sem("st_cT"), mk.sem("st_cT").n))
            mk.op("vector", (lambda e, i=i, pf=pf: e.tensor_copy(cstage[:, i * 32:(i + 1) * 32], pf)),
                  wait=w5, inc=[dv.inc()])
        mk.store(d["sc_cols"][:, eL * 512:(eL + 1) * 512], cstage[:, 0:512], q="sync", sem="st_cT",
                 serialize=True, wait=[(dv, dv.n)])
        def cF(e, eL=eL, C_=C_, nb=nb, gcv=gcv):
            mk.fen(e, e.tensor_copy(mcols[0:32, 0:1], GC[:, 0:1]))
            if nb > 1:
                mk.fen(e, e.tensor_copy(mcols[0:32, 1:nb], GC[:, C_ - 1:2048 - C_:C_]))
            mx = mcols[0:32, 0:nb].rearrange("p (b o) -> p b o", o=1).broadcast_to([32, nb, C_])
            return e.tensor_tensor(csc2[0:32, :].rearrange("p (b c) -> p b c", c=C_), gcv, mx, ALU.subtract)
        mk.op("vector", cF, wait=[(dv, dv.n), (sc, sc.n)], inc=[dv.inc()])
        mk.op("scalar", (lambda e: e.activation(csc2[0:32, :], csc2[0:32, :], AF.Exp)), wait=[(dv, dv.n)], inc=[sc.inc()])
        for i in range(TT):
            pf = mk.pb[5][:, (i % 2) * 128:(i % 2) * 128 + 32]
            mk.op("tensor", (lambda e, i=i, pf=pf: e.transpose(pf, csc2[0:32, i * 128:(i + 1) * 128], C["eye32"][0:32, 0:32])),
                  wait=[(sc, sc.n), (dv, dv.n)], inc=[mm.inc()])
            w6 = [(mm, mm.n)]
            if i == 0:
                w6.append((mk.sem("st_cT"), mk.sem("st_cT").n))
            mk.op("vector", (lambda e, i=i, pf=pf: e.tensor_copy(cstage[:, 512 + i * 32:512 + (i + 1) * 32], pf)),
                  wait=w6, inc=[dv.inc()])
        def cFl(e, eL=eL):
            lamap = lam_tm[:].rearrange("p (i c) -> p i c", c=512)[:, :, 32 + 8 + eL:32 + 8 + eL + 15 * 31 + 1:15]
            mk.fen(e, e.tensor_copy(cstage[:, 512:1024], cstage[:, 512:1024])) if False else None
            return e.tensor_tensor(cstage[:, 512:1024].rearrange("p (i h) -> p i h", h=32),
                                   cstage[:, 512:1024].rearrange("p (i h) -> p i h", h=32), lamap, ALU.mult)
        mk.op("vector", cFl, wait=[(dv, dv.n), (mk.sems["p_ev"], t_lam)], inc=[dv.inc()])
        mk.store(d["sc_cols"][:, 2048 + eL * 512:2048 + (eL + 1) * 512], cstage[:, 512:1024], q="sync",
                 sem="st_cT", serialize=True, wait=[(dv, dv.n)])
    t_cols = dv.n

    for o in (16, 17):
        slot, t_slot = rest_tile(o)
        def conv(e, o=o, slot=slot):
            wc = C["a_convc"]
            mk.fen(e, e.tensor_scalar(cacc[:], slot[:, 0:2048], wc[:, o * 4:o * 4 + 1], None, ALU.mult))
            mk.fen(e, e.scalar_tensor_tensor(cacc[:], slot[:, 1:2049], wc[:, o * 4 + 1:o * 4 + 2], cacc[:], ALU.mult, ALU.add))
            mk.fen(e, e.scalar_tensor_tensor(cacc[:], slot[:, 2:2050], wc[:, o * 4 + 2:o * 4 + 3], cacc[:], ALU.mult, ALU.add))
            return e.scalar_tensor_tensor(cacc[:], slot[:, 3:2051], wc[:, o * 4 + 3:o * 4 + 4], cacc[:], ALU.mult, ALU.add)
        mk.op("vector", conv, wait=[(dv, t_slot), (sc, sc.n), (mk.sem("st_cT"), mk.sem("st_cT").n)], inc=[dv.inc()])
        mk.op("scalar", (lambda e: e.activation(csc2[:], cacc[:], AF.Silu)), wait=[(dv, dv.n)], inc=[sc.inc()])
        mk.op("vector", (lambda e: e.tensor_copy(cb16[:], csc2[:])),
              wait=[(sc, sc.n), (mk.sem("st_bc"), mk.sem("st_bc").n)], inc=[dv.inc()])
        mk.store(d["sc_B" if o == 16 else "sc_C"], cb16[:], q="sync", sem="st_bc", serialize=True, wait=[(dv, dv.n)])

    for o in range(16):
        slot, t_slot = rest_tile(o)
        def conv(e, o=o, slot=slot):
            wc = C["a_convc"]
            mk.fen(e, e.tensor_scalar(cacc[:], slot[:, 0:2048], wc[:, o * 4:o * 4 + 1], None, ALU.mult))
            mk.fen(e, e.scalar_tensor_tensor(cacc[:], slot[:, 1:2049], wc[:, o * 4 + 1:o * 4 + 2], cacc[:], ALU.mult, ALU.add))
            mk.fen(e, e.scalar_tensor_tensor(cacc[:], slot[:, 2:2050], wc[:, o * 4 + 2:o * 4 + 3], cacc[:], ALU.mult, ALU.add))
            return e.scalar_tensor_tensor(cacc[:], slot[:, 3:2051], wc[:, o * 4 + 3:o * 4 + 4], cacc[:], ALU.mult, ALU.add)
        wv = [(dv, t_slot), (sc, sc.n)]
        if o == 0:
            wv.append((mk.sem("st_bc"), mk.sem("st_bc").n))
        mk.op("vector", conv, wait=wv, inc=[dv.inc()])
        mk.op("scalar", (lambda e: e.activation(csc2[:], cacc[:], AF.Silu)), wait=[(dv, dv.n), (mm, mm.n)], inc=[sc.inc()])
        t_silu = sc.n
        for i in range(TT):
            pf = mk.pb[6][:, (i % 4) * 128:(i % 4) * 128 + 128]
            def tpv(e, i=i, pf=pf):
                return e.transpose(pf, csc2[:, i * 128:(i + 1) * 128], C["eye32"][:])
            w2 = [(sc, t_silu), (dv, dv.n)]
            mk.op("tensor", tpv, wait=w2, inc=[mm.inc()])
            vsl = vst[:, (i % 2) * 128:(i % 2) * 128 + 128]
            def vev(e, o=o, i=i, pf=pf, vsl=vsl):
                mk.fen(e, e.tensor_scalar(vsl[:, 0:64], pf[:, 0:64], dtm[:, i * 32 + 2 * o:i * 32 + 2 * o + 1], None, ALU.mult))
                return e.tensor_scalar(vsl[:, 64:128], pf[:, 64:128], dtm[:, i * 32 + 2 * o + 1:i * 32 + 2 * o + 2], None, ALU.mult)
            w3 = [(mm, mm.n), (dv, t_cols)]
            if mk.sem("st_v").n > 0:
                w3.append((mk.sem("st_v"), mk.sem("st_v").n))
            mk.op("vector", vev, wait=w3, inc=[dv.inc()])
            mk.store(d["sc_v"][i * 128:(i + 1) * 128, o * 128:(o + 1) * 128], vsl, q="sync", sem="st_v",
                     serialize=True, wait=[(dv, dv.n)])
    mk.bar()


def l0a_core_phase(mk):
    nc = mk.nc
    mk.prog.phase = "l0ac"
    mk.reset()
    d = mk.dram
    C = mk.consts
    vtm = mk.c16(32768)
    qk = mk.c16(2 * 2048)
    lamsl = mk.c16(16 * 512)
    LT = mk.c16(4096)
    LF = mk.c16(4096)
    asc = mk.c16(16 * 128)
    bsb = mk.c16(2048)
    csb = mk.c16(2048)
    colsT = mk.c32(5632)
    ebc = mk.c32(8 * 512)
    e32 = mk.c32(4 * 128)
    yac = mk.c32(2048)
    ld = mk.sem("ld_sync"); ldg = mk.sem("ld_gpsimd")
    mm = mk.sem("p_mm"); dv = mk.sem("p_dv"); sc = mk.sem("p_sc"); ev = mk.sem("p_ev")
    mk.load(vtm[:], d["sc_v"].rearrange("(i p) c -> p i c", p=128), q="sync")
    mk.load(bsb[:], d["sc_B"], q="sync")
    mk.load(csb[:], d["sc_C"], q="sync")
    mk.load(colsT[:], d["sc_cols"][:, 0:5632], q="sync")
    t_ld0 = ld.n
    mk.load(lamsl[:], d["sc_lam"][:, 0:16 * 512], q="gpsimd", sem="ld_lam")
    t_lam0 = mk.sem("ld_lam").n

    GCL = 4096; DDL = 4608
    slot_mm = getattr(mk, "_l0ac_slot_mm", None) or [0] * 16
    bank_free = getattr(mk, "_l0ac_bank_free", None) or [0] * 5
    mk._l0ac_slot_mm = slot_mm
    mk._l0ac_bank_free = bank_free
    qk_free = getattr(mk, "_l0ac_qk_free", None) or [0, 0]
    mk._l0ac_qk_free = qk_free
    gb_free = getattr(mk, "_l0ac_gb_free", None) or [0] * 8
    mk._l0ac_gb_free = gb_free
    t_qk_tab = getattr(mk, "_l0ac_t_qk", None) or [0, 0]
    mk._l0ac_t_qk = t_qk_tab

    t_ltp_prev = [getattr(mk, "_l0ac_t_ltp", 0)]
    sl = getattr(mk, "_l0ac_sl", 0)

    def emit_qk_season(si):
        par = si % 2
        for j in range(si + 1):
            pbk = mk.pb[4 + j % 2]
            def qmm(e, si=si, j=j, pbk=pbk):
                return e.matmul(pbk[:, 0:128], bsb[:, j * 128:(j + 1) * 128], csb[:, si * 128:(si + 1) * 128],
                                start=True, stop=True)
            w = [(ld, t_ld0), (sc, max(0, sc.n - 1))]
            mk.op("tensor", qmm, wait=w, inc=[mm.inc()])
            def qev(e, par=par, j=j, pbk=pbk):
                return e.activation(qk[:, par * 2048 + j * 128:par * 2048 + (j + 1) * 128], pbk[:, 0:128], AF.Copy)
            mk.op("scalar", qev, wait=[(mm, mm.n), (dv, qk_free[par])], inc=[sc.inc()])
        t_qk_tab[par] = sc.n

    emit_qk_season(0)
    for i in range(TT):
        qkb = qk[:, (i % 2) * 2048:(i % 2) * 2048 + 2048]
        t_gb_tab = []
        for g in range(8):
            hb_g = ebc[:, g * 512:(g + 1) * 512]
            gsrc = d["sc_gc"][4 * g:4 * g + 4, i * 128:(i + 1) * 128].rearrange("(o g) t -> o g t", o=1).broadcast_to([128, 4, 128])
            mk.load(hb_g.rearrange("p (g t) -> p g t", g=4), gsrc, q="gpsimd", sem="ld_gb", serialize=True,
                    wait=[(sc, gb_free[g])])
            t_gb_tab.append(mk.sem("ld_gb").n)
        lsl = lamsl[:, i * 512:(i + 1) * 512]
        def mk_lbuild(h0, h1):
            def lbuild(e, i=i, lsl=lsl, h0=h0, h1=h1):
                fen = C["fen_tm"]
                nh = h1 - h0
                dst = LT[:, h0 * 128:h1 * 128].rearrange("p (h s) -> p h s", s=128)
                tmp = LF[:, h0 * 128:h1 * 128].rearrange("p (h s) -> p h s", s=128)
                last = None
                for eL in range(8):
                    mblk = fen[:, eL * 128:(eL + 1) * 128].rearrange("p (o s) -> p o s", o=1).broadcast_to([128, nh, 128])
                    lamb = lsl[:, 32 + eL + 15 * h0:32 + eL + 15 * (h1 - 1) + 1:15].rearrange("p (h o) -> p h o", o=1).broadcast_to([128, nh, 128])
                    if eL == 0:
                        last = e.tensor_tensor(dst, mblk, lamb, ALU.mult)
                    else:
                        mk.fen(e, last)
                        t1 = e.tensor_tensor(tmp, mblk, lamb, ALU.mult)
                        mk.fen(e, t1)
                        last = e.tensor_tensor(dst, dst, tmp, ALU.add)
                return last
            return lbuild
        mk.op("vector", mk_lbuild(0, 18), wait=[(mk.sem("ld_lam"), t_lam0), (mm, t_ltp_prev[0])], inc=[dv.inc()])
        gpb = mk.sem("gp_lb")
        mk.op("gpsimd", mk_lbuild(18, 32), wait=[(mk.sem("ld_lam"), t_lam0), (mm, t_ltp_prev[0]),
                                                 (dv, qk_free[(i + 1) % 2])], inc=[gpb.inc()])
        lf_q = []
        for wv2 in range(4):
            def ltp(e, wv2=wv2):
                last = None
                for h2 in range(8):
                    h = wv2 * 8 + h2
                    last = e.transpose(mk.ptp[:, h2 * 128:(h2 + 1) * 128], LT[:, h * 128:(h + 1) * 128], C["eye16"][:])
                return last
            wl = [(dv, dv.n)] + ([(gpb, gpb.n)] if wv2 >= 2 else [])
            mk.op("tensor", ltp, wait=wl + ([(ev, ev.n)] if (i > 0 or wv2 > 0) else []), inc=[mm.inc()])
            def lev(e, wv2=wv2):
                last = None
                for h2 in range(8):
                    h = wv2 * 8 + h2
                    last = e.activation(LF[:, h * 128:(h + 1) * 128], mk.ptp[:, h2 * 128:(h2 + 1) * 128], AF.Copy)
                return last
            mk.op("scalar", lev, wait=[(mm, mm.n)], inc=[ev.inc()])
            lf_q.append(ev.n)
        t_ltp_prev[0] = mm.n
        groups = [eL for eL in range(4) if (i >> eL) & 1]
        pending = None
        first_ab = True
        for h in range(A_H):
            g = h // 4
            hb = ebc[:, g * 512:(g + 1) * 512]
            t_gb = t_gb_tab[g]
            built = []
            for gi, eL in enumerate(groups):
                j0 = (i >> (eL + 1)) << (eL + 1)
                njt = 1 << eL
                if (sl % 16) + njt > 16:
                    sl += 16 - (sl % 16)
                s0 = sl % 16
                sl += njt
                js = []
                for q2 in range(njt):
                    j = j0 + q2
                    st_ = (s0 + q2) % 16
                    a_sl = asc[:, st_ * 128:(st_ + 1) * 128]
                    def abuild(e, j=j, eL=eL, a_sl=a_sl, h=h, qkb=qkb):
                        cc = colsT[:, eL * 512 + j * 32 + h:eL * 512 + j * 32 + h + 1]
                        return e.tensor_scalar(a_sl, qkb[:, j * 128:(j + 1) * 128], cc, None, ALU.mult)
                    w = [(mm, slot_mm[st_])]
                    if first_ab:
                        w += [(sc, t_qk_tab[i % 2]), (ld, t_ld0)]
                        first_ab = False
                    mk.op("vector", abuild, wait=w, inc=[dv.inc()])
                    js.append((j, a_sl, st_, dv.n))
                built.append((gi, eL, js))
            s_ = sl % 16
            sl += 1
            a_d = asc[:, s_ * 128:(s_ + 1) * 128]
            def eexp1(e, h=h, i=i, hb=hb):
                gcc = colsT[:, GCL + i * 32 + h:GCL + i * 32 + h + 1]
                return e.activation(e32[:, (h % 4) * 128:(h % 4) * 128 + 128], hb[:, (h % 4) * 128:(h % 4) * 128 + 128],
                                    AF.Relu, bias=gcc, scale=-1.0)
            wb = [(mk.sem("ld_gb"), t_gb)]
            if i == 0 and h == 0:
                wb.append((ld, t_ld0))
            mk.op("scalar", eexp1, wait=wb, inc=[sc.inc()])
            if h % 4 == 3:
                gb_free[g] = sc.n
            def eexp(e, h=h, s_=s_, a_d=a_d):
                return e.activation(a_d, e32[:, (h % 4) * 128:(h % 4) * 128 + 128], AF.Exp, scale=-1.0)
            mk.op("scalar", eexp, wait=[(sc, sc.n), (mm, slot_mm[s_])], inc=[sc.inc()])
            def dmul(e, h=h, i=i, a_d=a_d, qkb=qkb):
                mk.fen(e, e.tensor_tensor(a_d, a_d, qkb[:, i * 128:(i + 1) * 128], ALU.mult))
                return e.tensor_tensor(a_d, a_d, LF[:, h * 128:(h + 1) * 128], ALU.mult)
            mk.op("vector", dmul, wait=[(sc, sc.n), (ev, lf_q[h // 8])], inc=[dv.inc()])
            t_dmul = dv.n
            if pending is not None:
                ph, pbuilt, pdiag = pending
                for (gi, eL, js) in pbuilt:
                    pb_ = mk.pb[gi][:, 0:64]
                    def comb(e, h=ph, i=i, eL=eL, pb_=pb_, first=(gi == 0)):
                        s1 = colsT[:, 2048 + eL * 512 + i * 32 + h:2048 + eL * 512 + i * 32 + h + 1]
                        dst = yac[:, h * 64:(h + 1) * 64]
                        if first:
                            return e.tensor_scalar(dst, pb_, s1, None, ALU.mult)
                        return e.scalar_tensor_tensor(dst, pb_, s1, dst, ALU.mult, ALU.add)
                    wcb = [(mm, pdiag[2]), (dv, dv.n)]
                    if gi == 0:
                        wcb.append((mk.sem("st_y"), mk.sem("st_y").n))
                    mk.op("vector", comb, wait=wcb, inc=[dv.inc()])
                    bank_free[gi] = dv.n
                nbd = pdiag[0]
                pbd = mk.pb[nbd][:, 0:64]
                def yfin(e, h=ph, i=i, pbd=pbd, first=(pdiag[1] == 0)):
                    dst = yac[:, h * 64:(h + 1) * 64]
                    if first:
                        mk.fen(e, e.tensor_copy(dst, pbd))
                    else:
                        mk.fen(e, e.tensor_tensor(dst, dst, pbd, ALU.add))
                    ddt = colsT[:, DDL + i * 32 + h:DDL + i * 32 + h + 1]
                    return e.scalar_tensor_tensor(dst, vtm[:, i * 2048 + h * 64:i * 2048 + h * 64 + 64], ddt, dst, ALU.mult, ALU.add)
                wyf = [(mm, pdiag[2]), (dv, dv.n)]
                if pdiag[1] == 0:
                    wyf.append((mk.sem("st_y"), mk.sem("st_y").n))
                mk.op("vector", yfin, wait=wyf, inc=[dv.inc()])
                bank_free[nbd] = dv.n
            for (gi, eL, js) in built:
                pb_ = mk.pb[gi][:, 0:64]
                for k2, (j, a_s, s_2, t_ab) in enumerate(js):
                    def pmm(e, h=h, j=j, a_s=a_s, pb_=pb_, st=(k2 == 0), sp=(k2 == len(js) - 1)):
                        return e.matmul(pb_, a_s, vtm[:, j * 2048 + h * 64:j * 2048 + h * 64 + 64],
                                        start=st, stop=sp)
                    w = [(dv, t_ab)]
                    if k2 == 0:
                        w.append((dv, bank_free[gi]))
                    mk.op("tensor", pmm, wait=w, inc=[mm.inc()])
                    slot_mm[s_2] = mm.n
            nbd = len(built)
            pbd = mk.pb[nbd][:, 0:64]
            def dpmm2(e, h=h, i=i, a_d=a_d, pbd=pbd):
                return e.matmul(pbd, a_d, vtm[:, i * 2048 + h * 64:i * 2048 + h * 64 + 64], start=True, stop=True)
            mk.op("tensor", dpmm2, wait=[(dv, t_dmul), (dv, bank_free[nbd])], inc=[mm.inc()])
            slot_mm[s_] = mm.n
            pending = (h, built, (nbd, len(built), mm.n))
            if h == 16 and i + 1 < TT:
                emit_qk_season(i + 1)
        qk_free[i % 2] = dv.n
        ph, pbuilt, pdiag = pending
        for (gi, eL, js) in pbuilt:
            pb_ = mk.pb[gi][:, 0:64]
            def comb(e, h=ph, i=i, eL=eL, pb_=pb_, first=(gi == 0)):
                s1 = colsT[:, 2048 + eL * 512 + i * 32 + h:2048 + eL * 512 + i * 32 + h + 1]
                dst = yac[:, h * 64:(h + 1) * 64]
                if first:
                    return e.tensor_scalar(dst, pb_, s1, None, ALU.mult)
                return e.scalar_tensor_tensor(dst, pb_, s1, dst, ALU.mult, ALU.add)
            wcb = [(mm, pdiag[2]), (dv, dv.n)]
            if gi == 0:
                wcb.append((mk.sem("st_y"), mk.sem("st_y").n))
            mk.op("vector", comb, wait=wcb, inc=[dv.inc()])
            bank_free[gi] = dv.n
        nbd = pdiag[0]
        pbd = mk.pb[nbd][:, 0:64]
        def yfin(e, h=ph, i=i, pbd=pbd, first=(pdiag[1] == 0)):
            dst = yac[:, h * 64:(h + 1) * 64]
            if first:
                mk.fen(e, e.tensor_copy(dst, pbd))
            else:
                mk.fen(e, e.tensor_tensor(dst, dst, pbd, ALU.add))
            ddt = colsT[:, DDL + i * 32 + h:DDL + i * 32 + h + 1]
            return e.scalar_tensor_tensor(dst, vtm[:, i * 2048 + h * 64:i * 2048 + h * 64 + 64], ddt, dst, ALU.mult, ALU.add)
        wyf = [(mm, pdiag[2]), (dv, dv.n)]
        if pdiag[1] == 0:
            wyf.append((mk.sem("st_y"), mk.sem("st_y").n))
        mk.op("vector", yfin, wait=wyf, inc=[dv.inc()])
        bank_free[nbd] = dv.n
        mk._l0ac_sl = sl
        t_y = dv.n
        mk.store(d["sc_y"][i * 128:(i + 1) * 128, :], yac[:], q="sync", sem="st_y", serialize=True, wait=[(dv, t_y)])
    mk._l0ac_t_ltp = t_ltp_prev[0]
    mk.bar()


def l0a_epi_phase(mk, xnfm_d, x0_d, xa_d):
    nc = mk.nc
    mk.prog.phase = "l0ae"
    mk.reset()
    d = mk.dram
    C = mk.consts
    xnl = mk.c16(16384)
    wz = mk.c16(16384)
    wo = mk.c16(16384)
    nwb = mk.c16(2048)
    ytb = mk.c16(2048)
    yT = mk.c16(2048)
    yac = mk.c32(2 * 2048)
    zsb = mk.c32(2048)
    cols = mk.c32(8)
    osb = mk.c32(2 * D)
    x0t = mk.c32(2 * D)
    ld = mk.sem("ld_sync"); ldg = mk.sem("ld_gpsimd")
    mm = mk.sem("p_mm"); dv = mk.sem("p_dv"); sc = mk.sem("p_sc"); ev = mk.sem("p_ev")

    mk.load(xnl[:], xnfm_d.rearrange("(k p) t -> p k t", p=128), q="sync")
    mk.load(wz[:], d["a_wz"].rearrange("(k p) o -> p k o", p=128), q="gpsimd", sem="ld_wz")
    mk.load(wo[:], d["a_wo"].rearrange("(k p) o -> p k o", p=128), q="gpsimd", sem="ld_wz")
    t_ld = (ld.n, mk.sem("ld_wz").n)
    def nwo(e):
        e.matmul(mk.pb[6][:, 0:512], C["ones_row"][0:1, 0:128], C["a_nw"][0:1, 0:512], start=True, stop=True)
        e.matmul(mk.pb[5][:, 0:512], C["ones_row"][0:1, 0:128], C["a_nw"][0:1, 512:1024], start=True, stop=True)
        e.matmul(mk.pb[4][:, 0:512], C["ones_row"][0:1, 0:128], C["a_nw"][0:1, 1024:1536], start=True, stop=True)
        return e.matmul(mk.pb[3][:, 0:512], C["ones_row"][0:1, 0:128], C["a_nw"][0:1, 1536:2048], start=True, stop=True)
    mk.op("tensor", nwo, inc=[mm.inc()])
    def nwe(e):
        e.tensor_copy(nwb[:, 0:512], mk.pb[6][:, 0:512])
        e.tensor_copy(nwb[:, 512:1024], mk.pb[5][:, 0:512])
        e.tensor_copy(nwb[:, 1024:1536], mk.pb[4][:, 0:512])
        return e.tensor_copy(nwb[:, 1536:2048], mk.pb[3][:, 0:512])
    mk.op("vector", nwe, wait=[(mm, mm.n)], inc=[dv.inc()])
    t_nw = dv.n

    for i in range(TT):
        ysl = yac[:, (i % 2) * 2048:(i % 2) * 2048 + 2048]
        mk.load(ysl, d["sc_y"][i * 128:(i + 1) * 128, :], q="sync", sem="ld_y", serialize=True,
                wait=[(dv, dv.n), (sc, sc.n)])
        t_yld = mk.sem("ld_y").n
        for zc in range(4):
            pbk = mk.pb[zc % 2]
            def zmm(e, i=i, zc=zc, pbk=pbk):
                last = e.matmul(pbk[:], C["ones_row"][0:1, 0:128], C["a_bz"][0:1, zc * 512:(zc + 1) * 512],
                                start=True, stop=False)
                for k in range(8):
                    last = e.matmul(pbk[:], xnl[:, k * 2048 + i * 128:k * 2048 + (i + 1) * 128],
                                    wz[:, k * 2048 + zc * 512:k * 2048 + (zc + 1) * 512],
                                    start=False, stop=(k == 7))
                return last
            w = [(ld, t_ld[0]), (mk.sem("ld_wz"), t_ld[1])]
            if i > 0 or zc >= 2:
                w.append((sc, sc.n))
            mk.op("tensor", zmm, wait=w, inc=[mm.inc()])
            def sil(e, zc=zc, pbk=pbk):
                return e.activation(zsb[:, zc * 512:(zc + 1) * 512], pbk[:], AF.Silu)
            mk.op("scalar", sil, wait=[(mm, mm.n), (sc, sc.n)], inc=[sc.inc()])
        def yz2(e, ysl=ysl):
            return e.tensor_tensor(ysl, ysl, zsb[:], ALU.mult)
        mk.op("vector", yz2, wait=[(sc, sc.n), (mk.sem("ld_y"), t_yld)], inc=[dv.inc()])
        def sqr(e, ysl=ysl):
            return e.activation(zsb[:], ysl, AF.Square, accum_out=cols[:, 0:1])
        mk.op("scalar", sqr, wait=[(dv, dv.n)], inc=[sc.inc()])
        def rstd(e, ysl=ysl):
            mk.fen(e, e.activation(cols[:, 1:2], cols[:, 0:1], AF.Ln, bias=mk.fc(1e-5), scale=1.0 / A_DIM))
            return e.activation(cols[:, 2:3], cols[:, 1:2], AF.Exp, scale=-0.5)
        mk.op("scalar", rstd, wait=[(sc, sc.n)], inc=[sc.inc()])
        def ynorm(e, ysl=ysl):
            mk.fen(e, e.tensor_scalar(ysl, ysl, cols[:, 2:3], None, ALU.mult))
            return e.tensor_tensor(ytb[:], ysl, nwb[:], ALU.mult)
        mk.op("vector", ynorm, wait=[(sc, sc.n), (dv, t_nw)], inc=[dv.inc()])
        for wv2 in range(2):
            def ytp(e, wv2=wv2):
                last = None
                for k2 in range(8):
                    k = wv2 * 8 + k2
                    last = e.transpose(mk.ptp[:, k2 * 128:(k2 + 1) * 128], ytb[:, k * 128:(k + 1) * 128], C["eye16"][:])
                return last
            mk.op("tensor", ytp, wait=[(dv, dv.n)] + ([(ev, ev.n)] if (i > 0 or wv2 > 0) else []), inc=[mm.inc()])
            def yte(e, wv2=wv2):
                last = None
                for k2 in range(8):
                    k = wv2 * 8 + k2
                    last = e.tensor_copy(yT[:, k * 128:(k + 1) * 128], mk.ptp[:, k2 * 128:(k2 + 1) * 128])
                return last
            mk.op("vector", yte, wait=[(mm, mm.n)], inc=[ev.inc()])
        t_yT = mk.sems["p_ev"].n
        x0s = x0t[:, (i % 2) * D:(i % 2) * D + D]
        mk.load(x0s, x0_d[i * 128:(i + 1) * 128, :], q="sync", sem="ld_x0", serialize=True)
        for dc in range(2):
            pbk = mk.pb[2 + dc]
            def omm(e, i=i, dc=dc, pbk=pbk):
                last = e.matmul(pbk[:], C["ones_row"][0:1, 0:128], C["a_bo"][0:1, dc * 512:(dc + 1) * 512], start=True, stop=False)
                for k in range(16):
                    last = e.matmul(pbk[:], yT[:, k * 128:(k + 1) * 128], wo[:, k * 1024 + dc * 512:k * 1024 + (dc + 1) * 512],
                                    start=False, stop=(k == 15))
                return last
            mk.op("tensor", omm, wait=[(mk.sems["p_ev"], t_yT)] + ([(dv, dv.n)] if i > 0 else []), inc=[mm.inc()])
            os_ = osb[:, (i % 2) * D + dc * 512:(i % 2) * D + (dc + 1) * 512]
            def radd(e, os_=os_, x0s=x0s, dc=dc, pbk=pbk):
                return e.scalar_tensor_tensor(os_, pbk[:], 1.0, x0s[:, dc * 512:(dc + 1) * 512], ALU.mult, ALU.add)
            mk.op("vector", radd, wait=[(mm, mm.n), (mk.sem("ld_x0"), mk.sem("ld_x0").n)], inc=[dv.inc()])
        mk.store(xa_d[i * 128:(i + 1) * 128, :], osb[:, (i % 2) * D:(i % 2) * D + D], q="sync", sem="st_xa",
                 serialize=True, wait=[(dv, dv.n)])
    mk.bar()


def emit_batch(mk, b):
    d = mk.dram
    x0 = d["x_in"][b]
    out = d["out"][b]
    ln_phase(mk, x0, d["sc_xn32"], d["sc_xnfm"])
    l0a_prep_phase(mk, d["sc_xnfm"])
    l0a_core_phase(mk)
    l0a_epi_phase(mk, d["sc_xnfm"], x0, d["sc_xa"])
    l0f_attn_phase(mk, 0, d["sc_xn32"], d["sc_fx"])
    l0f_mlp_phase(mk, 0, d["sc_fx"], d["sc_fn32"])
    l0f_attn_phase(mk, 1, d["sc_fn32"], d["sc_fx"])
    l0f_mlp_phase(mk, 1, d["sc_fx"], d["sc_xb"], xa_in_d=d["sc_xa"])
    ln_phase(mk, d["sc_xb"], d["sc_xn132"], d["sc_xn1fm"])
    l1_attn_phase(mk, d["sc_xn1fm"], d["sc_xb"], out)
    moe_phase(mk, d["sc_xn1fm"], d["sc_xn132"], out)


def build_full(debug=False, batches=(0, 1)):
    mk = MK(debug=debug)
    setup(mk)
    for b in batches:
        emit_batch(mk, b)
    mk.prog.emit()
    mk.es.close()
    return mk


_MK_CACHE = {}

def kernel(**inputs):
    prep = host_prep(inputs)
    x = np.asarray(inputs["L0_attn_hidden_states"], dtype=np.float32)
    key = "full"
    if key not in _MK_CACHE:
        _MK_CACHE[key] = build_full()
    mk = _MK_CACHE[key]
    in_maps = []
    for c in range(NCORES):
        m = {k: v for k, v in prep.items()}
        m["x_in"] = np.ascontiguousarray(x[2 * c:2 * c + 2])
        in_maps.append(m)
    res = run_bass_kernel_spmd(mk.nc, in_maps, core_ids=list(range(NCORES)))
    out = np.concatenate([r["out"] for r in res.results], axis=0)
    return out.astype(np.float32)
